# version 10
# baseline (speedup 1.0000x reference)
import contextlib
import numpy as np

import concourse.bass as bass
import concourse.mybir as mybir
from concourse.bass_utils import run_bass_kernel_spmd

F32 = mybir.dt.float32
BF16 = mybir.dt.bfloat16
AX = mybir.AxisListType.X
ALU = mybir.AluOpType
ACT = mybir.ActivationFunctionType

NCORES = 8
S = 16384
D = 1024
T = S // NCORES
G = 1024
NG = T // G
NH = 16
HD = 64
HP = NH // 2
LH = 8
LHD = 64
LD = 512
NL = 64
W = 256
ST = G // 128
CH = (G + 256) // 128
E = 8
NEG = -1.0e30


class Sem:
    def __init__(self, es, nc, name):
        self.h = es.enter_context(nc.semaphore(name))
        self.n = 0


class Kb:
    def __init__(self, nc, es):
        self.nc = nc
        self.steps = []
        self.s_gd = Sem(es, nc, "s_gd")
        self.s_pe = Sem(es, nc, "s_pe")
        self.s_sc = Sem(es, nc, "s_sc")
        self.s_vx = Sem(es, nc, "s_vx")
        self.s_sd = Sem(es, nc, "s_sd")
        self.s_out = Sem(es, nc, "s_out")
        self.pe = self._mk("tensor", self.s_pe, 1)
        self.sc = self._mk("scalar", self.s_sc, 1)
        self.vx = self._mk("vector", self.s_vx, 1)
        self.gd = self._mk("gpsimd", self.s_gd, 16)
        self.sy = self._mk("sync", self.s_sd, 16)
        self.so = self._mk("sync", self.s_out, 16)

    def _mk(self, name, sem, bump):
        def add(fn, wait=None, inc=True):
            if inc:
                sem.n += bump
            tgt = sem.n if inc else None
            wl = [wv for wv in (wait or []) if wv is not None and wv[1] is not None]
            def step(eng, fn=fn, wl=wl, tgt=tgt):
                for (s, v) in wl:
                    eng.wait_ge(s.h, v)
                last = fn(eng)
                if tgt is not None:
                    assert last is not None, f"{name} step returned None with inc"
                    last.then_inc(sem.h, bump)
            self.steps.append((name, step))
            return tgt
        return add

    def gdma(self, dst, src, wait=None):
        return self.gd(lambda g, dst=dst, src=src: g.dma_start(out=dst, in_=src), wait=wait)

    def sdma(self, dst, src, wait=None):
        return self.sy(lambda sy, dst=dst, src=src: sy.dma_start(out=dst, in_=src), wait=wait)

    def odma(self, dst, src, wait=None):
        return self.so(lambda sy, dst=dst, src=src: sy.dma_start(out=dst, in_=src), wait=wait)

    def godma(self, dst, src, wait=None):
        self.s_out.n += 16
        tgt = self.s_out.n
        wl = [wv for wv in (wait or []) if wv is not None and wv[1] is not None]
        def step(eng, dst=dst, src=src, wl=wl, tgt=tgt):
            for (s, v) in wl:
                eng.wait_ge(s.h, v)
            eng.dma_start(out=dst, in_=src).then_inc(self.s_out.h, 16)
        self.steps.append(("gpsimd", step))
        return tgt

    def final_wait(self):
        v = self.s_out.n
        self.steps.append(("sync", lambda sy, v=v: sy.wait_ge(self.s_out.h, v)))

    def emit(self, blk):
        for name in ("sync", "gpsimd", "tensor", "scalar", "vector"):
            fns = [f for (e, f) in self.steps if e == name]
            if not fns:
                continue
            def body(eng, fns=fns):
                for f in fns:
                    f(eng)
            getattr(blk, name)(body)


def ap_off(t, extra_off, pattern):
    a = t.ap() if hasattr(t, "ap") else t
    return bass.AP(a.tensor, a.offset + extra_off, pattern)


def build_nc(stage="full", debug=()):
    nc = bass.Bass("TRN2", target_bir_lowering=False)
    es = contextlib.ExitStack()

    di = lambda n, shp: nc.dram_tensor(n, shp, F32, kind="ExternalInput").ap()
    xT = di("xT", [D, T + 256])
    x_own = di("x_own", [T, D])
    x_all = di("x_all", [S, D])
    wq_d = di("wq", [D, D]); wk_d = di("wk", [D, D]); wv_d = di("wv", [D, D])
    wgq_d = di("wgq", [D, LD]); wgk_d = di("wgk", [D, LD]); wgv_d = di("wgv", [D, LD])
    rout_d = di("r_out_w", [LD, D])
    wout_d = di("w_out", [D, D])
    wet_d = di("expert_wT", [E, D, D])
    gwt_d = di("gate_wT", [D, E])
    gbt_d = di("gate_b_t", [128, E])
    eb_d = di("expert_b", [E, D])
    glbd_d = di("glw_bd", [D, NH])
    grbd_d = di("grw_bd", [D, NH])
    iden_d = di("iden_i", [128, 128])
    tri_d = di("trimask_i", [128, 384])
    trih_d = di("trihalo_i", [128, 384])
    psel_d = di("poolsel_i", [128, 256])
    lmask_d = di("lmask_i", [128, 16 * NL])
    halov_d = di("halov_i", [128, 1])
    out_d = nc.dram_tensor("out", [T, D], F32, kind="ExternalOutput").ap()
    dbg = {}
    for name, shp in debug:
        dbg[name] = nc.dram_tensor("dbg_" + name, list(shp), F32, kind="ExternalOutput").ap()

    sbt = lambda n, shp, dty: es.enter_context(nc.sbuf_tensor(n, shp, dty))
    iden = sbt("iden", [128, 128], BF16)
    gwT = sbt("gwT", [128, 8, E], BF16)
    gbt = sbt("gbt", [128, E], F32)
    ebT = sbt("ebT", [E, D], BF16)
    sc_comb = sbt("sc_comb", [128, 8], F32)
    rd4 = sbt("rd4", [128, 8], F32)
    x2pre = sbt("x2pre", [128, 2, D], F32)
    s1 = sbt("s1", [128, 2], F32); s2 = sbt("s2", [128, 2], F32)
    mvv = sbt("mvv", [128, 2], F32)
    varv = sbt("varv", [128, 2], F32)
    lnl = sbt("lnl", [128, 2], F32)
    rstd = sbt("rstd", [128, 2], F32); nmr = sbt("nmr", [128, 2], F32)
    gsb = sbt("gsb", [128, E], F32)
    gsb2 = sbt("gsb2", [128, E], F32)
    gnm = sbt("gnm", [128, 1], F32)
    gden = sbt("gden", [128, 1], F32)
    grd = sbt("grd", [128, 1], F32)
    gmx = sbt("gmx", [128, E], F32)
    gind = sbt("gind", [128, E], F32)
    w_tm = sbt("w_tm", [128, ST, E], F32)
    wbf = sbt("wbf", [128, E], BF16)
    w_fm = sbt("w_fm", [8, ST, 128], BF16)
    attn_g = [sbt(f"attn{g}", [128, ST, D], BF16) for g in range(NG)]
    wbufB = sbt("wbufB", [128, 8, D], BF16)
    es1 = contextlib.ExitStack()
    sb1 = lambda n, shp, dty: es1.enter_context(nc.sbuf_tensor(n, shp, dty))
    trimask = sb1("trimask", [128, 384], BF16)
    trihalo = sb1("trihalo", [128, 384], BF16)
    poolsel = sb1("poolsel", [128, 256], BF16)
    glwbd = sb1("glwbd", [128, 8, NH], BF16)
    grwbd = sb1("grwbd", [128, 8, NH], BF16)
    lmaskb = sb1("lmaskb", [128, 16 * NL], BF16)
    halov = sb1("halov", [128, 1], F32)
    pooled_tm = sb1("pooled_tm", [NL, D], BF16)
    pooled_fm = sb1("pooled_fm", [128, 8, NL], BF16)
    gk_fm = sb1("gk_fm", [128, 4, NL], BF16)
    gv_fm = sb1("gv_fm", [128, 4, NL], BF16)
    gv_tm = sb1("gv_tm", [128, 4, NL], BF16)
    lp_sb = sb1("lp_sb", [128, 2, 8, NL], BF16)
    lpt2 = sb1("lpt2", [128, 2, 4, 128], BF16)
    lden = sb1("lden", [128, 2, 8], F32)
    lrd = sb1("lrd", [128, 2, 8], F32)
    gl_tm = sb1("gl_tm", [128, ST, NH], F32)
    gr_tm = [sb1(f"gr_tm{g}", [128, ST, NH], F32) for g in range(NG)]
    glfm = sb1("glfm", [16, G], BF16)
    sige = sb1("sige", [128, 128], F32)
    pbt = sb1("pbt", [128, 6, 2, 384], BF16)
    rem_fm = sb1("rem_fm", [128, 2, 4, 128], BF16)
    wbufA = sb1("wbufA", [128, 8, D], BF16)
    gq_g = [sb1(f"gq{g}", [128, 4, G], BF16) for g in range(NG)]
    q_sb = sb1("q_sb", [128, HP, G], BF16)
    k_sb = sb1("k_sb", [128, HP, G + 256], BF16)
    v_sb = sb1("v_sb", [128, CH, NH, 65], BF16)
    xg = sb1("xg", [128, 8, G + 256], BF16)
    poolb = sb1("poolb", [128, 2, 2, D], BF16)
    x2tm = x2fm = x3acc = fmbuf = x2bf = sq_scr = outb = xo_buf = None
    wer = None

    ps = [es.enter_context(nc.psum_tensor(f"ps{i}", [128, 512], F32)) for i in range(6)]
    ps_pool = es.enter_context(nc.psum_tensor("ps_pool", [128, 1024], F32))

    K = Kb(nc, es)
    SGD, SPE, SSC, SVX, SSD, SOUT = K.s_gd, K.s_pe, K.s_sc, K.s_vx, K.s_sd, K.s_out
    bank_gate = [None] * 6

    def dump(name, src_ap, wait):
        if name in dbg:
            return K.godma(dbg[name], src_ap, wait=wait)
        return None

    rg = lambda d: d.rearrange("(a p) n -> p a n", p=128)
    K.gdma(iden[:], iden_d)
    K.gdma(trimask[:], tri_d)
    K.gdma(trihalo[:], trih_d)
    K.gdma(poolsel[:], psel_d)
    K.gdma(glwbd[:], rg(glbd_d))
    K.gdma(grwbd[:], rg(grbd_d))
    K.gdma(gwT[:], rg(gwt_d))
    K.gdma(gbt[:], gbt_d)
    K.gdma(ebT[:], eb_d)
    K.gdma(lmaskb[:], lmask_d)
    v_const = K.gdma(halov[:], halov_d)

    v_xg = [None, None]
    v_wq = [None, None]; v_wk = [None, None]; v_wv = [None, None]; v_wgq = [None, None]
    v_xg[0] = K.gdma(xg[:], rg(xT)[:, :, 0:G + 256])
    v_wq[0] = K.gdma(wbufA[:], rg(wq_d))
    v_wk[0] = K.gdma(wbufB[:], rg(wk_d))

    NPIECE = S // 256
    pool_dma_v = []
    pool_pe_v = []

    def _pool_dma1():
        i = len(pool_dma_v)
        w = [(SPE, pool_pe_v[i - 2])] if i >= 2 else None
        v = K.gdma(poolb[:, i % 2], rg(x_all[256 * i:256 * i + 256, :]), wait=w)
        pool_dma_v.append(v)

    def pool_step(n):
        for _ in range(n):
            i, j = len(pool_dma_v), len(pool_pe_v)
            if i >= NPIECE and j >= NPIECE:
                return
            if i < NPIECE and i <= j + 1:
                _pool_dma1()
            else:
                _pool_mm1()

    def _pool_mm1():
            i = len(pool_pe_v)
            if i >= NPIECE or i >= len(pool_dma_v):
                return
            def mm(te, i=i):
                last = None
                p = i
                for s4 in range(2):
                    for hf in range(2):
                        last = te.matmul(ps_pool[:, 512 * hf:512 * hf + 512],
                                         poolsel[:, 128 - p:256 - p],
                                         poolb[:, i % 2, s4, 512 * hf:512 * hf + 512],
                                         start=(i == 0 and s4 == 0),
                                         stop=(i == NPIECE - 1 and s4 == 1))
                return last
            v = K.pe(mm, wait=[(SGD, pool_dma_v[i])])
            pool_pe_v.append(v)

    def proj_fm(dst_fn, wsb, w_v, xg_lo, xg_n, nchunk, v_xg_cur, wfree=1024):
        nblk = (xg_n + 511) // 512
        last_pe = last_vx = None
        wflat = wsb.ap().rearrange("p a n -> p (a n)")
        for i in range(nchunk):
            for nb in range(nblk):
                n0 = 512 * nb
                nn = min(xg_n, n0 + 512) - n0
                bi = (i * nblk + nb) % 2
                bank = ps[bi]
                def mm(te, i=i, n0=n0, nn=nn, bank=bank, wflat=wflat, wfree=wfree):
                    last = None
                    for hc in range(8):
                        last = te.matmul(bank[:, 0:nn],
                                         wflat[:, wfree * hc + 128 * i:wfree * hc + 128 * i + 128],
                                         xg[:, hc, xg_lo + n0:xg_lo + n0 + nn],
                                         start=(hc == 0), stop=(hc == 7))
                    return last
                last_pe = K.pe(mm, wait=[(SGD, max(w_v, v_xg_cur)), bank_gate[bi]])
                def ev(vec, i=i, n0=n0, nn=nn, bank=bank, dst_fn=dst_fn):
                    return vec.tensor_copy(dst_fn(i, n0, nn), bank[:, 0:nn])
                last_vx = K.vx(ev, wait=[(SPE, last_pe)])
                bank_gate[bi] = (SVX, last_vx)
        return last_pe, last_vx

    def emit_glgr(g, qvx_gate):
        last = None
        for (wsb_, dst) in ((glwbd, gl_tm), (grwbd, gr_tm[g])):
            ev = None
            for nb in range(2):
                bank = ps[4]
                def mm(te, wsb_=wsb_, nb=nb, bank=bank):
                    l = None
                    for hpp in range(HP):
                        l = te.matmul(bank[0:16, 0:512], wsb_[:, hpp, :],
                                      q_sb[:, hpp, 512 * nb:512 * nb + 512],
                                      start=(hpp == 0), stop=(hpp == HP - 1))
                    return l
                pv = K.pe(mm, wait=[(SVX, qvx_gate), (SGD, v_const), bank_gate[4]])
                ev = K.vx(lambda vec, nb=nb, bank=bank: vec.tensor_copy(
                    glfm[:, 512 * nb:512 * nb + 512], bank[0:16, 0:512]),
                    wait=[(SPE, pv)])
                bank_gate[4] = (SVX, ev)
            bank2 = ps[5]
            def mmt(te, bank2=bank2):
                l = None
                for st in range(ST):
                    l = te.matmul(bank2[:, 16 * st:16 * st + 16],
                                  glfm[:, 128 * st:128 * st + 128],
                                  iden[0:16, 0:16], start=True, stop=True)
                return l
            pv2 = K.pe(mmt, wait=[(SVX, ev), bank_gate[5]])
            c1 = K.sc(lambda s, bank2=bank2: s.activation(
                sige[:], bank2[:, 0:128], ACT.Exp, scale=-1.0),
                wait=[(SPE, pv2)])
            bank_gate[5] = (SSC, c1)
            sa = K.vx(lambda vec: vec.tensor_scalar_add(sige[:], sige[:], 1.0),
                      wait=[(SSC, c1)])
            last = K.vx(lambda vec, dst=dst: vec.reciprocal(
                dst.ap().rearrange("p a n -> p (a n)"), sige[:]), wait=[(SVX, sa)])
        return last

    xg_free = [None, None]

    def emit_projections(g):
        res = {}
        vxc = v_xg[g]
        qpe, qvx = proj_fm(lambda i, n0, nn: q_sb.ap()[:, i, n0:n0 + nn],
                           wbufA, v_wq[g], 256, G, HP, vxc)
        pool_step(4 if g == 0 else 10)
        glv = emit_glgr(g, qvx)
        v_wv[g] = K.gdma(wbufA[:], rg(wv_d), wait=[(SPE, qpe)])
        kpe, kvx = proj_fm(lambda i, n0, nn: k_sb.ap()[:, i, n0:n0 + nn],
                           wbufB, v_wk[g], 0, G + 256, HP, vxc)
        pool_step(4 if g == 0 else 10)
        v_wgq[g] = K.gdma(wbufB[:, 0:4], rg(wgq_d), wait=[(SPE, kpe)])
        vvx = vpe = None
        for sc_ in range(CH):
            for oh in range(2):
                bi = (sc_ * 2 + oh) % 2
                bank = ps[bi]
                def mm(te, sc_=sc_, oh=oh, bank=bank):
                    last = None
                    for hc in range(8):
                        last = te.matmul(bank[:, 0:512],
                                         xg[:, hc, 128 * sc_:128 * sc_ + 128],
                                         wbufA[:, hc, 512 * oh:512 * oh + 512],
                                         start=(hc == 0), stop=(hc == 7))
                    return last
                vpe = K.pe(mm, wait=[(SGD, max(v_wv[g], vxc)), bank_gate[bi]])
                def ev(vec, sc_=sc_, oh=oh, bank=bank):
                    dst = v_sb.ap()[:, sc_, 8 * oh:8 * oh + 8, 0:64]
                    src = bass.AP(bank.ap().tensor, bank.ap().offset,
                                  [[512, 128], [64, 8], [1, 64]])
                    return vec.tensor_copy(dst, src)
                vvx = K.vx(ev, wait=[(SPE, vpe)])
                bank_gate[bi] = (SVX, vvx)
        vone = K.vx(lambda vec: vec.memset(v_sb.ap()[:, :, :, 64:65], 1.0)) if g == 0 else None
        pool_step(4 if g == 0 else 10)
        res["wA_free_pe"] = vpe
        gqpe, gqvx = proj_fm(lambda i, n0, nn: gq_g[g].ap()[:, i, n0:n0 + nn],
                             wbufB, v_wgq[g], 256, G, 4, vxc, wfree=512)
        pool_step(6 if g == 0 else 16)
        xg_free[g] = (SPE, gqpe)
        res.update(qvx=qvx, kvx=kvx, vvx=vvx, vone=vone, glv=glv,
                   gqvx=gqvx, gqpe=gqpe, qpe=qpe)
        return res

    def emit_local(g, pre_gates):
        attn = attn_g[g]
        first = True
        slot_pv = {}
        units = []
        for hpp in range(HP):
            for half in range(2):
                tlo_h, thi_h = 4 * half, 4 * half + 4
                for ca in range(tlo_h, thi_h + 2):
                    tlo = max(tlo_h, ca - 2)
                    thi = min(thi_h, ca + 1)
                    if tlo < thi:
                        units.append((hpp, half, ca, tlo, thi))
        uidx = {(u[0], u[1], u[2]): i for i, u in enumerate(units)}
        pvjobs = []
        for hpp in range(HP):
            for half in range(2):
                for t in range(4 * half, 4 * half + 4):
                    pvjobs.append((hpp, half, t, uidx[(hpp, half, t + 2)]))
        pvjobs.sort(key=lambda j: j[3])
        LAG = 4
        mask_val = {}
        span_of = {}
        slot_of = {}
        pv_emitted = 0
        finv = None
        fin_gate = None
        last_pv_of_half = {}

        def emit_pv(j):
            nonlocal finv, fin_gate
            (hpp, half, t, _) = pvjobs[j]
            parts = tuple((t + jj, slot_of[(hpp, half, t + jj)],
                           t - span_of[(hpp, half, t + jj)][0]) for jj in range(3))
            def mmpv(te, t=t, hpp=hpp, half=half, parts=parts):
                last = None
                for jj, (cj, slj, blk) in enumerate(parts):
                    for par in range(2):
                        head = 2 * hpp + par
                        acc = ps[par][:, 66 * (t - 4 * half):66 * (t - 4 * half) + 65]
                        last = te.matmul(acc, pbt[:, slj, par, 128 * blk:128 * blk + 128],
                                         v_sb[:, cj, head, :],
                                         start=(jj == 0), stop=(jj == 2))
                return last
            wpv = [(SVX, mask_val[(hpp, half, t + 2)])]
            if t == 4 * half:
                wpv += [bank_gate[0], bank_gate[1]]
            pvv = K.pe(mmpv, wait=wpv)
            for jj in range(3):
                slot_pv[slot_of[(hpp, half, t + jj)]] = (SPE, pvv)
            if t == 4 * half + 3:
                def fin_a(vec):
                    vec.reciprocal(rd4[:, 0:4], ap_off(ps[0], 64, [[512, 128], [66, 4]]))
                    return vec.reciprocal(rd4[:, 4:8], ap_off(ps[1], 64, [[512, 128], [66, 4]]))
                fa = K.vx(fin_a, wait=[(SPE, pvv)])
                def fin_b(vec, half=half, hpp=hpp):
                    r = None
                    for par in range(2):
                        head = 2 * hpp + par
                        glc = ap_off(gl_tm, (4 * half) * NH + head, [[ST * NH, 128], [NH, 4]])
                        r = vec.tensor_tensor(sc_comb[:, 4 * par:4 * par + 4],
                                              rd4[:, 4 * par:4 * par + 4], glc, ALU.mult)
                    return r
                fb = K.vx(fin_b, wait=[(SVX, fa)])
                def fin(vec, half=half, hpp=hpp, attn=attn):
                    r = None
                    for par in range(2):
                        head = 2 * hpp + par
                        for t4 in range(4):
                            tt = 4 * half + t4
                            r = vec.tensor_scalar_mul(
                                attn[:, tt, 64 * head:64 * head + 64],
                                ps[par][:, 66 * t4:66 * t4 + 64],
                                sc_comb[:, 4 * par + t4:4 * par + t4 + 1])
                    return r
                finv = K.vx(fin, wait=[(SVX, fb)])
                bank_gate[0] = (SVX, finv)
                bank_gate[1] = (SVX, finv)

        for i, (hpp, half, ca, tlo, thi) in enumerate(units):
            while pv_emitted < len(pvjobs) and pvjobs[pv_emitted][3] <= i - LAG:
                emit_pv(pv_emitted)
                pv_emitted += 1
            nn = 128 * (thi - tlo)
            span_of[(hpp, half, ca)] = (tlo, thi)
            sl = i % 6
            slot_of[(hpp, half, ca)] = sl
            be, bo = 2 + (i % 2) * 2, 3 + (i % 2) * 2
            def mmqk(te, hpp=hpp, ca=ca, tlo=tlo, nn=nn, be=be, bo=bo):
                te.matmul(ps[be][:, 0:nn],
                          k_sb[0:64, hpp, 128 * ca:128 * ca + 128],
                          q_sb[0:64, hpp, 128 * tlo:128 * tlo + nn],
                          start=True, stop=True)
                return te.matmul(ps[bo][:, 0:nn],
                                 k_sb[64:128, hpp, 128 * ca:128 * ca + 128],
                                 q_sb[64:128, hpp, 128 * tlo:128 * tlo + nn],
                                 start=True, stop=True, tile_position=(64, 0))
            wt = list(pre_gates) if first else []
            first = False
            wt += [bank_gate[be], bank_gate[bo]]
            qk = K.pe(mmqk, wait=wt)
            wexp = [(SPE, qk), slot_pv.get(sl)]
            def ex(sc_, sl=sl, nn=nn, be=be, bo=bo):
                sc_.activation(pbt[:, sl, 0, 0:nn], ps[be][:, 0:nn], ACT.Exp, scale=0.125)
                return sc_.activation(pbt[:, sl, 1, 0:nn], ps[bo][:, 0:nn],
                                      ACT.Exp, scale=0.125)
            exv = K.sc(ex, wait=wexp)
            bank_gate[be] = (SSC, exv)
            bank_gate[bo] = (SSC, exv)
            rlo = tlo + 2 - ca
            msk = trihalo if (g == 0 and ca < 2) else trimask
            def mk(vec, sl=sl, nn=nn, rlo=rlo, msk=msk):
                vec.tensor_tensor(pbt[:, sl, 0, 0:nn], pbt[:, sl, 0, 0:nn],
                                  msk[:, 128 * rlo:128 * rlo + nn], ALU.mult)
                return vec.tensor_tensor(pbt[:, sl, 1, 0:nn], pbt[:, sl, 1, 0:nn],
                                         msk[:, 128 * rlo:128 * rlo + nn], ALU.mult)
            mkv = K.vx(mk, wait=[(SSC, exv)])
            mask_val[(hpp, half, ca)] = mkv
            if i % 12 == 0:
                pool_step(3)
        while pv_emitted < len(pvjobs):
            emit_pv(pv_emitted)
            pv_emitted += 1
        return finv

    def emit_pooled_finish(v_wgk, v_wgv):
        pe_last = pool_pe_v[-1]
        c1 = K.sc(lambda s: s.activation(pooled_tm[:], ps_pool[0:64, :], ACT.Copy,
                                         scale=1.0 / 256), wait=[(SPE, pe_last)])
        def mmt(te):
            l = None
            for hc in range(8):
                l = te.matmul(ps[0][:, 64 * hc:64 * hc + 64],
                              pooled_tm[:, 128 * hc:128 * hc + 128],
                              iden[0:64, 0:64], start=True, stop=True)
            return l
        p1 = K.pe(mmt, wait=[(SSC, c1), bank_gate[0]])
        vfm = K.vx(lambda vec: vec.tensor_copy(
            pooled_fm.ap().rearrange("p a n -> p (a n)"), ps[0][:, 0:512]),
            wait=[(SPE, p1)])
        bank_gate[0] = (SVX, vfm)
        evs = []
        pvl = None
        for (wv_, wsb_, dst, bi) in ((v_wgk, wbufA, gk_fm, 1), (v_wgv, wbufB, gv_fm, 2)):
            wflat = wsb_.ap().rearrange("p a n -> p (a n)")
            def mm(te, wflat=wflat, bi=bi):
                l = None
                for lc in range(4):
                    for hc in range(8):
                        l = te.matmul(ps[bi][:, 64 * lc:64 * lc + 64],
                                      wflat[:, 512 * hc + 128 * lc:512 * hc + 128 * lc + 128],
                                      pooled_fm[:, hc, :],
                                      start=(hc == 0), stop=(hc == 7))
                return l
            pvl = K.pe(mm, wait=[(SGD, wv_), (SVX, vfm), bank_gate[bi]])
            ev = K.vx(lambda vec, dst=dst, bi=bi: vec.tensor_copy(
                dst.ap().rearrange("p a n -> p (a n)"), ps[bi][:, 0:256]),
                wait=[(SPE, pvl)])
            bank_gate[bi] = (SVX, ev)
            evs.append(ev)
        tvs = None
        for lc in range(4):
            def mmtv(te, lc=lc):
                te.matmul(ps[3][0:64, 0:64], gv_fm[0:64, lc, :], iden[0:64, 0:64],
                          start=True, stop=True)
                return te.matmul(ps[4][64:128, 0:64], gv_fm[64:128, lc, :],
                                 iden[64:128, 64:128], start=True, stop=True,
                                 tile_position=(64, 64))
            wtv = [(SVX, evs[-1]), bank_gate[3], bank_gate[4]]
            ptv = K.pe(mmtv, wait=wtv)
            def evtv(vec, lc=lc):
                vec.tensor_copy(gv_tm[0:64, lc, :], ps[3][0:64, 0:64])
                return vec.tensor_copy(gv_tm[64:128, lc, :], ps[4][64:128, 0:64])
            tvs = K.vx(evtv, wait=[(SPE, ptv)])
            bank_gate[3] = (SVX, tvs)
            bank_gate[4] = (SVX, tvs)
        return tvs, pvl

    def emit_latent(g, gkgv_v, gq_gate, v_rout, extra_gate=None, only_st=None):
        attn = attn_g[g]
        lp_gate = {}
        lpt_gate = {}
        rem_gate = {}
        st_state = {}
        nst = ST if only_st is None else only_st

        def s1_lsc(st):
            b = st % 2
            gst0 = ST * g + st
            def mml(te, st=st, g=g, gst0=gst0):
                last = None
                onesrow = trimask[0:1, 128:256]
                lrow = lmaskb.ap()[0:1, NL * gst0:NL * gst0 + NL]
                for lp_ in range(4):
                    te.matmul(ps[2][:, 64 * lp_:64 * lp_ + 64],
                              gq_g[g][0:64, lp_, 128 * st:128 * st + 128],
                              gk_fm[0:64, lp_, :], start=True, stop=False)
                    te.matmul(ps[3][:, 64 * lp_:64 * lp_ + 64],
                              gq_g[g][64:128, lp_, 128 * st:128 * st + 128],
                              gk_fm[64:128, lp_, :], start=True, stop=False,
                              tile_position=(64, 0))
                    te.matmul(ps[2][:, 64 * lp_:64 * lp_ + 64], onesrow, lrow,
                              start=False, stop=True)
                    last = te.matmul(ps[3][:, 64 * lp_:64 * lp_ + 64], onesrow, lrow,
                                     start=False, stop=True)
                return last
            wt = [bank_gate[2], bank_gate[3]]
            if st == 0 and g == 0:
                wt += [(SVX, gkgv_v)]
            if st == 0:
                wt += [gq_gate]
                if extra_gate is not None:
                    wt.append(extra_gate)
            return K.pe(mml, wait=wt)

        def s2_softmax(st, p1):
            b = st % 2
            wex = [(SPE, p1), lp_gate.get(b)]
            def ex(s, b=b):
                r = None
                for lp_ in range(4):
                    s.activation(lp_sb[:, b, 2 * lp_, :], ps[2][:, 64 * lp_:64 * lp_ + 64],
                                 ACT.Exp, scale=0.125,
                                 accum_out=lden[:, b, 2 * lp_:2 * lp_ + 1])
                    r = s.activation(lp_sb[:, b, 2 * lp_ + 1, :], ps[3][:, 64 * lp_:64 * lp_ + 64],
                                     ACT.Exp, scale=0.125,
                                     accum_out=lden[:, b, 2 * lp_ + 1:2 * lp_ + 2])
                return r
            c1 = K.sc(ex, wait=wex)
            bank_gate[2] = (SSC, c1)
            bank_gate[3] = (SSC, c1)
            vr = K.vx(lambda vec, b=b: vec.reciprocal(lrd[:, b, :], lden[:, b, :]),
                      wait=[(SSC, c1)])
            def nrm(vec, b=b):
                r = None
                for lh in range(8):
                    r = vec.tensor_scalar_mul(lp_sb[:, b, lh, :], lp_sb[:, b, lh, :],
                                              lrd[:, b, lh:lh + 1])
                return r
            return K.vx(nrm, wait=[(SVX, vr)])

        def s3_transpose(st, v2):
            b = st % 2
            def mmt(te, b=b):
                last = None
                for lp_ in range(4):
                    te.matmul(ps[4][0:64, 128 * lp_:128 * lp_ + 128],
                              lp_sb[:, b, 2 * lp_, :], iden[:], start=True, stop=True)
                    last = te.matmul(ps[5][64:128, 128 * lp_:128 * lp_ + 128],
                                     lp_sb[:, b, 2 * lp_ + 1, :], iden[:],
                                     start=True, stop=True, tile_position=(0, 64))
                return last
            p2 = K.pe(mmt, wait=[(SVX, v2), bank_gate[4], bank_gate[5]])
            lp_gate[b] = (SPE, p2)
            def evt(sc, b=b):
                sc.activation(lpt2.ap()[0:64, b].rearrange("p a n -> p (a n)"),
                              ps[4][0:64, 0:512], ACT.Copy)
                return sc.activation(lpt2.ap()[64:128, b].rearrange("p a n -> p (a n)"),
                                     ps[5][64:128, 0:512], ACT.Copy)
            v3 = K.sc(evt, wait=[(SPE, p2), lpt_gate.get(b)])
            bank_gate[4] = (SSC, v3)
            bank_gate[5] = (SSC, v3)
            return v3

        def s5_pv(st, v3):
            b = st % 2
            def mmp(te, b=b):
                last = None
                for lp_ in range(4):
                    te.matmul(ps[0][0:64, 128 * lp_:128 * lp_ + 128],
                              gv_tm[0:64, lp_, :], lpt2[0:64, b, lp_, :],
                              start=True, stop=True)
                    last = te.matmul(ps[1][64:128, 128 * lp_:128 * lp_ + 128],
                                     gv_tm[64:128, lp_, :], lpt2[64:128, b, lp_, :],
                                     start=True, stop=True, tile_position=(64, 64))
                return last
            p3 = K.pe(mmp, wait=[(SSC, v3), bank_gate[0], bank_gate[1]])
            lpt_gate[b] = (SPE, p3)
            def evp(sc, b=b):
                sc.activation(rem_fm.ap()[0:64, b].rearrange("p a n -> p (a n)"),
                              ps[0][0:64, 0:512], ACT.Copy)
                return sc.activation(rem_fm.ap()[64:128, b].rearrange("p a n -> p (a n)"),
                                     ps[1][64:128, 0:512], ACT.Copy)
            v4 = K.sc(evp, wait=[(SPE, p3), rem_gate.get(b)])
            bank_gate[0] = (SSC, v4)
            bank_gate[1] = (SSC, v4)
            return v4

        def s7_rout(st, v4):
            b = st % 2
            prs = []
            for oh in range(2):
                def mmr(te, oh=oh, b=b):
                    l = None
                    for lc in range(4):
                        l = te.matmul(ps[oh][:, 0:512], rem_fm[:, b, lc, :],
                                      wbufA[:, 4 + lc, 512 * oh:512 * oh + 512],
                                      start=(lc == 0), stop=(lc == 3))
                    return l
                wt = [(SSC, v4), bank_gate[oh]]
                if st == 0 and g == 0:
                    wt.append((SGD, v_rout))
                pr = K.pe(mmr, wait=wt)
                prs.append(pr)
            rem_gate[b] = (SPE, prs[-1])
            last = None
            for oh in range(2):
                def comb(vec, oh=oh, st=st, g=g, attn=attn):
                    r = None
                    for hh in range(8):
                        head = 8 * oh + hh
                        r = vec.scalar_tensor_tensor(
                            attn[:, st, 64 * head:64 * head + 64],
                            ps[oh][:, 64 * hh:64 * hh + 64],
                            gr_tm[g][:, st, head:head + 1],
                            attn[:, st, 64 * head:64 * head + 64],
                            op0=ALU.mult, op1=ALU.add)
                    return r
                cv = K.vx(comb, wait=[(SPE, prs[oh]), (SPE, prs[-1])])
                bank_gate[oh] = (SVX, cv)
                last = cv
            return last

        last = None
        for st in range(nst + 1):
            if st < nst:
                p1 = s1_lsc(st)
                v2 = s2_softmax(st, p1)
                st_state[st] = v2
            if st >= 1:
                stp = st - 1
                v3 = s3_transpose(stp, st_state[stp])
                v4 = s5_pv(stp, v3)
                last = s7_rout(stp, v4)
        return last

    ln_prev = {}

    def emit_ln(src_ap, wait_gates, dsts, b=0, dummy=None):
        sq = dummy if dummy is not None else sq_scr.ap()[:, b, :]
        wg = list(wait_gates)
        if b in ln_prev:
            wg.append(ln_prev[b])
        def sums(s):
            s.activation(sq, src_ap, ACT.Copy, accum_out=s1[:, b:b + 1])
            return s.activation(sq, src_ap, ACT.Square, accum_out=s2[:, b:b + 1])
        c1 = K.sc(sums, wait=wg)
        v1 = c1
        def stats_a(vec):
            vec.tensor_scalar_mul(mvv[:, b:b + 1], s1[:, b:b + 1], -1.0 / D)
            return vec.tensor_scalar_mul(varv[:, b:b + 1], s2[:, b:b + 1], 1.0 / D)
        v2a = K.vx(stats_a, wait=[(SSC, c1)])
        v2b = K.vx(lambda vec: vec.scalar_tensor_tensor(
            lnl[:, b:b + 1], mvv[:, b:b + 1], mvv[:, b:b + 1], varv[:, b:b + 1],
            op0=ALU.mult, op1=ALU.subtract), wait=[(SVX, v2a)])
        v2 = K.vx(lambda vec: vec.tensor_scalar(
            varv[:, b:b + 1], lnl[:, b:b + 1], -1.0, 1e-5,
            op0=ALU.mult, op1=ALU.add), wait=[(SVX, v2b)])
        c2 = K.sc(lambda s: s.activation(lnl[:, b:b + 1], varv[:, b:b + 1], ACT.Ln),
                  wait=[(SVX, v2)])
        c3 = K.sc(lambda s: s.activation(rstd[:, b:b + 1], lnl[:, b:b + 1], ACT.Exp,
                                         scale=-0.5), wait=[(SSC, c2)])
        v3 = K.vx(lambda vec: vec.tensor_tensor(nmr[:, b:b + 1], mvv[:, b:b + 1],
                                                rstd[:, b:b + 1], ALU.mult),
                  wait=[(SSC, c3)])
        def norm(vec):
            r = None
            for d_ in dsts:
                r = vec.tensor_scalar(d_, src_ap, rstd[:, b:b + 1], nmr[:, b:b + 1],
                                      op0=ALU.mult, op1=ALU.add)
            return r
        out = K.vx(norm, wait=[(SVX, v3)])
        ln_prev[b] = (SVX, out)
        return out

    xo_q = {}

    def xo_fetch(g, st):
        if st >= ST:
            return
        w = [xo_last.get(st % 3)]
        xo_q[(g, st)] = K.sdma(xo_buf[:, st % 3],
                               x_own[G * g + 128 * st:G * g + 128 * st + 128, :], wait=w)

    def make_wout(g, attn_done, v_wout, prev_x2_free, wout_banks=(2, 3)):
        attn = attn_g[g]
        ln_done = {}
        x2fm_ev = {}
        tr_prev = {}
        afm_gate = {}
        wb0, wb1 = wout_banks
        for st in range(3):
            if (g, st) not in xo_q:
                xo_fetch(g, st)

        def stage_A(st):
            tvs = []
            for b2 in range(2):
                bank = ps[4 + b2]
                def mmt(te, st=st, b2=b2, bank=bank, attn=attn):
                    l = None
                    for fc4 in range(4):
                        fc = 4 * b2 + fc4
                        l = te.matmul(bank[:, 128 * fc4:128 * fc4 + 128],
                                      attn[:, st, 128 * fc:128 * fc + 128],
                                      iden[:], start=True, stop=True)
                    return l
                wt = [bank_gate[4 + b2], afm_gate.get(st % 2)]
                if st == 0:
                    wt.append((SVX, attn_done))
                    wt.append(prev_x2_free)
                ptv = K.pe(mmt, wait=wt)
                def evt(vec, st=st, b2=b2, bank=bank):
                    return vec.tensor_copy(afm_s.ap()[:, st % 2, 4 * b2:4 * b2 + 4, :],
                                           bank[:, 0:512])
                tv = K.vx(evt, wait=[(SPE, ptv)])
                bank_gate[4 + b2] = (SVX, tv)
                tvs.append(tv)
            pw = None
            for oh in range(2):
                bi = wb0 if oh == 0 else wb1
                def mmw(te, st=st, oh=oh, bi=bi):
                    l = None
                    for fc in range(8):
                        l = te.matmul(ps[bi][:, 0:512],
                                      afm_s[:, st % 2, fc, :],
                                      wbufB[:, fc, 512 * oh:512 * oh + 512],
                                      start=(fc == 0), stop=(fc == 7))
                    return l
                pw = K.pe(mmw, wait=[(SVX, tvs[1]), (SGD, v_wout), bank_gate[bi]])
            afm_gate[st % 2] = (SPE, pw)
            def resid(vec, st=st, g=g):
                vec.tensor_tensor(x2pre[:, st % 2, 0:512], ps[wb0][:, 0:512],
                                  xo_buf[:, st % 3, 0:512], ALU.add)
                return vec.tensor_tensor(x2pre[:, st % 2, 512:1024], ps[wb1][:, 0:512],
                                         xo_buf[:, st % 3, 512:1024], ALU.add)
            rv = K.vx(resid, wait=[(SPE, pw), (SSD, xo_q[(g, st)]),
                                   x2pre_gate.get(st % 2)])
            bank_gate[wb0] = (SVX, rv)
            bank_gate[wb1] = (SVX, rv)
            xo_last[st % 3] = (SVX, rv)
            xo_fetch(g, st + 3)
            return rv

        def stage_B(st, rv):
            wln = [(SVX, rv)]
            lnv = emit_ln(x2pre.ap()[:, st % 2, :], wln, [x2tm[:, st, :]],
                          b=st % 2, dummy=sq_scr[:])
            ln_done[st] = lnv
            x2pre_gate[st % 2] = (SVX, lnv)
            return lnv

        def stage_C(st):
            cb = K.vx(lambda vec, st=st: vec.tensor_copy(x2bf[:, st % 2, :],
                                                         x2tm[:, st, :]),
                      wait=[(SVX, ln_done[st]), tr_prev.get(st % 2)])
            for b2 in range(2):
                bank = ps[b2]
                def mmx(te, st=st, b2=b2, bank=bank):
                    l = None
                    for fc4 in range(4):
                        fc = 4 * b2 + fc4
                        l = te.matmul(bank[:, 128 * fc4:128 * fc4 + 128],
                                      x2bf[:, st % 2, 128 * fc:128 * fc + 128],
                                      iden[:], start=True, stop=True)
                    return l
                px = K.pe(mmx, wait=[(SVX, cb), bank_gate[b2]])
                tr_prev[st % 2] = (SPE, px)
                def evx(vec, st=st, b2=b2, bank=bank):
                    dst = bass.AP(fmbuf.ap().tensor,
                                  fmbuf.ap().offset + (4 * b2) * G + 128 * st,
                                  [[8 * G, 128], [G, 4], [1, 128]])
                    return vec.tensor_copy(dst, bank[:, 0:512])
                xv = K.vx(evx, wait=[(SPE, px)])
                bank_gate[b2] = (SVX, xv)
                x2fm_ev[(st, b2)] = xv

        def step(st):
            rv = stage_A(st)
            stage_B(st, rv)

        def finish():
            for st in range(ST):
                stage_C(st)

        return step, finish, ln_done, x2fm_ev

    def emit_wout_ln1(g, attn_done, v_wout, prev_x2_free):
        step, finish, ln_done, x2fm_ev = make_wout(g, attn_done, v_wout, prev_x2_free)
        for st in range(ST):
            step(st)
        finish()
        return ln_done, x2fm_ev

    def emit_ffn(g, x2fm_ev, ln_done, filler=None):
        wprev = None
        for st in range(ST):
            def mmg(te, st=st):
                l = None
                for hc in range(8):
                    l = te.matmul(ps[4][:, 0:E], fmbuf[:, hc, 128 * st:128 * st + 128],
                                  gwT[:, hc, :], start=(hc == 0), stop=(hc == 7))
                return l
            wt = [(SVX, x2fm_ev[(st, 1)]), bank_gate[4]]
            pg = K.pe(mmg, wait=wt)
            v1 = K.vx(lambda vec: vec.tensor_tensor(gsb[:], ps[4][:, 0:E], gbt[:], ALU.add),
                      wait=[(SPE, pg)])
            bank_gate[4] = (SVX, v1)
            v2 = K.vx(lambda vec: vec.reduce_max(gnm[:], gsb[:], axis=AX, negate=True),
                      wait=[(SVX, v1)])
            c1 = K.sc(lambda s: s.activation(gsb2[:], gsb[:], ACT.Exp, bias=gnm[:],
                                             accum_out=gden[:]), wait=[(SVX, v2)])
            vgr = K.vx(lambda vec: vec.reciprocal(grd[:], gden[:]), wait=[(SSC, c1)])
            vmx = K.vx(lambda vec: vec.max(out=gmx[:], in_=gsb2[:]), wait=[(SSC, c1)])
            vsel = K.vx(lambda vec: vec.scalar_tensor_tensor(
                gind[:], gsb2[:], gmx[:, 1:2], gsb2[:], op0=ALU.is_ge, op1=ALU.mult),
                wait=[(SVX, vmx)])
            v3 = K.vx(lambda vec, st=st: vec.tensor_scalar_mul(
                w_tm[:, st, :], gind[:], grd[:]), wait=[(SVX, vsel), (SVX, vgr)])
            import os as _os7
            if _os7.environ.get("KFFN") == "1" and g == 0 and st == 0:
                def dcpg(vec):
                    vec.tensor_copy(x2pre[:, 0, 0:8], gsb[:])
                    vec.tensor_copy(x2pre[:, 0, 8:16], gsb2[:])
                    vec.tensor_copy(x2pre[:, 0, 16:24], gmx[:])
                    vec.tensor_copy(x2pre[:, 0, 24:32], gind[:])
                    return vec.tensor_copy(x2pre[:, 0, 32:40], w_tm[:, 0, :])
                dvg = K.vx(dcpg, wait=[(SVX, v3)])
                ddg = dump("gate0", x2pre.ap()[:, 0, :], [(SVX, dvg)])
                bank_gate[4] = (SOUT, ddg)
            v4 = K.vx(lambda vec, st=st: vec.tensor_copy(wbf[:], w_tm[:, st, :]),
                      wait=[(SVX, v3), (SPE, wprev) if wprev else None])
            def mmwt(te, st=st):
                return te.matmul(ps[5][0:E, 0:128], wbf[:], iden[:],
                                 start=True, stop=True)
            pwt = K.pe(mmwt, wait=[(SVX, v4), bank_gate[5]])
            wprev = pwt
            v5 = K.vx(lambda vec, st=st: vec.tensor_copy(w_fm[:, st, :], ps[5][0:E, 0:128]),
                      wait=[(SPE, pwt)])
            bank_gate[5] = (SVX, v5)
            wfm_done = v5
        for st in range(ST):
            pb = None
            for oh in range(2):
                def mmb(te, st=st, oh=oh):
                    return te.matmul(ps[2 * (st % 2) + oh][:, 0:512], w_fm[:, st, :],
                                     ebT[:, 512 * oh:512 * oh + 512],
                                     start=True, stop=True)
                pb = K.pe(mmb, wait=[(SVX, wfm_done), (SGD, v_const),
                                     bank_gate[2 * (st % 2) + oh]])
            def init3(vec, st=st):
                b0 = ps[2 * (st % 2)]
                b1 = ps[2 * (st % 2) + 1]
                vec.tensor_tensor(x3acc[:, st, 0:512], b0[:, 0:512],
                                  x2tm[:, st, 0:512], ALU.add)
                return vec.tensor_tensor(x3acc[:, st, 512:1024], b1[:, 0:512],
                                         x2tm[:, st, 512:1024], ALU.add)
            iv = K.vx(init3, wait=[(SPE, pb), (SVX, ln_done[st])])
            bank_gate[2 * (st % 2)] = (SVX, iv)
            bank_gate[2 * (st % 2) + 1] = (SVX, iv)
        ev_done = {}
        v_wer = {}
        for e in range(2):
            v_wer[e] = K.gdma(wer[e][:], rg(wet_d[e]), wait=[ring_gate[e]])
        for e in range(E):
            pm = None
            for st in range(ST):
                b0i, b1i = 2 * (st % 2), 2 * (st % 2) + 1
                def mme(te, e=e, st=st, b0i=b0i, b1i=b1i):
                    l = None
                    for hc in range(8):
                        te.matmul(ps[b0i][:, 0:512], fmbuf[:, hc, 128 * st:128 * st + 128],
                                  wer[e % 2][:, hc, 0:512],
                                  start=(hc == 0), stop=(hc == 7))
                        l = te.matmul(ps[b1i][:, 0:512], fmbuf[:, hc, 128 * st:128 * st + 128],
                                      wer[e % 2][:, hc, 512:1024],
                                      start=(hc == 0), stop=(hc == 7))
                    return l
                wt = [(SGD, v_wer[e]), bank_gate[b0i], bank_gate[b1i]]
                pm = K.pe(mme, wait=wt)
                def evac(vec, e=e, st=st, b0i=b0i, b1i=b1i):
                    vec.scalar_tensor_tensor(
                        x3acc[:, st, 0:512], ps[b0i][:, 0:512],
                        w_tm[:, st, e:e + 1], x3acc[:, st, 0:512],
                        op0=ALU.mult, op1=ALU.add)
                    return vec.scalar_tensor_tensor(
                        x3acc[:, st, 512:1024], ps[b1i][:, 0:512],
                        w_tm[:, st, e:e + 1], x3acc[:, st, 512:1024],
                        op0=ALU.mult, op1=ALU.add)
                vv = K.vx(evac, wait=[(SPE, pm)])
                bank_gate[b0i] = (SVX, vv)
                bank_gate[b1i] = (SVX, vv)
                ev_done[(e, st)] = vv
            ring_gate[e % 2] = (SPE, pm)
            if e + 2 < E:
                v_wer[e + 2] = K.gdma(wer[(e + 2) % 2][:], rg(wet_d[e + 2]),
                                      wait=[(SPE, pm)])
            if filler is not None:
                filler(e)
        last_out = None
        for st in range(ST):
            wg = [(SVX, ev_done[(E - 1, st)])]
            if len(out_hist) >= 2:
                wg.append((SOUT, out_hist[-2]))
            lnv = emit_ln(x3acc[:, st, :], wg, [outb[:, st % 2, :]], b=st % 2,
                          dummy=sq_scr[:])
            ov = K.odma(out_d[G * g + 128 * st:G * g + 128 * st + 128, :],
                        outb[:, st % 2, :], wait=[(SVX, lnv)])
            out_hist.append(ov)
            last_out = ov
        return ev_done[(E - 1, ST - 1)], last_out

    xo_last = {}
    xo_rv = []
    out_hist = []
    x2pre_gate = {}
    ring_gate = {0: None, 1: None}

    r0 = emit_projections(0)
    loc0 = emit_local(0, [(SVX, r0["qvx"]), (SVX, r0["kvx"]), (SVX, r0["vvx"]),
                          (SVX, r0["vone"]), (SVX, r0["glv"])])

    import os as _os
    if _os.environ.get("KONEGROUP") == "1" and stage == "attn":
        dump("q0", q_sb.ap().rearrange("p a n -> p (a n)"), [(SVX, r0["qvx"])])
        dump("k0", k_sb.ap().rearrange("p a n -> p (a n)"), [(SVX, r0["kvx"])])
        dump("v0", v_sb.ap().rearrange("p a b n -> p (a b n)"), [(SVX, r0["vvx"]), (SVX, r0["vone"])])
        dump("gl", gl_tm.ap().rearrange("p a n -> p (a n)"), [(SVX, r0["glv"])])
        dump("attnL0", attn_g[0].ap().rearrange("p a n -> p (a n)"), [(SVX, loc0)])
        K.final_wait()
        with nc.Block() as blk:
            K.emit(blk)
        es1.close()
        es.close()
        return nc

    v_xg[1] = K.gdma(xg[:], rg(xT)[:, :, G:G + G + 256], wait=[xg_free[0]])
    v_wk[1] = K.gdma(wbufB[:], rg(wk_d), wait=[xg_free[0]])
    v_wq[1] = K.gdma(wbufA[:], rg(wq_d), wait=[(SPE, r0["wA_free_pe"])])

    r1 = emit_projections(1)
    loc1 = emit_local(1, [(SVX, r1["qvx"]), (SVX, r1["kvx"]), (SVX, r1["vvx"]),
                          (SVX, r1["glv"])])

    v_wgk = K.gdma(wbufA[:, 0:4], rg(wgk_d), wait=[(SPE, r1["wA_free_pe"])])
    pool_step(2 * NPIECE)
    v_wgv = K.gdma(wbufB[:, 0:4], rg(wgv_d), wait=[xg_free[1]])
    gkgv, wgv_pe = emit_pooled_finish(v_wgk, v_wgv)
    v_rout = K.gdma(wbufA[:, 4:8], rg(rout_d), wait=[xg_free[1]])
    v_wout = K.gdma(wbufB[:], rg(wout_d), wait=[(SPE, wgv_pe)])

    dL0 = dump("attnL0", attn_g[0].ap().rearrange("p a n -> p (a n)"), [(SVX, loc0)])
    dL1 = dump("attnL1", attn_g[1].ap().rearrange("p a n -> p (a n)"), [(SVX, loc1)])
    import os
    _lat_lim = int(os.environ.get("KLAT_LIMIT", "0")) or None
    lat0 = emit_latent(0, gkgv, (SVX, r0["gqvx"]), v_rout,
                       extra_gate=(SOUT, dL0) if dL0 else None, only_st=_lat_lim)
    lat1 = emit_latent(1, gkgv, (SVX, r1["gqvx"]), v_rout,
                       extra_gate=(SOUT, dL1) if dL1 else None,
                       only_st=_lat_lim) if _lat_lim is None else lat0
    pe_after_latents = K.s_pe.n
    phase2_ready = (SPE, pe_after_latents)

    dump("q0", q_sb.ap().rearrange("p a n -> p (a n)"), [(SVX, r1["qvx"])])
    dump("k0", k_sb.ap().rearrange("p a n -> p (a n)"), [(SVX, r1["kvx"])])
    dump("gq0", gq_g[0].ap().rearrange("p a n -> p (a n)"), [(SVX, r0["gqvx"])])
    dump("pooled", pooled_tm[:], [(SVX, gkgv)])
    dump("gk", gk_fm.ap().rearrange("p a n -> p (a n)"), [(SVX, gkgv)])
    dump("gl", gl_tm.ap().rearrange("p a n -> p (a n)"), [(SVX, r1["glv"])])
    dump("attn0", attn_g[0].ap().rearrange("p a n -> p (a n)"), [(SVX, lat0)])
    dump("attn1", attn_g[1].ap().rearrange("p a n -> p (a n)"), [(SVX, lat1)])

    if stage == "attn":
        K.final_wait()
        with nc.Block() as blk:
            K.emit(blk)
        es1.close()
        es.close()
        return nc

    es1.close()
    sb2 = lambda n, shp, dty: es.enter_context(nc.sbuf_tensor(n, shp, dty))
    x2tm = sb2("x2tm", [128, ST, D], F32)
    x3acc = sb2("x3acc", [128, ST, D], F32)
    fmbuf = sb2("fmbuf", [128, 8, G], BF16)
    x2bf = sb2("x2bf", [128, 2, D], BF16)
    afm_s = sb2("afm_s", [128, 2, 8, 128], BF16)
    sq_scr = sb2("sq_scr", [128, D], F32)
    outb = sb2("outb", [128, 2, D], F32)
    xo_buf = sb2("xo_buf", [128, 3, D], F32)
    wer = [sb2(f"wer{i}", [128, 8, D], BF16) for i in range(2)]
    ring_gate[0] = phase2_ready
    ring_gate[1] = phase2_ready
    xo_last[0] = phase2_ready
    xo_last[1] = phase2_ready

    ln0, x2ev0 = emit_wout_ln1(0, lat0, v_wout, None)
    dump("x2_0", x2tm.ap().rearrange("p a n -> p (a n)"), [(SVX, ln0[ST - 1])])
    for _st in range(3):
        xo_fetch(1, _st)
    ffn0_done, out0 = emit_ffn(0, x2ev0, ln0)
    dw0 = dump("wtm0", w_tm.ap().rearrange("p a n -> p (a n)"), [(SVX, ffn0_done)])
    dx0 = dump("x3a0", x3acc.ap().rearrange("p a n -> p (a n)"), [(SVX, ffn0_done)])
    g1gate = (SOUT, dx0) if dx0 else (SVX, ffn0_done)
    ln1, x2ev1 = emit_wout_ln1(1, lat1, v_wout, g1gate)
    ffn1_done, out1 = emit_ffn(1, x2ev1, ln1)

    K.final_wait()
    with nc.Block() as blk:
        K.emit(blk)
    es.close()
    return nc



def _host_inputs(inputs):
    x = np.ascontiguousarray(inputs["L0_attn_x"].reshape(S, D))
    xTf = np.ascontiguousarray(x.T)
    expert_wT = np.ascontiguousarray(
        np.asarray(inputs["L0_ffn_expert_w"]).transpose(0, 2, 1))
    gate_wT = np.ascontiguousarray(np.asarray(inputs["L0_ffn_gate_w"]).T)
    gate_b_t = np.ascontiguousarray(
        np.broadcast_to(np.asarray(inputs["L0_ffn_gate_b"]), (128, E)))
    glw = np.asarray(inputs["L0_attn_gate_local_w"])
    grw = np.asarray(inputs["L0_attn_gate_remote_w"])
    glbd = np.zeros((D, NH), dtype=np.float32)
    grbd = np.zeros((D, NH), dtype=np.float32)
    for h in range(NH):
        glbd[64 * h:64 * h + 64, h] = glw[h]
        grbd[64 * h:64 * h + 64, h] = grw[h]
    iden = np.eye(128, dtype=np.float32)
    tri = np.zeros((128, 384), dtype=np.float32)
    kk = np.arange(128)[:, None]
    qq = np.arange(128)[None, :]
    tri[:, 0:128] = (qq >= kk).astype(np.float32)
    tri[:, 128:256] = 1.0
    tri[:, 256:384] = (kk > qq).astype(np.float32)
    psel = np.zeros((128, 256), dtype=np.float32)
    psel[:, 128] = 1.0

    per_core = []
    for c in range(NCORES):
        lo = T * c
        xT_own = np.zeros((D, T + 256), dtype=np.float32)
        if c == 0:
            xT_own[:, 256:] = xTf[:, 0:T]
        else:
            xT_own[:] = xTf[:, lo - 256:lo + T]
        lm = np.zeros((128, 16 * NL), dtype=np.float32)
        for gst in range(16):
            cmax = 8 * c + gst // 2
            lm[:, gst * NL + cmax + 1:(gst + 1) * NL] = NEG
        halov = np.full((128, 1), 0.0 if c == 0 else 1.0, dtype=np.float32)
        per_core.append(dict(
            xT=np.ascontiguousarray(xT_own),
            x_own=np.ascontiguousarray(x[lo:lo + T]),
            x_all=x,
            wq=np.asarray(inputs["L0_attn_wq"]),
            wk=np.asarray(inputs["L0_attn_wk"]),
            wv=np.asarray(inputs["L0_attn_wv"]),
            wgq=np.asarray(inputs["L0_attn_wgq"]),
            wgk=np.asarray(inputs["L0_attn_wgk"]),
            wgv=np.asarray(inputs["L0_attn_wgv"]),
            r_out_w=np.asarray(inputs["L0_attn_r_out_w"]),
            w_out=np.asarray(inputs["L0_attn_w_out"]),
            expert_wT=expert_wT,
            gate_wT=gate_wT,
            gate_b_t=gate_b_t,
            expert_b=np.asarray(inputs["L0_ffn_expert_b"]),
            glw_bd=glbd,
            grw_bd=grbd,
            iden_i=iden,
            trimask_i=tri,
            trihalo_i=tri * (0.0 if c == 0 else 1.0),
            poolsel_i=psel,
            lmask_i=lm,
            halov_i=halov,
        ))
    return per_core


def kernel(**inputs):
    per_core = _host_inputs(inputs)
    nc = build_nc(stage="full")
    res = run_bass_kernel_spmd(nc, per_core, core_ids=list(range(NCORES)))
    out = np.concatenate([res.results[c]["out"] for c in range(NCORES)], axis=0)
    return out.reshape(1, S, D)


# version 11
# speedup vs baseline: 1.0611x; 1.0611x over previous
import contextlib
import numpy as np

import concourse.bass as bass
import concourse.mybir as mybir
from concourse.bass_utils import run_bass_kernel_spmd

F32 = mybir.dt.float32
BF16 = mybir.dt.bfloat16
AX = mybir.AxisListType.X
ALU = mybir.AluOpType
ACT = mybir.ActivationFunctionType

NCORES = 8
S = 16384
D = 1024
T = S // NCORES
G = 1024
NG = T // G
NH = 16
HD = 64
HP = NH // 2
LH = 8
LHD = 64
LD = 512
NL = 64
W = 256
ST = G // 128
CH = (G + 256) // 128
E = 8
NEG = -1.0e30


class Sem:
    def __init__(self, es, nc, name):
        self.h = es.enter_context(nc.semaphore(name))
        self.n = 0


class Kb:
    def __init__(self, nc, es):
        self.nc = nc
        self.steps = []
        self.s_gd = Sem(es, nc, "s_gd")
        self.s_pe = Sem(es, nc, "s_pe")
        self.s_sc = Sem(es, nc, "s_sc")
        self.s_vx = Sem(es, nc, "s_vx")
        self.s_sd = Sem(es, nc, "s_sd")
        self.s_out = Sem(es, nc, "s_out")
        self.pe = self._mk("tensor", self.s_pe, 1)
        self.sc = self._mk("scalar", self.s_sc, 1)
        self.vx = self._mk("vector", self.s_vx, 1)
        self.gd = self._mk("gpsimd", self.s_gd, 16)
        self.sy = self._mk("sync", self.s_sd, 16)
        self.so = self._mk("sync", self.s_out, 16)

    def _mk(self, name, sem, bump):
        def add(fn, wait=None, inc=True):
            if inc:
                sem.n += bump
            tgt = sem.n if inc else None
            wl = [wv for wv in (wait or []) if wv is not None and wv[1] is not None]
            def step(eng, fn=fn, wl=wl, tgt=tgt):
                for (s, v) in wl:
                    eng.wait_ge(s.h, v)
                last = fn(eng)
                if tgt is not None:
                    assert last is not None, f"{name} step returned None with inc"
                    last.then_inc(sem.h, bump)
            self.steps.append((name, step))
            return tgt
        return add

    def gdma(self, dst, src, wait=None):
        return self.gd(lambda g, dst=dst, src=src: g.dma_start(out=dst, in_=src), wait=wait)

    def sdma(self, dst, src, wait=None):
        return self.sy(lambda sy, dst=dst, src=src: sy.dma_start(out=dst, in_=src), wait=wait)

    def odma(self, dst, src, wait=None):
        return self.so(lambda sy, dst=dst, src=src: sy.dma_start(out=dst, in_=src), wait=wait)

    def godma(self, dst, src, wait=None):
        self.s_out.n += 16
        tgt = self.s_out.n
        wl = [wv for wv in (wait or []) if wv is not None and wv[1] is not None]
        def step(eng, dst=dst, src=src, wl=wl, tgt=tgt):
            for (s, v) in wl:
                eng.wait_ge(s.h, v)
            eng.dma_start(out=dst, in_=src).then_inc(self.s_out.h, 16)
        self.steps.append(("gpsimd", step))
        return tgt

    def final_wait(self):
        v = self.s_out.n
        self.steps.append(("sync", lambda sy, v=v: sy.wait_ge(self.s_out.h, v)))

    def emit(self, blk):
        for name in ("sync", "gpsimd", "tensor", "scalar", "vector"):
            fns = [f for (e, f) in self.steps if e == name]
            if not fns:
                continue
            def body(eng, fns=fns):
                for f in fns:
                    f(eng)
            getattr(blk, name)(body)


def ap_off(t, extra_off, pattern):
    a = t.ap() if hasattr(t, "ap") else t
    return bass.AP(a.tensor, a.offset + extra_off, pattern)


def build_nc(stage="full", debug=()):
    nc = bass.Bass("TRN2", target_bir_lowering=False)
    es = contextlib.ExitStack()

    di = lambda n, shp: nc.dram_tensor(n, shp, F32, kind="ExternalInput").ap()
    xT = di("xT", [D, T + 256])
    x_own = di("x_own", [T, D])
    x_all = di("x_all", [S, D])
    wq_d = di("wq", [D, D]); wk_d = di("wk", [D, D]); wv_d = di("wv", [D, D])
    wgq_d = di("wgq", [D, LD]); wgk_d = di("wgk", [D, LD]); wgv_d = di("wgv", [D, LD])
    rout_d = di("r_out_w", [LD, D])
    wout_d = di("w_out", [D, D])
    wet_d = di("expert_wT", [E, D, D])
    gwt_d = di("gate_wT", [D, E])
    gbt_d = di("gate_b_t", [128, E])
    eb_d = di("expert_b", [E, D])
    glbd_d = di("glw_bd", [D, NH])
    grbd_d = di("grw_bd", [D, NH])
    iden_d = di("iden_i", [128, 128])
    tri_d = di("trimask_i", [128, 384])
    trih_d = di("trihalo_i", [128, 384])
    psel_d = di("poolsel_i", [128, 256])
    lmask_d = di("lmask_i", [128, 16 * NL])
    halov_d = di("halov_i", [128, 1])
    out_d = nc.dram_tensor("out", [T, D], F32, kind="ExternalOutput").ap()
    dbg = {}
    for name, shp in debug:
        dbg[name] = nc.dram_tensor("dbg_" + name, list(shp), F32, kind="ExternalOutput").ap()

    sbt = lambda n, shp, dty: es.enter_context(nc.sbuf_tensor(n, shp, dty))
    iden = sbt("iden", [128, 128], BF16)
    gwT = sbt("gwT", [128, 8, E], BF16)
    gbt = sbt("gbt", [128, E], F32)
    ebT = sbt("ebT", [E, D], BF16)
    sc_comb = sbt("sc_comb", [128, 8], F32)
    rd4 = sbt("rd4", [128, 8], F32)
    x2pre = sbt("x2pre", [128, 2, D], F32)
    s1 = sbt("s1", [128, 2], F32); s2 = sbt("s2", [128, 2], F32)
    mvv = sbt("mvv", [128, 2], F32)
    varv = sbt("varv", [128, 2], F32)
    lnl = sbt("lnl", [128, 2], F32)
    rstd = sbt("rstd", [128, 2], F32); nmr = sbt("nmr", [128, 2], F32)
    gsb = sbt("gsb", [128, E], F32)
    gsb2 = sbt("gsb2", [128, E], F32)
    gnm = sbt("gnm", [128, 1], F32)
    gden = sbt("gden", [128, 1], F32)
    grd = sbt("grd", [128, 1], F32)
    gmx = sbt("gmx", [128, E], F32)
    gind = sbt("gind", [128, E], F32)
    w_tm = sbt("w_tm", [128, ST, E], F32)
    wbf = sbt("wbf", [128, E], BF16)
    w_fm = sbt("w_fm", [8, ST, 128], BF16)
    attn_g = [sbt(f"attn{g}", [128, ST, D], BF16) for g in range(NG)]
    wbufB = sbt("wbufB", [128, 8, D], BF16)
    es1 = contextlib.ExitStack()
    sb1 = lambda n, shp, dty: es1.enter_context(nc.sbuf_tensor(n, shp, dty))
    trimask = sb1("trimask", [128, 384], BF16)
    trihalo = sb1("trihalo", [128, 384], BF16)
    poolsel = sb1("poolsel", [128, 256], BF16)
    glwbd = sb1("glwbd", [128, 8, NH], BF16)
    grwbd = sb1("grwbd", [128, 8, NH], BF16)
    lmaskb = sb1("lmaskb", [128, 16 * NL], BF16)
    halov = sb1("halov", [128, 1], F32)
    pooled_tm = sb1("pooled_tm", [NL, D], BF16)
    pooled_fm = sb1("pooled_fm", [128, 8, NL], BF16)
    gk_fm = sb1("gk_fm", [128, 4, NL], BF16)
    gv_fm = sb1("gv_fm", [128, 4, NL], BF16)
    gv_tm = sb1("gv_tm", [128, 4, NL], BF16)
    lp_sb = sb1("lp_sb", [128, 2, 8, NL], BF16)
    lpt2 = sb1("lpt2", [128, 2, 4, 128], BF16)
    lden = sb1("lden", [128, 2, 8], F32)
    lrd = sb1("lrd", [128, 2, 8], F32)
    gl_tm = sb1("gl_tm", [128, ST, NH], F32)
    gr_tm = [sb1(f"gr_tm{g}", [128, ST, NH], F32) for g in range(NG)]
    glfm = sb1("glfm", [16, G], BF16)
    sige = sb1("sige", [128, 128], F32)
    pbt = sb1("pbt", [128, 7, 2, 384], BF16)
    rem_fm = sb1("rem_fm", [128, 2, 4, 128], BF16)
    wbufA = sb1("wbufA", [128, 8, D], BF16)
    gq_g = [sb1(f"gq{g}", [128, 4, G], BF16) for g in range(NG)]
    q_sb = sb1("q_sb", [128, HP, G], BF16)
    k_sb = sb1("k_sb", [128, HP, G + 256], BF16)
    v_sb = sb1("v_sb", [128, CH, NH, 65], BF16)
    xg = sb1("xg", [128, 8, G + 256], BF16)
    poolb = sb1("poolb", [128, 2, 2, D], BF16)
    x2tm = x2fm = x3acc = fmbuf = x2bf = sq_scr = outb = xo_buf = None
    wer = None

    ps = [es.enter_context(nc.psum_tensor(f"ps{i}", [128, 512], F32)) for i in range(6)]
    ps_pool = es.enter_context(nc.psum_tensor("ps_pool", [128, 1024], F32))

    K = Kb(nc, es)
    SGD, SPE, SSC, SVX, SSD, SOUT = K.s_gd, K.s_pe, K.s_sc, K.s_vx, K.s_sd, K.s_out
    bank_gate = [None] * 6

    def dump(name, src_ap, wait):
        if name in dbg:
            return K.godma(dbg[name], src_ap, wait=wait)
        return None

    rg = lambda d: d.rearrange("(a p) n -> p a n", p=128)
    K.gdma(iden[:], iden_d)
    K.gdma(trimask[:], tri_d)
    K.gdma(trihalo[:], trih_d)
    K.gdma(poolsel[:], psel_d)
    K.gdma(glwbd[:], rg(glbd_d))
    K.gdma(grwbd[:], rg(grbd_d))
    K.gdma(gwT[:], rg(gwt_d))
    K.gdma(gbt[:], gbt_d)
    K.gdma(ebT[:], eb_d)
    K.gdma(lmaskb[:], lmask_d)
    v_const = K.gdma(halov[:], halov_d)

    v_xg = [None, None]
    v_wq = [None, None]; v_wk = [None, None]; v_wv = [None, None]; v_wgq = [None, None]
    v_xg[0] = K.gdma(xg[:], rg(xT)[:, :, 0:G + 256])
    v_wq[0] = K.gdma(wbufA[:], rg(wq_d))
    v_wk[0] = K.gdma(wbufB[:], rg(wk_d))

    NPIECE = S // 256
    pool_dma_v = []
    pool_pe_v = []

    def _pool_dma1():
        i = len(pool_dma_v)
        w = [(SPE, pool_pe_v[i - 2])] if i >= 2 else None
        v = K.gdma(poolb[:, i % 2], rg(x_all[256 * i:256 * i + 256, :]), wait=w)
        pool_dma_v.append(v)

    def pool_step(n):
        for _ in range(n):
            i, j = len(pool_dma_v), len(pool_pe_v)
            if i >= NPIECE and j >= NPIECE:
                return
            if i < NPIECE and i <= j + 1:
                _pool_dma1()
            else:
                _pool_mm1()

    def _pool_mm1():
            i = len(pool_pe_v)
            if i >= NPIECE or i >= len(pool_dma_v):
                return
            def mm(te, i=i):
                last = None
                p = i
                for s4 in range(2):
                    for hf in range(2):
                        last = te.matmul(ps_pool[:, 512 * hf:512 * hf + 512],
                                         poolsel[:, 128 - p:256 - p],
                                         poolb[:, i % 2, s4, 512 * hf:512 * hf + 512],
                                         start=(i == 0 and s4 == 0),
                                         stop=(i == NPIECE - 1 and s4 == 1))
                return last
            v = K.pe(mm, wait=[(SGD, pool_dma_v[i])])
            pool_pe_v.append(v)

    def proj_fm(dst_fn, wsb, w_v, xg_lo, xg_n, nchunk, v_xg_cur, wfree=1024):
        nblk = (xg_n + 511) // 512
        last_pe = last_vx = None
        wflat = wsb.ap().rearrange("p a n -> p (a n)")
        for i in range(nchunk):
            for nb in range(nblk):
                n0 = 512 * nb
                nn = min(xg_n, n0 + 512) - n0
                bi = (i * nblk + nb) % 2
                bank = ps[bi]
                def mm(te, i=i, n0=n0, nn=nn, bank=bank, wflat=wflat, wfree=wfree):
                    last = None
                    for hc in range(8):
                        last = te.matmul(bank[:, 0:nn],
                                         wflat[:, wfree * hc + 128 * i:wfree * hc + 128 * i + 128],
                                         xg[:, hc, xg_lo + n0:xg_lo + n0 + nn],
                                         start=(hc == 0), stop=(hc == 7))
                    return last
                last_pe = K.pe(mm, wait=[(SGD, max(w_v, v_xg_cur)), bank_gate[bi]])
                def ev(vec, i=i, n0=n0, nn=nn, bank=bank, dst_fn=dst_fn):
                    return vec.tensor_copy(dst_fn(i, n0, nn), bank[:, 0:nn])
                last_vx = K.vx(ev, wait=[(SPE, last_pe)])
                bank_gate[bi] = (SVX, last_vx)
        return last_pe, last_vx

    def emit_glgr(g, qvx_gate):
        last = None
        for (wsb_, dst) in ((glwbd, gl_tm), (grwbd, gr_tm[g])):
            ev = None
            for nb in range(2):
                bank = ps[4]
                def mm(te, wsb_=wsb_, nb=nb, bank=bank):
                    l = None
                    for hpp in range(HP):
                        l = te.matmul(bank[0:16, 0:512], wsb_[:, hpp, :],
                                      q_sb[:, hpp, 512 * nb:512 * nb + 512],
                                      start=(hpp == 0), stop=(hpp == HP - 1))
                    return l
                pv = K.pe(mm, wait=[(SVX, qvx_gate), (SGD, v_const), bank_gate[4]])
                ev = K.vx(lambda vec, nb=nb, bank=bank: vec.tensor_copy(
                    glfm[:, 512 * nb:512 * nb + 512], bank[0:16, 0:512]),
                    wait=[(SPE, pv)])
                bank_gate[4] = (SVX, ev)
            bank2 = ps[5]
            def mmt(te, bank2=bank2):
                l = None
                for st in range(ST):
                    l = te.matmul(bank2[:, 16 * st:16 * st + 16],
                                  glfm[:, 128 * st:128 * st + 128],
                                  iden[0:16, 0:16], start=True, stop=True)
                return l
            pv2 = K.pe(mmt, wait=[(SVX, ev), bank_gate[5]])
            c1 = K.sc(lambda s, bank2=bank2: s.activation(
                sige[:], bank2[:, 0:128], ACT.Exp, scale=-1.0),
                wait=[(SPE, pv2)])
            bank_gate[5] = (SSC, c1)
            sa = K.vx(lambda vec: vec.tensor_scalar_add(sige[:], sige[:], 1.0),
                      wait=[(SSC, c1)])
            last = K.vx(lambda vec, dst=dst: vec.reciprocal(
                dst.ap().rearrange("p a n -> p (a n)"), sige[:]), wait=[(SVX, sa)])
        return last

    xg_free = [None, None]

    def emit_projections(g):
        res = {}
        vxc = v_xg[g]
        qpe, qvx = proj_fm(lambda i, n0, nn: q_sb.ap()[:, i, n0:n0 + nn],
                           wbufA, v_wq[g], 256, G, HP, vxc)
        pool_step(4 if g == 0 else 10)
        glv = emit_glgr(g, qvx)
        v_wv[g] = K.gdma(wbufA[:], rg(wv_d), wait=[(SPE, qpe)])
        kpe, kvx = proj_fm(lambda i, n0, nn: k_sb.ap()[:, i, n0:n0 + nn],
                           wbufB, v_wk[g], 0, G + 256, HP, vxc)
        pool_step(4 if g == 0 else 10)
        v_wgq[g] = K.gdma(wbufB[:, 0:4], rg(wgq_d), wait=[(SPE, kpe)])
        vvx = vpe = None
        for sc_ in range(CH):
            for oh in range(2):
                bi = (sc_ * 2 + oh) % 2
                bank = ps[bi]
                def mm(te, sc_=sc_, oh=oh, bank=bank):
                    last = None
                    for hc in range(8):
                        last = te.matmul(bank[:, 0:512],
                                         xg[:, hc, 128 * sc_:128 * sc_ + 128],
                                         wbufA[:, hc, 512 * oh:512 * oh + 512],
                                         start=(hc == 0), stop=(hc == 7))
                    return last
                vpe = K.pe(mm, wait=[(SGD, max(v_wv[g], vxc)), bank_gate[bi]])
                def ev(vec, sc_=sc_, oh=oh, bank=bank):
                    dst = v_sb.ap()[:, sc_, 8 * oh:8 * oh + 8, 0:64]
                    src = bass.AP(bank.ap().tensor, bank.ap().offset,
                                  [[512, 128], [64, 8], [1, 64]])
                    return vec.tensor_copy(dst, src)
                vvx = K.vx(ev, wait=[(SPE, vpe)])
                bank_gate[bi] = (SVX, vvx)
        vone = K.vx(lambda vec: vec.memset(v_sb.ap()[:, :, :, 64:65], 1.0)) if g == 0 else None
        pool_step(4 if g == 0 else 10)
        res["wA_free_pe"] = vpe
        gqpe, gqvx = proj_fm(lambda i, n0, nn: gq_g[g].ap()[:, i, n0:n0 + nn],
                             wbufB, v_wgq[g], 256, G, 4, vxc, wfree=512)
        pool_step(6 if g == 0 else 16)
        xg_free[g] = (SPE, gqpe)
        res.update(qvx=qvx, kvx=kvx, vvx=vvx, vone=vone, glv=glv,
                   gqvx=gqvx, gqpe=gqpe, qpe=qpe)
        return res

    def emit_local(g, pre_gates):
        attn = attn_g[g]
        first = True
        slot_pv = {}
        units = []
        for hpp in range(HP):
            for half in range(2):
                tlo_h, thi_h = 4 * half, 4 * half + 4
                for ca in range(tlo_h, thi_h + 2):
                    tlo = max(tlo_h, ca - 2)
                    thi = min(thi_h, ca + 1)
                    if tlo < thi:
                        units.append((hpp, half, ca, tlo, thi))
        uidx = {(u[0], u[1], u[2]): i for i, u in enumerate(units)}
        pvjobs = []
        for hpp in range(HP):
            for half in range(2):
                for t in range(4 * half, 4 * half + 4):
                    pvjobs.append((hpp, half, t, uidx[(hpp, half, t + 2)]))
        pvjobs.sort(key=lambda j: j[3])
        LAG = 5
        mask_val = {}
        span_of = {}
        slot_of = {}
        pv_emitted = 0
        finv = None
        fin_gate = None
        last_pv_of_half = {}

        def emit_pv(j):
            nonlocal finv, fin_gate
            (hpp, half, t, _) = pvjobs[j]
            parts = tuple((t + jj, slot_of[(hpp, half, t + jj)],
                           t - span_of[(hpp, half, t + jj)][0]) for jj in range(3))
            def mmpv(te, t=t, hpp=hpp, half=half, parts=parts):
                last = None
                for jj, (cj, slj, blk) in enumerate(parts):
                    for par in range(2):
                        head = 2 * hpp + par
                        acc = ps[par][:, 66 * (t - 4 * half):66 * (t - 4 * half) + 65]
                        last = te.matmul(acc, pbt[:, slj, par, 128 * blk:128 * blk + 128],
                                         v_sb[:, cj, head, :],
                                         start=(jj == 0), stop=(jj == 2))
                return last
            wpv = [(SVX, mask_val[(hpp, half, t + 2)])]
            if t == 4 * half:
                wpv += [bank_gate[0], bank_gate[1]]
            pvv = K.pe(mmpv, wait=wpv)
            for jj in range(3):
                slot_pv[slot_of[(hpp, half, t + jj)]] = (SPE, pvv)
            if t == 4 * half + 3:
                def fin_a(vec):
                    vec.reciprocal(rd4[:, 0:4], ap_off(ps[0], 64, [[512, 128], [66, 4]]))
                    return vec.reciprocal(rd4[:, 4:8], ap_off(ps[1], 64, [[512, 128], [66, 4]]))
                fa = K.vx(fin_a, wait=[(SPE, pvv)])
                def fin_b(vec, half=half, hpp=hpp):
                    r = None
                    for par in range(2):
                        head = 2 * hpp + par
                        glc = ap_off(gl_tm, (4 * half) * NH + head, [[ST * NH, 128], [NH, 4]])
                        r = vec.tensor_tensor(sc_comb[:, 4 * par:4 * par + 4],
                                              rd4[:, 4 * par:4 * par + 4], glc, ALU.mult)
                    return r
                fb = K.vx(fin_b, wait=[(SVX, fa)])
                def fin(vec, half=half, hpp=hpp, attn=attn):
                    r = None
                    for par in range(2):
                        head = 2 * hpp + par
                        for t4 in range(4):
                            tt = 4 * half + t4
                            r = vec.tensor_scalar_mul(
                                attn[:, tt, 64 * head:64 * head + 64],
                                ps[par][:, 66 * t4:66 * t4 + 64],
                                sc_comb[:, 4 * par + t4:4 * par + t4 + 1])
                    return r
                finv = K.vx(fin, wait=[(SVX, fb)])
                bank_gate[0] = (SVX, finv)
                bank_gate[1] = (SVX, finv)

        for i, (hpp, half, ca, tlo, thi) in enumerate(units):
            while pv_emitted < len(pvjobs) and pvjobs[pv_emitted][3] <= i - LAG:
                emit_pv(pv_emitted)
                pv_emitted += 1
            nn = 128 * (thi - tlo)
            span_of[(hpp, half, ca)] = (tlo, thi)
            sl = i % 7
            slot_of[(hpp, half, ca)] = sl
            be, bo = 2 + (i % 2) * 2, 3 + (i % 2) * 2
            def mmqk(te, hpp=hpp, ca=ca, tlo=tlo, nn=nn, be=be, bo=bo):
                te.matmul(ps[be][:, 0:nn],
                          k_sb[0:64, hpp, 128 * ca:128 * ca + 128],
                          q_sb[0:64, hpp, 128 * tlo:128 * tlo + nn],
                          start=True, stop=True)
                return te.matmul(ps[bo][:, 0:nn],
                                 k_sb[64:128, hpp, 128 * ca:128 * ca + 128],
                                 q_sb[64:128, hpp, 128 * tlo:128 * tlo + nn],
                                 start=True, stop=True, tile_position=(64, 0))
            wt = list(pre_gates) if first else []
            first = False
            wt += [bank_gate[be], bank_gate[bo]]
            qk = K.pe(mmqk, wait=wt)
            wexp = [(SPE, qk), slot_pv.get(sl)]
            def ex(sc_, sl=sl, nn=nn, be=be, bo=bo):
                sc_.activation(pbt[:, sl, 0, 0:nn], ps[be][:, 0:nn], ACT.Exp, scale=0.125)
                return sc_.activation(pbt[:, sl, 1, 0:nn], ps[bo][:, 0:nn],
                                      ACT.Exp, scale=0.125)
            exv = K.sc(ex, wait=wexp)
            bank_gate[be] = (SSC, exv)
            bank_gate[bo] = (SSC, exv)
            rlo = tlo + 2 - ca
            msk = trihalo if (g == 0 and ca < 2) else trimask
            def mk(vec, sl=sl, nn=nn, rlo=rlo, msk=msk):
                vec.tensor_tensor(pbt[:, sl, 0, 0:nn], pbt[:, sl, 0, 0:nn],
                                  msk[:, 128 * rlo:128 * rlo + nn], ALU.mult)
                return vec.tensor_tensor(pbt[:, sl, 1, 0:nn], pbt[:, sl, 1, 0:nn],
                                         msk[:, 128 * rlo:128 * rlo + nn], ALU.mult)
            mkv = K.vx(mk, wait=[(SSC, exv)])
            mask_val[(hpp, half, ca)] = mkv
            if i % 12 == 0:
                pool_step(3)
        while pv_emitted < len(pvjobs):
            emit_pv(pv_emitted)
            pv_emitted += 1
        return finv

    def emit_pooled_finish(v_wgk, v_wgv):
        pe_last = pool_pe_v[-1]
        c1 = K.sc(lambda s: s.activation(pooled_tm[:], ps_pool[0:64, :], ACT.Copy,
                                         scale=1.0 / 256), wait=[(SPE, pe_last)])
        def mmt(te):
            l = None
            for hc in range(8):
                l = te.matmul(ps[0][:, 64 * hc:64 * hc + 64],
                              pooled_tm[:, 128 * hc:128 * hc + 128],
                              iden[0:64, 0:64], start=True, stop=True)
            return l
        p1 = K.pe(mmt, wait=[(SSC, c1), bank_gate[0]])
        vfm = K.vx(lambda vec: vec.tensor_copy(
            pooled_fm.ap().rearrange("p a n -> p (a n)"), ps[0][:, 0:512]),
            wait=[(SPE, p1)])
        bank_gate[0] = (SVX, vfm)
        evs = []
        pvl = None
        for (wv_, wsb_, dst, bi) in ((v_wgk, wbufA, gk_fm, 1), (v_wgv, wbufB, gv_fm, 2)):
            wflat = wsb_.ap().rearrange("p a n -> p (a n)")
            def mm(te, wflat=wflat, bi=bi):
                l = None
                for lc in range(4):
                    for hc in range(8):
                        l = te.matmul(ps[bi][:, 64 * lc:64 * lc + 64],
                                      wflat[:, 512 * hc + 128 * lc:512 * hc + 128 * lc + 128],
                                      pooled_fm[:, hc, :],
                                      start=(hc == 0), stop=(hc == 7))
                return l
            pvl = K.pe(mm, wait=[(SGD, wv_), (SVX, vfm), bank_gate[bi]])
            ev = K.vx(lambda vec, dst=dst, bi=bi: vec.tensor_copy(
                dst.ap().rearrange("p a n -> p (a n)"), ps[bi][:, 0:256]),
                wait=[(SPE, pvl)])
            bank_gate[bi] = (SVX, ev)
            evs.append(ev)
        tvs = None
        for lc in range(4):
            def mmtv(te, lc=lc):
                te.matmul(ps[3][0:64, 0:64], gv_fm[0:64, lc, :], iden[0:64, 0:64],
                          start=True, stop=True)
                return te.matmul(ps[4][64:128, 0:64], gv_fm[64:128, lc, :],
                                 iden[64:128, 64:128], start=True, stop=True,
                                 tile_position=(64, 64))
            wtv = [(SVX, evs[-1]), bank_gate[3], bank_gate[4]]
            ptv = K.pe(mmtv, wait=wtv)
            def evtv(vec, lc=lc):
                vec.tensor_copy(gv_tm[0:64, lc, :], ps[3][0:64, 0:64])
                return vec.tensor_copy(gv_tm[64:128, lc, :], ps[4][64:128, 0:64])
            tvs = K.vx(evtv, wait=[(SPE, ptv)])
            bank_gate[3] = (SVX, tvs)
            bank_gate[4] = (SVX, tvs)
        return tvs, pvl

    def emit_latent(g, gkgv_v, gq_gate, v_rout, extra_gate=None, only_st=None):
        attn = attn_g[g]
        lp_gate = {}
        lpt_gate = {}
        rem_gate = {}
        st_state = {}
        nst = ST if only_st is None else only_st

        def s1_lsc(st):
            b = st % 2
            gst0 = ST * g + st
            def mml(te, st=st, g=g, gst0=gst0):
                last = None
                onesrow = trimask[0:1, 128:256]
                lrow = lmaskb.ap()[0:1, NL * gst0:NL * gst0 + NL]
                for lp_ in range(4):
                    te.matmul(ps[2][:, 64 * lp_:64 * lp_ + 64],
                              gq_g[g][0:64, lp_, 128 * st:128 * st + 128],
                              gk_fm[0:64, lp_, :], start=True, stop=False)
                    te.matmul(ps[3][:, 64 * lp_:64 * lp_ + 64],
                              gq_g[g][64:128, lp_, 128 * st:128 * st + 128],
                              gk_fm[64:128, lp_, :], start=True, stop=False,
                              tile_position=(64, 0))
                    te.matmul(ps[2][:, 64 * lp_:64 * lp_ + 64], onesrow, lrow,
                              start=False, stop=True)
                    last = te.matmul(ps[3][:, 64 * lp_:64 * lp_ + 64], onesrow, lrow,
                                     start=False, stop=True)
                return last
            wt = [bank_gate[2], bank_gate[3]]
            if st == 0 and g == 0:
                wt += [(SVX, gkgv_v)]
            if st == 0:
                wt += [gq_gate]
                if extra_gate is not None:
                    wt.append(extra_gate)
            return K.pe(mml, wait=wt)

        def s2_softmax(st, p1):
            b = st % 2
            wex = [(SPE, p1), lp_gate.get(b)]
            def ex(s, b=b):
                r = None
                for lp_ in range(4):
                    s.activation(lp_sb[:, b, 2 * lp_, :], ps[2][:, 64 * lp_:64 * lp_ + 64],
                                 ACT.Exp, scale=0.125,
                                 accum_out=lden[:, b, 2 * lp_:2 * lp_ + 1])
                    r = s.activation(lp_sb[:, b, 2 * lp_ + 1, :], ps[3][:, 64 * lp_:64 * lp_ + 64],
                                     ACT.Exp, scale=0.125,
                                     accum_out=lden[:, b, 2 * lp_ + 1:2 * lp_ + 2])
                return r
            c1 = K.sc(ex, wait=wex)
            bank_gate[2] = (SSC, c1)
            bank_gate[3] = (SSC, c1)
            vr = K.vx(lambda vec, b=b: vec.reciprocal(lrd[:, b, :], lden[:, b, :]),
                      wait=[(SSC, c1)])
            def nrm(vec, b=b):
                r = None
                for lh in range(8):
                    r = vec.tensor_scalar_mul(lp_sb[:, b, lh, :], lp_sb[:, b, lh, :],
                                              lrd[:, b, lh:lh + 1])
                return r
            return K.vx(nrm, wait=[(SVX, vr)])

        def s3_transpose(st, v2):
            b = st % 2
            def mmt(te, b=b):
                last = None
                for lp_ in range(4):
                    te.matmul(ps[4][0:64, 128 * lp_:128 * lp_ + 128],
                              lp_sb[:, b, 2 * lp_, :], iden[:], start=True, stop=True)
                    last = te.matmul(ps[5][64:128, 128 * lp_:128 * lp_ + 128],
                                     lp_sb[:, b, 2 * lp_ + 1, :], iden[:],
                                     start=True, stop=True, tile_position=(0, 64))
                return last
            p2 = K.pe(mmt, wait=[(SVX, v2), bank_gate[4], bank_gate[5]])
            lp_gate[b] = (SPE, p2)
            def evt(sc, b=b):
                sc.activation(lpt2.ap()[0:64, b].rearrange("p a n -> p (a n)"),
                              ps[4][0:64, 0:512], ACT.Copy)
                return sc.activation(lpt2.ap()[64:128, b].rearrange("p a n -> p (a n)"),
                                     ps[5][64:128, 0:512], ACT.Copy)
            v3 = K.sc(evt, wait=[(SPE, p2), lpt_gate.get(b)])
            bank_gate[4] = (SSC, v3)
            bank_gate[5] = (SSC, v3)
            return v3

        def s5_pv(st, v3):
            b = st % 2
            def mmp(te, b=b):
                last = None
                for lp_ in range(4):
                    te.matmul(ps[0][0:64, 128 * lp_:128 * lp_ + 128],
                              gv_tm[0:64, lp_, :], lpt2[0:64, b, lp_, :],
                              start=True, stop=True)
                    last = te.matmul(ps[1][64:128, 128 * lp_:128 * lp_ + 128],
                                     gv_tm[64:128, lp_, :], lpt2[64:128, b, lp_, :],
                                     start=True, stop=True, tile_position=(64, 64))
                return last
            p3 = K.pe(mmp, wait=[(SSC, v3), bank_gate[0], bank_gate[1]])
            lpt_gate[b] = (SPE, p3)
            def evp(sc, b=b):
                sc.activation(rem_fm.ap()[0:64, b].rearrange("p a n -> p (a n)"),
                              ps[0][0:64, 0:512], ACT.Copy)
                return sc.activation(rem_fm.ap()[64:128, b].rearrange("p a n -> p (a n)"),
                                     ps[1][64:128, 0:512], ACT.Copy)
            v4 = K.sc(evp, wait=[(SPE, p3), rem_gate.get(b)])
            bank_gate[0] = (SSC, v4)
            bank_gate[1] = (SSC, v4)
            return v4

        def s7_rout(st, v4):
            b = st % 2
            prs = []
            for oh in range(2):
                def mmr(te, oh=oh, b=b):
                    l = None
                    for lc in range(4):
                        l = te.matmul(ps[oh][:, 0:512], rem_fm[:, b, lc, :],
                                      wbufA[:, 4 + lc, 512 * oh:512 * oh + 512],
                                      start=(lc == 0), stop=(lc == 3))
                    return l
                wt = [(SSC, v4), bank_gate[oh]]
                if st == 0 and g == 0:
                    wt.append((SGD, v_rout))
                pr = K.pe(mmr, wait=wt)
                prs.append(pr)
            rem_gate[b] = (SPE, prs[-1])
            last = None
            for oh in range(2):
                def comb(vec, oh=oh, st=st, g=g, attn=attn):
                    r = None
                    for hh in range(8):
                        head = 8 * oh + hh
                        r = vec.scalar_tensor_tensor(
                            attn[:, st, 64 * head:64 * head + 64],
                            ps[oh][:, 64 * hh:64 * hh + 64],
                            gr_tm[g][:, st, head:head + 1],
                            attn[:, st, 64 * head:64 * head + 64],
                            op0=ALU.mult, op1=ALU.add)
                    return r
                cv = K.vx(comb, wait=[(SPE, prs[oh]), (SPE, prs[-1])])
                bank_gate[oh] = (SVX, cv)
                last = cv
            return last

        last = None
        for st in range(nst + 1):
            if st >= 1:
                stp = st - 1
                v3 = s3_transpose(stp, st_state[stp])
                v4 = s5_pv(stp, v3)
            if st < nst:
                p1 = s1_lsc(st)
                v2 = s2_softmax(st, p1)
                st_state[st] = v2
            if st >= 1:
                last = s7_rout(st - 1, v4)
        return last

    ln_prev = {}

    def emit_ln(src_ap, wait_gates, dsts, b=0, dummy=None):
        sq = dummy if dummy is not None else sq_scr.ap()[:, b, :]
        wg = list(wait_gates)
        if b in ln_prev:
            wg.append(ln_prev[b])
        def sums(s):
            s.activation(sq, src_ap, ACT.Copy, accum_out=s1[:, b:b + 1])
            return s.activation(sq, src_ap, ACT.Square, accum_out=s2[:, b:b + 1])
        c1 = K.sc(sums, wait=wg)
        v1 = c1
        def stats_a(vec):
            vec.tensor_scalar_mul(mvv[:, b:b + 1], s1[:, b:b + 1], -1.0 / D)
            return vec.tensor_scalar_mul(varv[:, b:b + 1], s2[:, b:b + 1], 1.0 / D)
        v2a = K.vx(stats_a, wait=[(SSC, c1)])
        v2b = K.vx(lambda vec: vec.scalar_tensor_tensor(
            lnl[:, b:b + 1], mvv[:, b:b + 1], mvv[:, b:b + 1], varv[:, b:b + 1],
            op0=ALU.mult, op1=ALU.subtract), wait=[(SVX, v2a)])
        v2 = K.vx(lambda vec: vec.tensor_scalar(
            varv[:, b:b + 1], lnl[:, b:b + 1], -1.0, 1e-5,
            op0=ALU.mult, op1=ALU.add), wait=[(SVX, v2b)])
        c2 = K.sc(lambda s: s.activation(lnl[:, b:b + 1], varv[:, b:b + 1], ACT.Ln),
                  wait=[(SVX, v2)])
        c3 = K.sc(lambda s: s.activation(rstd[:, b:b + 1], lnl[:, b:b + 1], ACT.Exp,
                                         scale=-0.5), wait=[(SSC, c2)])
        v3 = K.vx(lambda vec: vec.tensor_tensor(nmr[:, b:b + 1], mvv[:, b:b + 1],
                                                rstd[:, b:b + 1], ALU.mult),
                  wait=[(SSC, c3)])
        def norm(vec):
            r = None
            for d_ in dsts:
                r = vec.tensor_scalar(d_, src_ap, rstd[:, b:b + 1], nmr[:, b:b + 1],
                                      op0=ALU.mult, op1=ALU.add)
            return r
        out = K.vx(norm, wait=[(SVX, v3)])
        ln_prev[b] = (SVX, out)
        return out

    xo_q = {}

    def xo_fetch(g, st):
        if st >= ST:
            return
        w = [xo_last.get(st % 3)]
        xo_q[(g, st)] = K.sdma(xo_buf[:, st % 3],
                               x_own[G * g + 128 * st:G * g + 128 * st + 128, :], wait=w)

    def make_wout(g, attn_done, v_wout, prev_x2_free, wout_banks=(2, 3)):
        attn = attn_g[g]
        ln_done = {}
        x2fm_ev = {}
        tr_prev = {}
        afm_gate = {}
        wb0, wb1 = wout_banks
        for st in range(3):
            if (g, st) not in xo_q:
                xo_fetch(g, st)

        tvs_of = {}

        def stage_A1(st):
            tvs = []
            for b2 in range(2):
                bank = ps[4 + b2]
                def mmt(te, st=st, b2=b2, bank=bank, attn=attn):
                    l = None
                    for fc4 in range(4):
                        fc = 4 * b2 + fc4
                        l = te.matmul(bank[:, 128 * fc4:128 * fc4 + 128],
                                      attn[:, st, 128 * fc:128 * fc + 128],
                                      iden[:], start=True, stop=True)
                    return l
                wt = [bank_gate[4 + b2], afm_gate.get(st % 2)]
                if st == 0:
                    wt.append((SVX, attn_done))
                    wt.append(prev_x2_free)
                ptv = K.pe(mmt, wait=wt)
                def evt(vec, st=st, b2=b2, bank=bank):
                    return vec.tensor_copy(afm_s.ap()[:, st % 2, 4 * b2:4 * b2 + 4, :],
                                           bank[:, 0:512])
                tv = K.vx(evt, wait=[(SPE, ptv)])
                bank_gate[4 + b2] = (SVX, tv)
                tvs.append(tv)
            tvs_of[st] = tvs

        def stage_A2(st):
            tvs = tvs_of[st]
            pw = None
            for oh in range(2):
                bi = wb0 if oh == 0 else wb1
                def mmw(te, st=st, oh=oh, bi=bi):
                    l = None
                    for fc in range(8):
                        l = te.matmul(ps[bi][:, 0:512],
                                      afm_s[:, st % 2, fc, :],
                                      wbufB[:, fc, 512 * oh:512 * oh + 512],
                                      start=(fc == 0), stop=(fc == 7))
                    return l
                pw = K.pe(mmw, wait=[(SVX, tvs[1]), (SGD, v_wout), bank_gate[bi]])
            afm_gate[st % 2] = (SPE, pw)
            def resid(vec, st=st, g=g):
                vec.tensor_tensor(x2pre[:, st % 2, 0:512], ps[wb0][:, 0:512],
                                  xo_buf[:, st % 3, 0:512], ALU.add)
                return vec.tensor_tensor(x2pre[:, st % 2, 512:1024], ps[wb1][:, 0:512],
                                         xo_buf[:, st % 3, 512:1024], ALU.add)
            rv = K.vx(resid, wait=[(SPE, pw), (SSD, xo_q[(g, st)]),
                                   x2pre_gate.get(st % 2)])
            bank_gate[wb0] = (SVX, rv)
            bank_gate[wb1] = (SVX, rv)
            xo_last[st % 3] = (SVX, rv)
            xo_fetch(g, st + 3)
            return rv

        def stage_B(st, rv):
            wln = [(SVX, rv)]
            lnv = emit_ln(x2pre.ap()[:, st % 2, :], wln, [x2tm[:, st, :]],
                          b=st % 2, dummy=sq_scr[:])
            ln_done[st] = lnv
            x2pre_gate[st % 2] = (SVX, lnv)
            return lnv

        def stage_C(st):
            cb = K.vx(lambda vec, st=st: vec.tensor_copy(x2bf[:, st % 2, :],
                                                         x2tm[:, st, :]),
                      wait=[(SVX, ln_done[st]), tr_prev.get(st % 2)])
            for b2 in range(2):
                bank = ps[b2]
                def mmx(te, st=st, b2=b2, bank=bank):
                    l = None
                    for fc4 in range(4):
                        fc = 4 * b2 + fc4
                        l = te.matmul(bank[:, 128 * fc4:128 * fc4 + 128],
                                      x2bf[:, st % 2, 128 * fc:128 * fc + 128],
                                      iden[:], start=True, stop=True)
                    return l
                px = K.pe(mmx, wait=[(SVX, cb), bank_gate[b2]])
                tr_prev[st % 2] = (SPE, px)
                def evx(vec, st=st, b2=b2, bank=bank):
                    dst = bass.AP(fmbuf.ap().tensor,
                                  fmbuf.ap().offset + (4 * b2) * G + 128 * st,
                                  [[8 * G, 128], [G, 4], [1, 128]])
                    return vec.tensor_copy(dst, bank[:, 0:512])
                xv = K.vx(evx, wait=[(SPE, px)])
                bank_gate[b2] = (SVX, xv)
                x2fm_ev[(st, b2)] = xv

        def step_pre(st):
            stage_A1(st)

        def step_post(st):
            rv = stage_A2(st)
            stage_B(st, rv)

        def step(st):
            stage_A1(st)
            rv = stage_A2(st)
            stage_B(st, rv)

        def finish():
            for st in range(ST):
                stage_C(st)

        return step, step_pre, step_post, finish, ln_done, x2fm_ev

    def emit_wout_ln1(g, attn_done, v_wout, prev_x2_free):
        step, _, _, finish, ln_done, x2fm_ev = make_wout(g, attn_done, v_wout,
                                                         prev_x2_free)
        for st in range(ST):
            step(st)
        finish()
        return ln_done, x2fm_ev

    def emit_ffn(g, x2fm_ev, ln_done, filler=None):
        wprev = None
        for st in range(ST):
            def mmg(te, st=st):
                l = None
                for hc in range(8):
                    l = te.matmul(ps[4][:, 0:E], fmbuf[:, hc, 128 * st:128 * st + 128],
                                  gwT[:, hc, :], start=(hc == 0), stop=(hc == 7))
                return l
            wt = [(SVX, x2fm_ev[(st, 1)]), bank_gate[4]]
            pg = K.pe(mmg, wait=wt)
            v1 = K.vx(lambda vec: vec.tensor_tensor(gsb[:], ps[4][:, 0:E], gbt[:], ALU.add),
                      wait=[(SPE, pg)])
            bank_gate[4] = (SVX, v1)
            v2 = K.vx(lambda vec: vec.reduce_max(gnm[:], gsb[:], axis=AX, negate=True),
                      wait=[(SVX, v1)])
            c1 = K.sc(lambda s: s.activation(gsb2[:], gsb[:], ACT.Exp, bias=gnm[:],
                                             accum_out=gden[:]), wait=[(SVX, v2)])
            vgr = K.vx(lambda vec: vec.reciprocal(grd[:], gden[:]), wait=[(SSC, c1)])
            vmx = K.vx(lambda vec: vec.max(out=gmx[:], in_=gsb2[:]), wait=[(SSC, c1)])
            vsel = K.vx(lambda vec: vec.scalar_tensor_tensor(
                gind[:], gsb2[:], gmx[:, 1:2], gsb2[:], op0=ALU.is_ge, op1=ALU.mult),
                wait=[(SVX, vmx)])
            v3 = K.vx(lambda vec, st=st: vec.tensor_scalar_mul(
                w_tm[:, st, :], gind[:], grd[:]), wait=[(SVX, vsel), (SVX, vgr)])
            import os as _os7
            if _os7.environ.get("KFFN") == "1" and g == 0 and st == 0:
                def dcpg(vec):
                    vec.tensor_copy(x2pre[:, 0, 0:8], gsb[:])
                    vec.tensor_copy(x2pre[:, 0, 8:16], gsb2[:])
                    vec.tensor_copy(x2pre[:, 0, 16:24], gmx[:])
                    vec.tensor_copy(x2pre[:, 0, 24:32], gind[:])
                    return vec.tensor_copy(x2pre[:, 0, 32:40], w_tm[:, 0, :])
                dvg = K.vx(dcpg, wait=[(SVX, v3)])
                ddg = dump("gate0", x2pre.ap()[:, 0, :], [(SVX, dvg)])
                bank_gate[4] = (SOUT, ddg)
            v4 = K.vx(lambda vec, st=st: vec.tensor_copy(wbf[:], w_tm[:, st, :]),
                      wait=[(SVX, v3), (SPE, wprev) if wprev else None])
            def mmwt(te, st=st):
                return te.matmul(ps[5][0:E, 0:128], wbf[:], iden[:],
                                 start=True, stop=True)
            pwt = K.pe(mmwt, wait=[(SVX, v4), bank_gate[5]])
            wprev = pwt
            v5 = K.vx(lambda vec, st=st: vec.tensor_copy(w_fm[:, st, :], ps[5][0:E, 0:128]),
                      wait=[(SPE, pwt)])
            bank_gate[5] = (SVX, v5)
            wfm_done = v5
        for st in range(ST):
            pb = None
            for oh in range(2):
                def mmb(te, st=st, oh=oh):
                    return te.matmul(ps[2 * (st % 2) + oh][:, 0:512], w_fm[:, st, :],
                                     ebT[:, 512 * oh:512 * oh + 512],
                                     start=True, stop=True)
                pb = K.pe(mmb, wait=[(SVX, wfm_done), (SGD, v_const),
                                     bank_gate[2 * (st % 2) + oh]])
            def init3(vec, st=st):
                b0 = ps[2 * (st % 2)]
                b1 = ps[2 * (st % 2) + 1]
                vec.tensor_tensor(x3acc[:, st, 0:512], b0[:, 0:512],
                                  x2tm[:, st, 0:512], ALU.add)
                return vec.tensor_tensor(x3acc[:, st, 512:1024], b1[:, 0:512],
                                         x2tm[:, st, 512:1024], ALU.add)
            iv = K.vx(init3, wait=[(SPE, pb), (SVX, ln_done[st])])
            bank_gate[2 * (st % 2)] = (SVX, iv)
            bank_gate[2 * (st % 2) + 1] = (SVX, iv)
        ev_done = {}
        v_wer = {}
        for e in range(2):
            v_wer[e] = K.gdma(wer[e][:], rg(wet_d[e]), wait=[ring_gate[e]])
        for e in range(E):
            if filler is not None:
                filler(e, "pre")
            pm = None
            for st in range(ST):
                b0i, b1i = 2 * (st % 2), 2 * (st % 2) + 1
                def mme(te, e=e, st=st, b0i=b0i, b1i=b1i):
                    l = None
                    for hc in range(8):
                        te.matmul(ps[b0i][:, 0:512], fmbuf[:, hc, 128 * st:128 * st + 128],
                                  wer[e % 2][:, hc, 0:512],
                                  start=(hc == 0), stop=(hc == 7))
                        l = te.matmul(ps[b1i][:, 0:512], fmbuf[:, hc, 128 * st:128 * st + 128],
                                      wer[e % 2][:, hc, 512:1024],
                                      start=(hc == 0), stop=(hc == 7))
                    return l
                wt = [(SGD, v_wer[e]), bank_gate[b0i], bank_gate[b1i]]
                pm = K.pe(mme, wait=wt)
                def evac(vec, e=e, st=st, b0i=b0i, b1i=b1i):
                    vec.scalar_tensor_tensor(
                        x3acc[:, st, 0:512], ps[b0i][:, 0:512],
                        w_tm[:, st, e:e + 1], x3acc[:, st, 0:512],
                        op0=ALU.mult, op1=ALU.add)
                    return vec.scalar_tensor_tensor(
                        x3acc[:, st, 512:1024], ps[b1i][:, 0:512],
                        w_tm[:, st, e:e + 1], x3acc[:, st, 512:1024],
                        op0=ALU.mult, op1=ALU.add)
                vv = K.vx(evac, wait=[(SPE, pm)])
                bank_gate[b0i] = (SVX, vv)
                bank_gate[b1i] = (SVX, vv)
                ev_done[(e, st)] = vv
            ring_gate[e % 2] = (SPE, pm)
            if e + 2 < E:
                v_wer[e + 2] = K.gdma(wer[(e + 2) % 2][:], rg(wet_d[e + 2]),
                                      wait=[(SPE, pm)])
            if filler is not None:
                filler(e, "post")
        last_out = None
        for st in range(ST):
            wg = [(SVX, ev_done[(E - 1, st)])]
            if len(out_hist) >= 2:
                wg.append((SOUT, out_hist[-2]))
            lnv = emit_ln(x3acc[:, st, :], wg, [outb[:, st % 2, :]], b=st % 2,
                          dummy=sq_scr[:])
            ov = K.odma(out_d[G * g + 128 * st:G * g + 128 * st + 128, :],
                        outb[:, st % 2, :], wait=[(SVX, lnv)])
            out_hist.append(ov)
            last_out = ov
        return ev_done[(E - 1, ST - 1)], last_out

    xo_last = {}
    xo_rv = []
    out_hist = []
    x2pre_gate = {}
    ring_gate = {0: None, 1: None}

    r0 = emit_projections(0)
    loc0 = emit_local(0, [(SVX, r0["qvx"]), (SVX, r0["kvx"]), (SVX, r0["vvx"]),
                          (SVX, r0["vone"]), (SVX, r0["glv"])])

    import os as _os
    if _os.environ.get("KONEGROUP") == "1" and stage == "attn":
        dump("q0", q_sb.ap().rearrange("p a n -> p (a n)"), [(SVX, r0["qvx"])])
        dump("k0", k_sb.ap().rearrange("p a n -> p (a n)"), [(SVX, r0["kvx"])])
        dump("v0", v_sb.ap().rearrange("p a b n -> p (a b n)"), [(SVX, r0["vvx"]), (SVX, r0["vone"])])
        dump("gl", gl_tm.ap().rearrange("p a n -> p (a n)"), [(SVX, r0["glv"])])
        dump("attnL0", attn_g[0].ap().rearrange("p a n -> p (a n)"), [(SVX, loc0)])
        K.final_wait()
        with nc.Block() as blk:
            K.emit(blk)
        es1.close()
        es.close()
        return nc

    v_xg[1] = K.gdma(xg[:], rg(xT)[:, :, G:G + G + 256], wait=[xg_free[0]])
    v_wk[1] = K.gdma(wbufB[:], rg(wk_d), wait=[xg_free[0]])
    v_wq[1] = K.gdma(wbufA[:], rg(wq_d), wait=[(SPE, r0["wA_free_pe"])])

    r1 = emit_projections(1)
    loc1 = emit_local(1, [(SVX, r1["qvx"]), (SVX, r1["kvx"]), (SVX, r1["vvx"]),
                          (SVX, r1["glv"])])

    v_wgk = K.gdma(wbufA[:, 0:4], rg(wgk_d), wait=[(SPE, r1["wA_free_pe"])])
    pool_step(2 * NPIECE)
    v_wgv = K.gdma(wbufB[:, 0:4], rg(wgv_d), wait=[xg_free[1]])
    gkgv, wgv_pe = emit_pooled_finish(v_wgk, v_wgv)
    v_rout = K.gdma(wbufA[:, 4:8], rg(rout_d), wait=[xg_free[1]])
    v_wout = K.gdma(wbufB[:], rg(wout_d), wait=[(SPE, wgv_pe)])

    dL0 = dump("attnL0", attn_g[0].ap().rearrange("p a n -> p (a n)"), [(SVX, loc0)])
    dL1 = dump("attnL1", attn_g[1].ap().rearrange("p a n -> p (a n)"), [(SVX, loc1)])
    import os
    _lat_lim = int(os.environ.get("KLAT_LIMIT", "0")) or None
    lat0 = emit_latent(0, gkgv, (SVX, r0["gqvx"]), v_rout,
                       extra_gate=(SOUT, dL0) if dL0 else None, only_st=_lat_lim)
    lat1 = emit_latent(1, gkgv, (SVX, r1["gqvx"]), v_rout,
                       extra_gate=(SOUT, dL1) if dL1 else None,
                       only_st=_lat_lim) if _lat_lim is None else lat0
    pe_after_latents = K.s_pe.n
    phase2_ready = (SPE, pe_after_latents)

    dump("q0", q_sb.ap().rearrange("p a n -> p (a n)"), [(SVX, r1["qvx"])])
    dump("k0", k_sb.ap().rearrange("p a n -> p (a n)"), [(SVX, r1["kvx"])])
    dump("gq0", gq_g[0].ap().rearrange("p a n -> p (a n)"), [(SVX, r0["gqvx"])])
    dump("pooled", pooled_tm[:], [(SVX, gkgv)])
    dump("gk", gk_fm.ap().rearrange("p a n -> p (a n)"), [(SVX, gkgv)])
    dump("gl", gl_tm.ap().rearrange("p a n -> p (a n)"), [(SVX, r1["glv"])])
    dump("attn0", attn_g[0].ap().rearrange("p a n -> p (a n)"), [(SVX, lat0)])
    dump("attn1", attn_g[1].ap().rearrange("p a n -> p (a n)"), [(SVX, lat1)])

    if stage == "attn":
        K.final_wait()
        with nc.Block() as blk:
            K.emit(blk)
        es1.close()
        es.close()
        return nc

    es1.close()
    sb2 = lambda n, shp, dty: es.enter_context(nc.sbuf_tensor(n, shp, dty))
    x2tm = sb2("x2tm", [128, ST, D], F32)
    x3acc = sb2("x3acc", [128, ST, D], F32)
    fmbuf = sb2("fmbuf", [128, 8, G], BF16)
    x2bf = sb2("x2bf", [128, 2, D], BF16)
    afm_s = sb2("afm_s", [128, 2, 8, 128], BF16)
    sq_scr = sb2("sq_scr", [128, D], F32)
    outb = sb2("outb", [128, 2, D], F32)
    xo_buf = sb2("xo_buf", [128, 3, D], F32)
    wer = [sb2(f"wer{i}", [128, 8, D], BF16) for i in range(2)]
    ring_gate[0] = phase2_ready
    ring_gate[1] = phase2_ready
    xo_last[0] = phase2_ready
    xo_last[1] = phase2_ready

    ln0, x2ev0 = emit_wout_ln1(0, lat0, v_wout, None)
    dump("x2_0", x2tm.ap().rearrange("p a n -> p (a n)"), [(SVX, ln0[ST - 1])])
    for _st in range(3):
        xo_fetch(1, _st)
    ffn0_done, out0 = emit_ffn(0, x2ev0, ln0)
    dw0 = dump("wtm0", w_tm.ap().rearrange("p a n -> p (a n)"), [(SVX, ffn0_done)])
    dx0 = dump("x3a0", x3acc.ap().rearrange("p a n -> p (a n)"), [(SVX, ffn0_done)])
    g1gate = (SOUT, dx0) if dx0 else (SVX, ffn0_done)
    ln1, x2ev1 = emit_wout_ln1(1, lat1, v_wout, g1gate)
    ffn1_done, out1 = emit_ffn(1, x2ev1, ln1)

    K.final_wait()
    with nc.Block() as blk:
        K.emit(blk)
    es.close()
    return nc



def _host_inputs(inputs):
    x = np.ascontiguousarray(inputs["L0_attn_x"].reshape(S, D))
    xTf = np.ascontiguousarray(x.T)
    expert_wT = np.ascontiguousarray(
        np.asarray(inputs["L0_ffn_expert_w"]).transpose(0, 2, 1))
    gate_wT = np.ascontiguousarray(np.asarray(inputs["L0_ffn_gate_w"]).T)
    gate_b_t = np.ascontiguousarray(
        np.broadcast_to(np.asarray(inputs["L0_ffn_gate_b"]), (128, E)))
    glw = np.asarray(inputs["L0_attn_gate_local_w"])
    grw = np.asarray(inputs["L0_attn_gate_remote_w"])
    glbd = np.zeros((D, NH), dtype=np.float32)
    grbd = np.zeros((D, NH), dtype=np.float32)
    for h in range(NH):
        glbd[64 * h:64 * h + 64, h] = glw[h]
        grbd[64 * h:64 * h + 64, h] = grw[h]
    iden = np.eye(128, dtype=np.float32)
    tri = np.zeros((128, 384), dtype=np.float32)
    kk = np.arange(128)[:, None]
    qq = np.arange(128)[None, :]
    tri[:, 0:128] = (qq >= kk).astype(np.float32)
    tri[:, 128:256] = 1.0
    tri[:, 256:384] = (kk > qq).astype(np.float32)
    psel = np.zeros((128, 256), dtype=np.float32)
    psel[:, 128] = 1.0

    per_core = []
    for c in range(NCORES):
        lo = T * c
        xT_own = np.zeros((D, T + 256), dtype=np.float32)
        if c == 0:
            xT_own[:, 256:] = xTf[:, 0:T]
        else:
            xT_own[:] = xTf[:, lo - 256:lo + T]
        lm = np.zeros((128, 16 * NL), dtype=np.float32)
        for gst in range(16):
            cmax = 8 * c + gst // 2
            lm[:, gst * NL + cmax + 1:(gst + 1) * NL] = NEG
        halov = np.full((128, 1), 0.0 if c == 0 else 1.0, dtype=np.float32)
        per_core.append(dict(
            xT=np.ascontiguousarray(xT_own),
            x_own=np.ascontiguousarray(x[lo:lo + T]),
            x_all=x,
            wq=np.asarray(inputs["L0_attn_wq"]),
            wk=np.asarray(inputs["L0_attn_wk"]),
            wv=np.asarray(inputs["L0_attn_wv"]),
            wgq=np.asarray(inputs["L0_attn_wgq"]),
            wgk=np.asarray(inputs["L0_attn_wgk"]),
            wgv=np.asarray(inputs["L0_attn_wgv"]),
            r_out_w=np.asarray(inputs["L0_attn_r_out_w"]),
            w_out=np.asarray(inputs["L0_attn_w_out"]),
            expert_wT=expert_wT,
            gate_wT=gate_wT,
            gate_b_t=gate_b_t,
            expert_b=np.asarray(inputs["L0_ffn_expert_b"]),
            glw_bd=glbd,
            grw_bd=grbd,
            iden_i=iden,
            trimask_i=tri,
            trihalo_i=tri * (0.0 if c == 0 else 1.0),
            poolsel_i=psel,
            lmask_i=lm,
            halov_i=halov,
        ))
    return per_core


def kernel(**inputs):
    per_core = _host_inputs(inputs)
    nc = build_nc(stage="full")
    res = run_bass_kernel_spmd(nc, per_core, core_ids=list(range(NCORES)))
    out = np.concatenate([res.results[c]["out"] for c in range(NCORES)], axis=0)
    return out.reshape(1, S, D)


# version 12
# speedup vs baseline: 1.0797x; 1.0174x over previous
import contextlib
import numpy as np

import concourse.bass as bass
import concourse.mybir as mybir
from concourse.bass_utils import run_bass_kernel_spmd

F32 = mybir.dt.float32
BF16 = mybir.dt.bfloat16
AX = mybir.AxisListType.X
ALU = mybir.AluOpType
ACT = mybir.ActivationFunctionType

NCORES = 8
S = 16384
D = 1024
T = S // NCORES
G = 1024
NG = T // G
NH = 16
HD = 64
HP = NH // 2
LH = 8
LHD = 64
LD = 512
NL = 64
W = 256
ST = G // 128
CH = (G + 256) // 128
E = 8
NEG = -1.0e30


class Sem:
    def __init__(self, es, nc, name):
        self.h = es.enter_context(nc.semaphore(name))
        self.n = 0


class Kb:
    def __init__(self, nc, es):
        self.nc = nc
        self.steps = []
        self.s_gd = Sem(es, nc, "s_gd")
        self.s_pe = Sem(es, nc, "s_pe")
        self.s_sc = Sem(es, nc, "s_sc")
        self.s_vx = Sem(es, nc, "s_vx")
        self.s_sd = Sem(es, nc, "s_sd")
        self.s_out = Sem(es, nc, "s_out")
        self.pe = self._mk("tensor", self.s_pe, 1)
        self.sc = self._mk("scalar", self.s_sc, 1)
        self.vx = self._mk("vector", self.s_vx, 1)
        self.gd = self._mk("gpsimd", self.s_gd, 16)
        self.sy = self._mk("sync", self.s_sd, 16)
        self.so = self._mk("sync", self.s_out, 16)

    def _mk(self, name, sem, bump):
        def add(fn, wait=None, inc=True):
            if inc:
                sem.n += bump
            tgt = sem.n if inc else None
            wl = [wv for wv in (wait or []) if wv is not None and wv[1] is not None]
            def step(eng, fn=fn, wl=wl, tgt=tgt):
                for (s, v) in wl:
                    eng.wait_ge(s.h, v)
                last = fn(eng)
                if tgt is not None:
                    assert last is not None, f"{name} step returned None with inc"
                    last.then_inc(sem.h, bump)
            self.steps.append((name, step))
            return tgt
        return add

    def gdma(self, dst, src, wait=None):
        return self.gd(lambda g, dst=dst, src=src: g.dma_start(out=dst, in_=src), wait=wait)

    def sdma(self, dst, src, wait=None):
        return self.sy(lambda sy, dst=dst, src=src: sy.dma_start(out=dst, in_=src), wait=wait)

    def odma(self, dst, src, wait=None):
        return self.so(lambda sy, dst=dst, src=src: sy.dma_start(out=dst, in_=src), wait=wait)

    def godma(self, dst, src, wait=None):
        self.s_out.n += 16
        tgt = self.s_out.n
        wl = [wv for wv in (wait or []) if wv is not None and wv[1] is not None]
        def step(eng, dst=dst, src=src, wl=wl, tgt=tgt):
            for (s, v) in wl:
                eng.wait_ge(s.h, v)
            eng.dma_start(out=dst, in_=src).then_inc(self.s_out.h, 16)
        self.steps.append(("gpsimd", step))
        return tgt

    def final_wait(self):
        v = self.s_out.n
        self.steps.append(("sync", lambda sy, v=v: sy.wait_ge(self.s_out.h, v)))

    def emit(self, blk):
        for name in ("sync", "gpsimd", "tensor", "scalar", "vector"):
            fns = [f for (e, f) in self.steps if e == name]
            if not fns:
                continue
            def body(eng, fns=fns):
                for f in fns:
                    f(eng)
            getattr(blk, name)(body)


def ap_off(t, extra_off, pattern):
    a = t.ap() if hasattr(t, "ap") else t
    return bass.AP(a.tensor, a.offset + extra_off, pattern)


def build_nc(stage="full", debug=()):
    nc = bass.Bass("TRN2", target_bir_lowering=False)
    es = contextlib.ExitStack()

    di = lambda n, shp: nc.dram_tensor(n, shp, F32, kind="ExternalInput").ap()
    xT = di("xT", [D, T + 256])
    x_own = di("x_own", [T, D])
    x_all = di("x_all", [S, D])
    wq_d = di("wq", [D, D]); wk_d = di("wk", [D, D]); wv_d = di("wv", [D, D])
    wgq_d = di("wgq", [D, LD]); wgk_d = di("wgk", [D, LD]); wgv_d = di("wgv", [D, LD])
    rout_d = di("r_out_w", [LD, D])
    wout_d = di("w_out", [D, D])
    wet_d = di("expert_wT", [E, D, D])
    gwt_d = di("gate_wT", [D, E])
    gbt_d = di("gate_b_t", [128, E])
    eb_d = di("expert_b", [E, D])
    glbd_d = di("glw_bd", [D, NH])
    grbd_d = di("grw_bd", [D, NH])
    iden_d = di("iden_i", [128, 128])
    tri_d = di("trimask_i", [128, 384])
    trih_d = di("trihalo_i", [128, 384])
    psel_d = di("poolsel_i", [128, 256])
    lmask_d = di("lmask_i", [128, 16 * NL])
    halov_d = di("halov_i", [128, 1])
    out_d = nc.dram_tensor("out", [T, D], F32, kind="ExternalOutput").ap()
    dbg = {}
    for name, shp in debug:
        dbg[name] = nc.dram_tensor("dbg_" + name, list(shp), F32, kind="ExternalOutput").ap()

    sbt = lambda n, shp, dty: es.enter_context(nc.sbuf_tensor(n, shp, dty))
    iden = sbt("iden", [128, 128], BF16)
    gwT = sbt("gwT", [128, 8, E], BF16)
    gbt = sbt("gbt", [128, E], F32)
    ebT = sbt("ebT", [E, D], BF16)
    sc_comb = sbt("sc_comb", [128, 8], F32)
    rd4 = sbt("rd4", [128, 8], F32)
    x2pre = sbt("x2pre", [128, 2, D], F32)
    s1 = sbt("s1", [128, 2], F32); s2 = sbt("s2", [128, 2], F32)
    mvv = sbt("mvv", [128, 2], F32)
    varv = sbt("varv", [128, 2], F32)
    lnl = sbt("lnl", [128, 2], F32)
    rstd = sbt("rstd", [128, 2], F32); nmr = sbt("nmr", [128, 2], F32)
    gsb = sbt("gsb", [128, E], F32)
    gsb2 = sbt("gsb2", [128, E], F32)
    gnm = sbt("gnm", [128, 1], F32)
    gden = sbt("gden", [128, 1], F32)
    grd = sbt("grd", [128, 1], F32)
    gmx = sbt("gmx", [128, E], F32)
    gind = sbt("gind", [128, E], F32)
    w_tm = sbt("w_tm", [128, ST, E], F32)
    wbf = sbt("wbf", [128, 2, E], BF16)
    w_fm = sbt("w_fm", [8, ST, 128], BF16)
    attn_g = [sbt(f"attn{g}", [128, ST, D], BF16) for g in range(NG)]
    wbufB = sbt("wbufB", [128, 8, D], BF16)
    es1 = contextlib.ExitStack()
    sb1 = lambda n, shp, dty: es1.enter_context(nc.sbuf_tensor(n, shp, dty))
    trimask = sb1("trimask", [128, 384], BF16)
    trihalo = sb1("trihalo", [128, 384], BF16)
    poolsel = sb1("poolsel", [128, 256], BF16)
    glwbd = sb1("glwbd", [128, 8, NH], BF16)
    grwbd = sb1("grwbd", [128, 8, NH], BF16)
    lmaskb = sb1("lmaskb", [128, 16 * NL], BF16)
    halov = sb1("halov", [128, 1], F32)
    pooled_tm = sb1("pooled_tm", [NL, D], BF16)
    pooled_fm = sb1("pooled_fm", [128, 8, NL], BF16)
    gk_fm = sb1("gk_fm", [128, 4, NL], BF16)
    gv_fm = sb1("gv_fm", [128, 4, NL], BF16)
    gv_tm = sb1("gv_tm", [128, 4, NL], BF16)
    lp_sb = sb1("lp_sb", [128, 2, 8, NL], BF16)
    lpt2 = sb1("lpt2", [128, 2, 4, 128], BF16)
    combt = sb1("combt", [128, 512], BF16)
    lden = sb1("lden", [128, 2, 8], F32)
    lrd = sb1("lrd", [128, 2, 8], F32)
    gl_tm = sb1("gl_tm", [128, ST, NH], F32)
    gr_tm = [sb1(f"gr_tm{g}", [128, ST, NH], F32) for g in range(NG)]
    glfm = sb1("glfm", [16, G], BF16)
    sige = sb1("sige", [128, 128], F32)
    pbt = sb1("pbt", [128, 6, 2, 384], BF16)
    rem_fm = sb1("rem_fm", [128, 2, 4, 128], BF16)
    wbufA = sb1("wbufA", [128, 8, D], BF16)
    gq_g = [sb1(f"gq{g}", [128, 4, G], BF16) for g in range(NG)]
    q_sb = sb1("q_sb", [128, HP, G], BF16)
    k_sb = sb1("k_sb", [128, HP, G + 256], BF16)
    v_sb = sb1("v_sb", [128, CH, NH, 65], BF16)
    xg = sb1("xg", [128, 8, G + 256], BF16)
    poolb = sb1("poolb", [128, 4, D], BF16)
    x2tm = x2fm = x3acc = fmbuf = x2bf = sq_scr = outb = xo_buf = None
    wer = None

    ps = [es.enter_context(nc.psum_tensor(f"ps{i}", [128, 512], F32)) for i in range(6)]
    ps_pool = es.enter_context(nc.psum_tensor("ps_pool", [128, 1024], F32))

    K = Kb(nc, es)
    SGD, SPE, SSC, SVX, SSD, SOUT = K.s_gd, K.s_pe, K.s_sc, K.s_vx, K.s_sd, K.s_out
    bank_gate = [None] * 6
    pspool_gate = [None, None]

    def dump(name, src_ap, wait):
        if name in dbg:
            return K.godma(dbg[name], src_ap, wait=wait)
        return None

    rg = lambda d: d.rearrange("(a p) n -> p a n", p=128)
    K.gdma(iden[:], iden_d)
    K.gdma(trimask[:], tri_d)
    K.gdma(trihalo[:], trih_d)
    K.gdma(poolsel[:], psel_d)
    K.gdma(glwbd[:], rg(glbd_d))
    K.gdma(grwbd[:], rg(grbd_d))
    K.gdma(gwT[:], rg(gwt_d))
    K.gdma(gbt[:], gbt_d)
    K.gdma(ebT[:], eb_d)
    K.gdma(lmaskb[:], lmask_d)
    v_const = K.gdma(halov[:], halov_d)

    v_xg = [None, None]
    v_wq = [None, None]; v_wk = [None, None]; v_wv = [None, None]; v_wgq = [None, None]
    v_xg[0] = K.gdma(xg[:], rg(xT)[:, :, 0:G + 256])
    v_wq[0] = K.gdma(wbufA[:], rg(wq_d))
    v_wk[0] = K.gdma(wbufB[:], rg(wk_d))

    NPIECE = S // 128
    pool_dma_v = []
    pool_pe_v = []

    def _pool_dma1():
        i = len(pool_dma_v)
        w = [(SPE, pool_pe_v[i - 4])] if i >= 4 else None
        v = K.gdma(poolb[:, i % 4], x_all[128 * i:128 * i + 128, :], wait=w)
        pool_dma_v.append(v)

    def pool_step(n):
        for _ in range(n):
            i, j = len(pool_dma_v), len(pool_pe_v)
            if i >= NPIECE and j >= NPIECE:
                return
            if i < NPIECE and i <= j + 3:
                _pool_dma1()
            else:
                _pool_mm1()

    def _pool_mm1():
            i = len(pool_pe_v)
            if i >= NPIECE or i >= len(pool_dma_v):
                return
            def mm(te, i=i):
                last = None
                p = i // 2
                for hf in range(2):
                    last = te.matmul(ps_pool[:, 512 * hf:512 * hf + 512],
                                     poolsel[:, 128 - p:256 - p],
                                     poolb[:, i % 4, 512 * hf:512 * hf + 512],
                                     start=(i == 0),
                                     stop=(i == NPIECE - 1 and hf == 1))
                return last
            v = K.pe(mm, wait=[(SGD, pool_dma_v[i])])
            pool_pe_v.append(v)

    def proj_fm(dst_fn, wsb, w_v, xg_lo, xg_n, nchunk, v_xg_cur, wfree=1024):
        nblk = (xg_n + 511) // 512
        last_pe = last_vx = None
        wflat = wsb.ap().rearrange("p a n -> p (a n)")
        for i in range(nchunk):
            for nb in range(nblk):
                n0 = 512 * nb
                nn = min(xg_n, n0 + 512) - n0
                bi = (i * nblk + nb) % 2
                bank = ps[bi]
                def mm(te, i=i, n0=n0, nn=nn, bank=bank, wflat=wflat, wfree=wfree):
                    last = None
                    for hc in range(8):
                        last = te.matmul(bank[:, 0:nn],
                                         wflat[:, wfree * hc + 128 * i:wfree * hc + 128 * i + 128],
                                         xg[:, hc, xg_lo + n0:xg_lo + n0 + nn],
                                         start=(hc == 0), stop=(hc == 7))
                    return last
                last_pe = K.pe(mm, wait=[(SGD, max(w_v, v_xg_cur)), bank_gate[bi]])
                def ev(vec, i=i, n0=n0, nn=nn, bank=bank, dst_fn=dst_fn):
                    return vec.tensor_copy(dst_fn(i, n0, nn), bank[:, 0:nn])
                last_vx = K.vx(ev, wait=[(SPE, last_pe)])
                bank_gate[bi] = (SVX, last_vx)
        return last_pe, last_vx

    def emit_glgr(g, qvx_gate):
        last = None
        for (wsb_, dst) in ((glwbd, gl_tm), (grwbd, gr_tm[g])):
            ev = None
            for nb in range(2):
                bank = ps[4]
                def mm(te, wsb_=wsb_, nb=nb, bank=bank):
                    l = None
                    for hpp in range(HP):
                        l = te.matmul(bank[0:16, 0:512], wsb_[:, hpp, :],
                                      q_sb[:, hpp, 512 * nb:512 * nb + 512],
                                      start=(hpp == 0), stop=(hpp == HP - 1))
                    return l
                pv = K.pe(mm, wait=[(SVX, qvx_gate), (SGD, v_const), bank_gate[4]])
                ev = K.vx(lambda vec, nb=nb, bank=bank: vec.tensor_copy(
                    glfm[:, 512 * nb:512 * nb + 512], bank[0:16, 0:512]),
                    wait=[(SPE, pv)])
                bank_gate[4] = (SVX, ev)
            bank2 = ps[5]
            def mmt(te, bank2=bank2):
                l = None
                for st in range(ST):
                    l = te.matmul(bank2[:, 16 * st:16 * st + 16],
                                  glfm[:, 128 * st:128 * st + 128],
                                  iden[0:16, 0:16], start=True, stop=True)
                return l
            pv2 = K.pe(mmt, wait=[(SVX, ev), bank_gate[5]])
            c1 = K.sc(lambda s, bank2=bank2: s.activation(
                sige[:], bank2[:, 0:128], ACT.Exp, scale=-1.0),
                wait=[(SPE, pv2)])
            bank_gate[5] = (SSC, c1)
            sa = K.vx(lambda vec: vec.tensor_scalar_add(sige[:], sige[:], 1.0),
                      wait=[(SSC, c1)])
            last = K.vx(lambda vec, dst=dst: vec.reciprocal(
                dst.ap().rearrange("p a n -> p (a n)"), sige[:]), wait=[(SVX, sa)])
        return last

    xg_free = [None, None]

    def emit_projections(g):
        res = {}
        vxc = v_xg[g]
        qpe, qvx = proj_fm(lambda i, n0, nn: q_sb.ap()[:, i, n0:n0 + nn],
                           wbufA, v_wq[g], 256, G, HP, vxc)
        pool_step(8 if g == 0 else 20)
        glv = emit_glgr(g, qvx)
        v_wv[g] = K.gdma(wbufA[:], rg(wv_d), wait=[(SPE, qpe)])
        kpe, kvx = proj_fm(lambda i, n0, nn: k_sb.ap()[:, i, n0:n0 + nn],
                           wbufB, v_wk[g], 0, G + 256, HP, vxc)
        pool_step(8 if g == 0 else 20)
        v_wgq[g] = K.gdma(wbufB[:, 0:4], rg(wgq_d), wait=[(SPE, kpe)])
        vvx = vpe = None
        for sc_ in range(CH):
            for oh in range(2):
                bi = (sc_ * 2 + oh) % 2
                bank = ps[bi]
                def mm(te, sc_=sc_, oh=oh, bank=bank):
                    last = None
                    for hc in range(8):
                        last = te.matmul(bank[:, 0:512],
                                         xg[:, hc, 128 * sc_:128 * sc_ + 128],
                                         wbufA[:, hc, 512 * oh:512 * oh + 512],
                                         start=(hc == 0), stop=(hc == 7))
                    return last
                vpe = K.pe(mm, wait=[(SGD, max(v_wv[g], vxc)), bank_gate[bi]])
                def ev(vec, sc_=sc_, oh=oh, bank=bank):
                    dst = v_sb.ap()[:, sc_, 8 * oh:8 * oh + 8, 0:64]
                    src = bass.AP(bank.ap().tensor, bank.ap().offset,
                                  [[512, 128], [64, 8], [1, 64]])
                    return vec.tensor_copy(dst, src)
                vvx = K.vx(ev, wait=[(SPE, vpe)])
                bank_gate[bi] = (SVX, vvx)
        vone = K.vx(lambda vec: vec.memset(v_sb.ap()[:, :, :, 64:65], 1.0)) if g == 0 else None
        pool_step(8 if g == 0 else 20)
        res["wA_free_pe"] = vpe
        gqpe, gqvx = proj_fm(lambda i, n0, nn: gq_g[g].ap()[:, i, n0:n0 + nn],
                             wbufB, v_wgq[g], 256, G, 4, vxc, wfree=512)
        pool_step(12 if g == 0 else 32)
        xg_free[g] = (SPE, gqpe)
        res.update(qvx=qvx, kvx=kvx, vvx=vvx, vone=vone, glv=glv,
                   gqvx=gqvx, gqpe=gqpe, qpe=qpe)
        return res

    def emit_local(g, pre_gates):
        attn = attn_g[g]
        first = True
        slot_pv = {}
        units = []
        for hpp in range(HP):
            for half in range(2):
                tlo_h, thi_h = 4 * half, 4 * half + 4
                for ca in range(tlo_h, thi_h + 2):
                    tlo = max(tlo_h, ca - 2)
                    thi = min(thi_h, ca + 1)
                    if tlo < thi:
                        units.append((hpp, half, ca, tlo, thi))
        uidx = {(u[0], u[1], u[2]): i for i, u in enumerate(units)}
        pvjobs = []
        for hpp in range(HP):
            for half in range(2):
                for t in range(4 * half, 4 * half + 4):
                    pvjobs.append((hpp, half, t, uidx[(hpp, half, t + 2)]))
        pvjobs.sort(key=lambda j: j[3])
        LAG = 4
        mask_val = {}
        span_of = {}
        slot_of = {}
        pv_emitted = 0
        finv = None
        fin_gate = None
        last_pv_of_half = {}

        def emit_pv(j):
            nonlocal finv, fin_gate
            (hpp, half, t, _) = pvjobs[j]
            parts = tuple((t + jj, slot_of[(hpp, half, t + jj)],
                           t - span_of[(hpp, half, t + jj)][0]) for jj in range(3))
            def mmpv(te, t=t, hpp=hpp, half=half, parts=parts):
                last = None
                for jj, (cj, slj, blk) in enumerate(parts):
                    for par in range(2):
                        head = 2 * hpp + par
                        acc = ps[par][:, 66 * (t - 4 * half):66 * (t - 4 * half) + 65]
                        last = te.matmul(acc, pbt[:, slj, par, 128 * blk:128 * blk + 128],
                                         v_sb[:, cj, head, :],
                                         start=(jj == 0), stop=(jj == 2))
                return last
            wpv = [(SVX, mask_val[(hpp, half, t + 2)])]
            if t == 4 * half:
                wpv += [bank_gate[0], bank_gate[1]]
            pvv = K.pe(mmpv, wait=wpv)
            for jj in range(3):
                slot_pv[slot_of[(hpp, half, t + jj)]] = (SPE, pvv)
            if t == 4 * half + 3:
                def fin_a(vec):
                    vec.reciprocal(rd4[:, 0:4], ap_off(ps[0], 64, [[512, 128], [66, 4]]))
                    return vec.reciprocal(rd4[:, 4:8], ap_off(ps[1], 64, [[512, 128], [66, 4]]))
                fa = K.vx(fin_a, wait=[(SPE, pvv)])
                def fin_b(vec, half=half, hpp=hpp):
                    r = None
                    for par in range(2):
                        head = 2 * hpp + par
                        glc = ap_off(gl_tm, (4 * half) * NH + head, [[ST * NH, 128], [NH, 4]])
                        r = vec.tensor_tensor(sc_comb[:, 4 * par:4 * par + 4],
                                              rd4[:, 4 * par:4 * par + 4], glc, ALU.mult)
                    return r
                fb = K.vx(fin_b, wait=[(SVX, fa)])
                def fin(vec, half=half, hpp=hpp, attn=attn):
                    r = None
                    for par in range(2):
                        head = 2 * hpp + par
                        for t4 in range(4):
                            tt = 4 * half + t4
                            r = vec.tensor_scalar_mul(
                                attn[:, tt, 64 * head:64 * head + 64],
                                ps[par][:, 66 * t4:66 * t4 + 64],
                                sc_comb[:, 4 * par + t4:4 * par + t4 + 1])
                    return r
                finv = K.vx(fin, wait=[(SVX, fb)])
                bank_gate[0] = (SVX, finv)
                bank_gate[1] = (SVX, finv)

        for i, (hpp, half, ca, tlo, thi) in enumerate(units):
            while pv_emitted < len(pvjobs) and pvjobs[pv_emitted][3] <= i - LAG:
                emit_pv(pv_emitted)
                pv_emitted += 1
            nn = 128 * (thi - tlo)
            span_of[(hpp, half, ca)] = (tlo, thi)
            sl = i % 6
            slot_of[(hpp, half, ca)] = sl
            be, bo = 2 + (i % 2) * 2, 3 + (i % 2) * 2
            def mmqk(te, hpp=hpp, ca=ca, tlo=tlo, nn=nn, be=be, bo=bo):
                te.matmul(ps[be][:, 0:nn],
                          k_sb[0:64, hpp, 128 * ca:128 * ca + 128],
                          q_sb[0:64, hpp, 128 * tlo:128 * tlo + nn],
                          start=True, stop=True)
                return te.matmul(ps[bo][:, 0:nn],
                                 k_sb[64:128, hpp, 128 * ca:128 * ca + 128],
                                 q_sb[64:128, hpp, 128 * tlo:128 * tlo + nn],
                                 start=True, stop=True, tile_position=(64, 0))
            wt = list(pre_gates) if first else []
            first = False
            wt += [bank_gate[be], bank_gate[bo]]
            qk = K.pe(mmqk, wait=wt)
            wexp = [(SPE, qk), slot_pv.get(sl)]
            def ex(sc_, sl=sl, nn=nn, be=be, bo=bo):
                sc_.activation(pbt[:, sl, 0, 0:nn], ps[be][:, 0:nn], ACT.Exp, scale=0.125)
                return sc_.activation(pbt[:, sl, 1, 0:nn], ps[bo][:, 0:nn],
                                      ACT.Exp, scale=0.125)
            exv = K.sc(ex, wait=wexp)
            bank_gate[be] = (SSC, exv)
            bank_gate[bo] = (SSC, exv)
            rlo = tlo + 2 - ca
            msk = trihalo if (g == 0 and ca < 2) else trimask
            def mk(vec, sl=sl, nn=nn, rlo=rlo, msk=msk):
                vec.tensor_tensor(pbt[:, sl, 0, 0:nn], pbt[:, sl, 0, 0:nn],
                                  msk[:, 128 * rlo:128 * rlo + nn], ALU.mult)
                return vec.tensor_tensor(pbt[:, sl, 1, 0:nn], pbt[:, sl, 1, 0:nn],
                                         msk[:, 128 * rlo:128 * rlo + nn], ALU.mult)
            mkv = K.vx(mk, wait=[(SSC, exv)])
            mask_val[(hpp, half, ca)] = mkv
            if i % 6 == 0:
                pool_step(3)
        while pv_emitted < len(pvjobs):
            emit_pv(pv_emitted)
            pv_emitted += 1
        return finv

    def emit_pooled_finish(v_wgk, v_wgv):
        pe_last = pool_pe_v[-1]
        c1 = K.sc(lambda s: s.activation(pooled_tm[:], ps_pool[0:64, :], ACT.Copy,
                                         scale=1.0 / 256), wait=[(SPE, pe_last)])
        pspool_gate[0] = (SSC, c1)
        pspool_gate[1] = (SSC, c1)
        def mmt(te):
            l = None
            for hc in range(8):
                l = te.matmul(ps[0][:, 64 * hc:64 * hc + 64],
                              pooled_tm[:, 128 * hc:128 * hc + 128],
                              iden[0:64, 0:64], start=True, stop=True)
            return l
        p1 = K.pe(mmt, wait=[(SSC, c1), bank_gate[0]])
        vfm = K.vx(lambda vec: vec.tensor_copy(
            pooled_fm.ap().rearrange("p a n -> p (a n)"), ps[0][:, 0:512]),
            wait=[(SPE, p1)])
        bank_gate[0] = (SVX, vfm)
        evs = []
        pvl = None
        for (wv_, wsb_, dst, bi) in ((v_wgk, wbufA, gk_fm, 1), (v_wgv, wbufB, gv_fm, 2)):
            wflat = wsb_.ap().rearrange("p a n -> p (a n)")
            def mm(te, wflat=wflat, bi=bi):
                l = None
                for lc in range(4):
                    for hc in range(8):
                        l = te.matmul(ps[bi][:, 64 * lc:64 * lc + 64],
                                      wflat[:, 512 * hc + 128 * lc:512 * hc + 128 * lc + 128],
                                      pooled_fm[:, hc, :],
                                      start=(hc == 0), stop=(hc == 7))
                return l
            pvl = K.pe(mm, wait=[(SGD, wv_), (SVX, vfm), bank_gate[bi]])
            ev = K.vx(lambda vec, dst=dst, bi=bi: vec.tensor_copy(
                dst.ap().rearrange("p a n -> p (a n)"), ps[bi][:, 0:256]),
                wait=[(SPE, pvl)])
            bank_gate[bi] = (SVX, ev)
            evs.append(ev)
        tvs = None
        for lc in range(4):
            def mmtv(te, lc=lc):
                te.matmul(ps[3][0:64, 0:64], gv_fm[0:64, lc, :], iden[0:64, 0:64],
                          start=True, stop=True)
                return te.matmul(ps[4][64:128, 0:64], gv_fm[64:128, lc, :],
                                 iden[64:128, 64:128], start=True, stop=True,
                                 tile_position=(64, 64))
            wtv = [(SVX, evs[-1]), bank_gate[3], bank_gate[4]]
            ptv = K.pe(mmtv, wait=wtv)
            def evtv(vec, lc=lc):
                vec.tensor_copy(gv_tm[0:64, lc, :], ps[3][0:64, 0:64])
                return vec.tensor_copy(gv_tm[64:128, lc, :], ps[4][64:128, 0:64])
            tvs = K.vx(evtv, wait=[(SPE, ptv)])
            bank_gate[3] = (SVX, tvs)
            bank_gate[4] = (SVX, tvs)
        return tvs, pvl

    def emit_latent(g, gkgv_v, gq_gate, v_rout, extra_gate=None, only_st=None):
        attn = attn_g[g]
        comb_gate = [None]
        lp_gate = {}
        lpt_gate = {}
        rem_gate = {}
        st_state = {}
        nst = ST if only_st is None else only_st

        def s1_lsc(st):
            b = st % 2
            gst0 = ST * g + st
            def mml(te, st=st, g=g, gst0=gst0):
                last = None
                onesrow = trimask[0:1, 128:256]
                lrow = lmaskb.ap()[0:1, NL * gst0:NL * gst0 + NL]
                for lp_ in range(4):
                    te.matmul(ps[2][:, 64 * lp_:64 * lp_ + 64],
                              gq_g[g][0:64, lp_, 128 * st:128 * st + 128],
                              gk_fm[0:64, lp_, :], start=True, stop=False)
                    te.matmul(ps[3][:, 64 * lp_:64 * lp_ + 64],
                              gq_g[g][64:128, lp_, 128 * st:128 * st + 128],
                              gk_fm[64:128, lp_, :], start=True, stop=False,
                              tile_position=(64, 0))
                    te.matmul(ps[2][:, 64 * lp_:64 * lp_ + 64], onesrow, lrow,
                              start=False, stop=True)
                    last = te.matmul(ps[3][:, 64 * lp_:64 * lp_ + 64], onesrow, lrow,
                                     start=False, stop=True)
                return last
            wt = [bank_gate[2], bank_gate[3]]
            if st == 0 and g == 0:
                wt += [(SVX, gkgv_v)]
            if st == 0:
                wt += [gq_gate]
                if extra_gate is not None:
                    wt.append(extra_gate)
            return K.pe(mml, wait=wt)

        def s2_softmax(st, p1):
            b = st % 2
            wex = [(SPE, p1), lp_gate.get(b)]
            def ex(s, b=b):
                r = None
                for lp_ in range(4):
                    s.activation(lp_sb[:, b, 2 * lp_, :], ps[2][:, 64 * lp_:64 * lp_ + 64],
                                 ACT.Exp, scale=0.125,
                                 accum_out=lden[:, b, 2 * lp_:2 * lp_ + 1])
                    r = s.activation(lp_sb[:, b, 2 * lp_ + 1, :], ps[3][:, 64 * lp_:64 * lp_ + 64],
                                     ACT.Exp, scale=0.125,
                                     accum_out=lden[:, b, 2 * lp_ + 1:2 * lp_ + 2])
                return r
            c1 = K.sc(ex, wait=wex)
            bank_gate[2] = (SSC, c1)
            bank_gate[3] = (SSC, c1)
            vr = K.vx(lambda vec, b=b: vec.reciprocal(lrd[:, b, :], lden[:, b, :]),
                      wait=[(SSC, c1)])
            def nrm(vec, b=b):
                r = None
                for lh in range(8):
                    r = vec.tensor_scalar_mul(lp_sb[:, b, lh, :], lp_sb[:, b, lh, :],
                                              lrd[:, b, lh:lh + 1])
                return r
            return K.vx(nrm, wait=[(SVX, vr)])

        def s3_transpose(st, v2):
            b = st % 2
            def mmt(te, b=b):
                last = None
                for lp_ in range(4):
                    te.matmul(ps[4][0:64, 128 * lp_:128 * lp_ + 128],
                              lp_sb[:, b, 2 * lp_, :], iden[:], start=True, stop=True)
                    last = te.matmul(ps[5][64:128, 128 * lp_:128 * lp_ + 128],
                                     lp_sb[:, b, 2 * lp_ + 1, :], iden[:],
                                     start=True, stop=True, tile_position=(0, 64))
                return last
            p2 = K.pe(mmt, wait=[(SVX, v2), bank_gate[4], bank_gate[5]])
            lp_gate[b] = (SPE, p2)
            def evt(sc, b=b):
                sc.activation(lpt2.ap()[0:64, b].rearrange("p a n -> p (a n)"),
                              ps[4][0:64, 0:512], ACT.Copy)
                return sc.activation(lpt2.ap()[64:128, b].rearrange("p a n -> p (a n)"),
                                     ps[5][64:128, 0:512], ACT.Copy)
            v3 = K.sc(evt, wait=[(SPE, p2), lpt_gate.get(b)])
            bank_gate[4] = (SSC, v3)
            bank_gate[5] = (SSC, v3)
            return v3

        def s5_pv(st, v3):
            b = st % 2
            def mmp(te, b=b):
                last = None
                for lp_ in range(4):
                    te.matmul(ps[0][0:64, 128 * lp_:128 * lp_ + 128],
                              gv_tm[0:64, lp_, :], lpt2[0:64, b, lp_, :],
                              start=True, stop=True)
                    last = te.matmul(ps[1][64:128, 128 * lp_:128 * lp_ + 128],
                                     gv_tm[64:128, lp_, :], lpt2[64:128, b, lp_, :],
                                     start=True, stop=True, tile_position=(64, 64))
                return last
            p3 = K.pe(mmp, wait=[(SSC, v3), bank_gate[0], bank_gate[1]])
            lpt_gate[b] = (SPE, p3)
            def evp(sc, b=b):
                sc.activation(rem_fm.ap()[0:64, b].rearrange("p a n -> p (a n)"),
                              ps[0][0:64, 0:512], ACT.Copy)
                return sc.activation(rem_fm.ap()[64:128, b].rearrange("p a n -> p (a n)"),
                                     ps[1][64:128, 0:512], ACT.Copy)
            v4 = K.sc(evp, wait=[(SPE, p3), rem_gate.get(b)])
            bank_gate[0] = (SSC, v4)
            bank_gate[1] = (SSC, v4)
            return v4

        def s7_rout(st, v4):
            b = st % 2
            prs = []
            for oh in range(2):
                def mmr(te, oh=oh, b=b):
                    l = None
                    for lc in range(4):
                        l = te.matmul(ps[oh][:, 0:512], rem_fm[:, b, lc, :],
                                      wbufA[:, 4 + lc, 512 * oh:512 * oh + 512],
                                      start=(lc == 0), stop=(lc == 3))
                    return l
                wt = [(SSC, v4), bank_gate[oh]]
                if st == 0 and g == 0:
                    wt.append((SGD, v_rout))
                pr = K.pe(mmr, wait=wt)
                prs.append(pr)
            rem_gate[b] = (SPE, prs[-1])
            last = None
            for oh in range(2):
                def comb_a(vec, oh=oh, st=st, g=g):
                    grx = gr_tm[g].ap()[:, st, 8 * oh:8 * oh + 8].unsqueeze(2)
                    grx = grx.broadcast_to([128, 8, 64])
                    src = ps[oh][:, 0:512].rearrange("p (a n) -> p a n", n=64)
                    return vec.tensor_tensor(
                        combt.ap().rearrange("p (a n) -> p a n", n=64),
                        src, grx, ALU.mult)
                ca_ = K.vx(comb_a, wait=[(SPE, prs[-1]), comb_gate[0]])
                bank_gate[oh] = (SVX, ca_)
                def comb_b(vec, oh=oh, st=st, attn=attn):
                    return vec.tensor_tensor(
                        attn[:, st, 512 * oh:512 * oh + 512],
                        attn[:, st, 512 * oh:512 * oh + 512],
                        combt[:], ALU.add)
                cv = K.vx(comb_b, wait=[(SVX, ca_)])
                comb_gate[0] = (SVX, cv)
                last = cv
            return last

        last = None
        for st in range(nst + 1):
            if st >= 1:
                stp = st - 1
                v3 = s3_transpose(stp, st_state[stp])
                v4 = s5_pv(stp, v3)
            if st < nst:
                p1 = s1_lsc(st)
                v2 = s2_softmax(st, p1)
                st_state[st] = v2
            if st >= 1:
                last = s7_rout(st - 1, v4)
        return last

    ln_prev = {}

    def emit_ln(src_ap, wait_gates, dsts, b=0, dummy=None):
        sq = dummy if dummy is not None else sq_scr.ap()[:, b, :]
        wg = list(wait_gates)
        if b in ln_prev:
            wg.append(ln_prev[b])
        def sums(s):
            s.activation(sq, src_ap, ACT.Copy, accum_out=s1[:, b:b + 1])
            return s.activation(sq, src_ap, ACT.Square, accum_out=s2[:, b:b + 1])
        c1 = K.sc(sums, wait=wg)
        v1 = c1
        def stats_a(vec):
            vec.tensor_scalar_mul(mvv[:, b:b + 1], s1[:, b:b + 1], -1.0 / D)
            return vec.tensor_scalar_mul(varv[:, b:b + 1], s2[:, b:b + 1], 1.0 / D)
        v2a = K.vx(stats_a, wait=[(SSC, c1)])
        v2b = K.vx(lambda vec: vec.scalar_tensor_tensor(
            lnl[:, b:b + 1], mvv[:, b:b + 1], mvv[:, b:b + 1], varv[:, b:b + 1],
            op0=ALU.mult, op1=ALU.subtract), wait=[(SVX, v2a)])
        v2 = K.vx(lambda vec: vec.tensor_scalar(
            varv[:, b:b + 1], lnl[:, b:b + 1], -1.0, 1e-5,
            op0=ALU.mult, op1=ALU.add), wait=[(SVX, v2b)])
        c2 = K.sc(lambda s: s.activation(lnl[:, b:b + 1], varv[:, b:b + 1], ACT.Ln),
                  wait=[(SVX, v2)])
        c3 = K.sc(lambda s: s.activation(rstd[:, b:b + 1], lnl[:, b:b + 1], ACT.Exp,
                                         scale=-0.5), wait=[(SSC, c2)])
        v3 = K.vx(lambda vec: vec.tensor_tensor(nmr[:, b:b + 1], mvv[:, b:b + 1],
                                                rstd[:, b:b + 1], ALU.mult),
                  wait=[(SSC, c3)])
        def norm(vec):
            r = None
            for d_ in dsts:
                r = vec.tensor_scalar(d_, src_ap, rstd[:, b:b + 1], nmr[:, b:b + 1],
                                      op0=ALU.mult, op1=ALU.add)
            return r
        out = K.vx(norm, wait=[(SVX, v3)])
        ln_prev[b] = (SVX, out)
        return out

    xo_q = {}

    def xo_fetch(g, st):
        if st >= ST:
            return
        w = [xo_last.get(st % 3)]
        xo_q[(g, st)] = K.sdma(xo_buf[:, st % 3],
                               x_own[G * g + 128 * st:G * g + 128 * st + 128, :], wait=w)

    def make_wout(g, attn_done, v_wout, prev_x2_free, wout_banks=(2, 3)):
        attn = attn_g[g]
        ln_done = {}
        x2fm_ev = {}
        tr_prev = {}
        afm_gate = {}
        wb0, wb1 = wout_banks
        for st in range(3):
            if (g, st) not in xo_q:
                xo_fetch(g, st)

        tvs_of = {}

        def stage_A1(st):
            tvs = []
            for b2 in range(2):
                bank = ps[4 + b2]
                def mmt(te, st=st, b2=b2, bank=bank, attn=attn):
                    l = None
                    for fc4 in range(4):
                        fc = 4 * b2 + fc4
                        l = te.matmul(bank[:, 128 * fc4:128 * fc4 + 128],
                                      attn[:, st, 128 * fc:128 * fc + 128],
                                      iden[:], start=True, stop=True)
                    return l
                wt = [bank_gate[4 + b2], afm_gate.get(st % 2)]
                if st == 0:
                    wt.append((SVX, attn_done))
                    wt.append(prev_x2_free)
                ptv = K.pe(mmt, wait=wt)
                def evt(vec, st=st, b2=b2, bank=bank):
                    return vec.tensor_copy(afm_s.ap()[:, st % 2, 4 * b2:4 * b2 + 4, :],
                                           bank[:, 0:512])
                tv = K.vx(evt, wait=[(SPE, ptv)])
                bank_gate[4 + b2] = (SVX, tv)
                tvs.append(tv)
            tvs_of[st] = tvs

        def stage_A2(st):
            tvs = tvs_of[st]
            pw = None
            for oh in range(2):
                bi = wb0 if oh == 0 else wb1
                def mmw(te, st=st, oh=oh, bi=bi):
                    l = None
                    for fc in range(8):
                        l = te.matmul(ps[bi][:, 0:512],
                                      afm_s[:, st % 2, fc, :],
                                      wbufB[:, fc, 512 * oh:512 * oh + 512],
                                      start=(fc == 0), stop=(fc == 7))
                    return l
                pw = K.pe(mmw, wait=[(SVX, tvs[1]), (SGD, v_wout), bank_gate[bi]])
            afm_gate[st % 2] = (SPE, pw)
            def resid(vec, st=st, g=g):
                vec.tensor_tensor(x2pre[:, st % 2, 0:512], ps[wb0][:, 0:512],
                                  xo_buf[:, st % 3, 0:512], ALU.add)
                return vec.tensor_tensor(x2pre[:, st % 2, 512:1024], ps[wb1][:, 0:512],
                                         xo_buf[:, st % 3, 512:1024], ALU.add)
            rv = K.vx(resid, wait=[(SPE, pw), (SSD, xo_q[(g, st)]),
                                   x2pre_gate.get(st % 2)])
            bank_gate[wb0] = (SVX, rv)
            bank_gate[wb1] = (SVX, rv)
            xo_last[st % 3] = (SVX, rv)
            xo_fetch(g, st + 3)
            return rv

        def stage_B(st, rv):
            wln = [(SVX, rv)]
            lnv = emit_ln(x2pre.ap()[:, st % 2, :], wln, [x2tm[:, st, :]],
                          b=st % 2, dummy=sq_scr[:])
            ln_done[st] = lnv
            x2pre_gate[st % 2] = (SVX, lnv)
            return lnv

        def stage_C(st):
            cb = K.vx(lambda vec, st=st: vec.tensor_copy(x2bf[:, st % 2, :],
                                                         x2tm[:, st, :]),
                      wait=[(SVX, ln_done[st]), tr_prev.get(st % 2)])
            for b2 in range(2):
                bank = ps[b2]
                def mmx(te, st=st, b2=b2, bank=bank):
                    l = None
                    for fc4 in range(4):
                        fc = 4 * b2 + fc4
                        l = te.matmul(bank[:, 128 * fc4:128 * fc4 + 128],
                                      x2bf[:, st % 2, 128 * fc:128 * fc + 128],
                                      iden[:], start=True, stop=True)
                    return l
                px = K.pe(mmx, wait=[(SVX, cb), bank_gate[b2]])
                tr_prev[st % 2] = (SPE, px)
                def evx(vec, st=st, b2=b2, bank=bank):
                    dst = bass.AP(fmbuf.ap().tensor,
                                  fmbuf.ap().offset + (4 * b2) * G + 128 * st,
                                  [[8 * G, 128], [G, 4], [1, 128]])
                    return vec.tensor_copy(dst, bank[:, 0:512])
                xv = K.vx(evx, wait=[(SPE, px)])
                bank_gate[b2] = (SVX, xv)
                x2fm_ev[(st, b2)] = xv

        def step_pre(st):
            stage_A1(st)

        def step_post(st):
            rv = stage_A2(st)
            stage_B(st, rv)

        def step(st):
            stage_A1(st)
            rv = stage_A2(st)
            stage_B(st, rv)

        def finish():
            for st in range(ST):
                stage_C(st)

        return step, step_pre, step_post, finish, ln_done, x2fm_ev

    def emit_wout_ln1(g, attn_done, v_wout, prev_x2_free):
        step, _, _, finish, ln_done, x2fm_ev = make_wout(g, attn_done, v_wout,
                                                         prev_x2_free)
        for st in range(ST):
            step(st)
        finish()
        return ln_done, x2fm_ev

    def emit_ffn(g, x2fm_ev, ln_done, filler=None):
        cprev = None
        wtprev = [None, None]
        v4_of = {}
        v5_of = {}

        def emit_wt(st):
            b = st % 2
            def mmwt(te, st=st, b=b):
                return te.matmul(ps_pool[0:E, 512 * b:512 * b + 128], wbf[:, b, :],
                                 iden[:], start=True, stop=True)
            pwt = K.pe(mmwt, wait=[(SVX, v4_of[st]), pspool_gate[b]])
            wtprev[b] = (SPE, pwt)
            v5 = K.vx(lambda vec, st=st, b=b: vec.tensor_copy(
                w_fm[:, st, :], ps_pool[0:E, 512 * b:512 * b + 128]),
                wait=[(SPE, pwt)])
            pspool_gate[b] = (SVX, v5)
            v5_of[st] = v5

        for st in range(ST):
            b = st % 2
            def mmg(te, st=st, b=b):
                l = None
                for hc in range(8):
                    l = te.matmul(ps[4 + b][:, 0:E], fmbuf[:, hc, 128 * st:128 * st + 128],
                                  gwT[:, hc, :], start=(hc == 0), stop=(hc == 7))
                return l
            wt = [(SVX, x2fm_ev[(st, 1)]), bank_gate[4 + b]]
            pg = K.pe(mmg, wait=wt)
            v1 = K.vx(lambda vec, b=b: vec.tensor_tensor(gsb[:], ps[4 + b][:, 0:E],
                                                         gbt[:], ALU.add),
                      wait=[(SPE, pg), cprev])
            bank_gate[4 + b] = (SVX, v1)
            v2 = K.vx(lambda vec: vec.reduce_max(gnm[:], gsb[:], axis=AX, negate=True),
                      wait=[(SVX, v1)])
            c1 = K.sc(lambda s: s.activation(gsb2[:], gsb[:], ACT.Exp, bias=gnm[:],
                                             accum_out=gden[:]), wait=[(SVX, v2)])
            cprev = (SSC, c1)
            vgr = K.vx(lambda vec: vec.reciprocal(grd[:], gden[:]), wait=[(SSC, c1)])
            vmx = K.vx(lambda vec: vec.max(out=gmx[:], in_=gsb2[:]), wait=[(SSC, c1)])
            vsel = K.vx(lambda vec: vec.scalar_tensor_tensor(
                gind[:], gsb2[:], gmx[:, 1:2], gsb2[:], op0=ALU.is_ge, op1=ALU.mult),
                wait=[(SVX, vmx)])
            v3 = K.vx(lambda vec, st=st: vec.tensor_scalar_mul(
                w_tm[:, st, :], gind[:], grd[:]), wait=[(SVX, vsel), (SVX, vgr)])
            import os as _os7
            if _os7.environ.get("KFFN") == "1" and g == 0 and st == 0:
                def dcpg(vec):
                    vec.tensor_copy(x2pre[:, 0, 0:8], gsb[:])
                    vec.tensor_copy(x2pre[:, 0, 8:16], gsb2[:])
                    vec.tensor_copy(x2pre[:, 0, 16:24], gmx[:])
                    vec.tensor_copy(x2pre[:, 0, 24:32], gind[:])
                    return vec.tensor_copy(x2pre[:, 0, 32:40], w_tm[:, 0, :])
                dvg = K.vx(dcpg, wait=[(SVX, v3)])
                ddg = dump("gate0", x2pre.ap()[:, 0, :], [(SVX, dvg)])
                bank_gate[4] = (SOUT, ddg)
            v4_of[st] = K.vx(lambda vec, st=st, b=b: vec.tensor_copy(
                wbf[:, b, :], w_tm[:, st, :]),
                wait=[(SVX, v3), wtprev[b]])
            if st >= 1:
                emit_wt(st - 1)
        emit_wt(ST - 1)
        for st in range(ST):
            pb = None
            for oh in range(2):
                def mmb(te, st=st, oh=oh):
                    return te.matmul(ps[2 * (st % 2) + oh][:, 0:512], w_fm[:, st, :],
                                     ebT[:, 512 * oh:512 * oh + 512],
                                     start=True, stop=True)
                pb = K.pe(mmb, wait=[(SVX, v5_of[st]), (SGD, v_const),
                                     bank_gate[2 * (st % 2) + oh]])
            def init3(vec, st=st):
                b0 = ps[2 * (st % 2)]
                b1 = ps[2 * (st % 2) + 1]
                vec.tensor_tensor(x3acc[:, st, 0:512], b0[:, 0:512],
                                  x2tm[:, st, 0:512], ALU.add)
                return vec.tensor_tensor(x3acc[:, st, 512:1024], b1[:, 0:512],
                                         x2tm[:, st, 512:1024], ALU.add)
            iv = K.vx(init3, wait=[(SPE, pb), (SVX, ln_done[st])])
            bank_gate[2 * (st % 2)] = (SVX, iv)
            bank_gate[2 * (st % 2) + 1] = (SVX, iv)
        ev_done = {}
        v_wer = {}
        for e in range(2):
            v_wer[e] = K.gdma(wer[e][:], rg(wet_d[e]), wait=[ring_gate[e]])
        for e in range(E):
            if filler is not None:
                filler(e, "pre")
            pm = None
            for st in range(ST):
                b0i, b1i = 2 * (st % 2), 2 * (st % 2) + 1
                def mme(te, e=e, st=st, b0i=b0i, b1i=b1i):
                    l = None
                    for hc in range(8):
                        te.matmul(ps[b0i][:, 0:512], fmbuf[:, hc, 128 * st:128 * st + 128],
                                  wer[e % 2][:, hc, 0:512],
                                  start=(hc == 0), stop=(hc == 7))
                        l = te.matmul(ps[b1i][:, 0:512], fmbuf[:, hc, 128 * st:128 * st + 128],
                                      wer[e % 2][:, hc, 512:1024],
                                      start=(hc == 0), stop=(hc == 7))
                    return l
                wt = [(SGD, v_wer[e]), bank_gate[b0i], bank_gate[b1i]]
                pm = K.pe(mme, wait=wt)
                def evac(vec, e=e, st=st, b0i=b0i, b1i=b1i):
                    vec.scalar_tensor_tensor(
                        x3acc[:, st, 0:512], ps[b0i][:, 0:512],
                        w_tm[:, st, e:e + 1], x3acc[:, st, 0:512],
                        op0=ALU.mult, op1=ALU.add)
                    return vec.scalar_tensor_tensor(
                        x3acc[:, st, 512:1024], ps[b1i][:, 0:512],
                        w_tm[:, st, e:e + 1], x3acc[:, st, 512:1024],
                        op0=ALU.mult, op1=ALU.add)
                vv = K.vx(evac, wait=[(SPE, pm)])
                bank_gate[b0i] = (SVX, vv)
                bank_gate[b1i] = (SVX, vv)
                ev_done[(e, st)] = vv
            ring_gate[e % 2] = (SPE, pm)
            if e + 2 < E:
                v_wer[e + 2] = K.gdma(wer[(e + 2) % 2][:], rg(wet_d[e + 2]),
                                      wait=[(SPE, pm)])
            if filler is not None:
                filler(e, "post")
        last_out = None
        for st in range(ST):
            wg = [(SVX, ev_done[(E - 1, st)])]
            if len(out_hist) >= 2:
                wg.append((SOUT, out_hist[-2]))
            lnv = emit_ln(x3acc[:, st, :], wg, [outb[:, st % 2, :]], b=st % 2,
                          dummy=sq_scr[:])
            ov = K.odma(out_d[G * g + 128 * st:G * g + 128 * st + 128, :],
                        outb[:, st % 2, :], wait=[(SVX, lnv)])
            out_hist.append(ov)
            last_out = ov
        return ev_done[(E - 1, ST - 1)], last_out

    xo_last = {}
    xo_rv = []
    out_hist = []
    x2pre_gate = {}
    ring_gate = {0: None, 1: None}

    r0 = emit_projections(0)
    loc0 = emit_local(0, [(SVX, r0["qvx"]), (SVX, r0["kvx"]), (SVX, r0["vvx"]),
                          (SVX, r0["vone"]), (SVX, r0["glv"])])

    import os as _os
    if _os.environ.get("KONEGROUP") == "1" and stage == "attn":
        dump("q0", q_sb.ap().rearrange("p a n -> p (a n)"), [(SVX, r0["qvx"])])
        dump("k0", k_sb.ap().rearrange("p a n -> p (a n)"), [(SVX, r0["kvx"])])
        dump("v0", v_sb.ap().rearrange("p a b n -> p (a b n)"), [(SVX, r0["vvx"]), (SVX, r0["vone"])])
        dump("gl", gl_tm.ap().rearrange("p a n -> p (a n)"), [(SVX, r0["glv"])])
        dump("attnL0", attn_g[0].ap().rearrange("p a n -> p (a n)"), [(SVX, loc0)])
        K.final_wait()
        with nc.Block() as blk:
            K.emit(blk)
        es1.close()
        es.close()
        return nc

    v_xg[1] = K.gdma(xg[:], rg(xT)[:, :, G:G + G + 256], wait=[xg_free[0]])
    v_wk[1] = K.gdma(wbufB[:], rg(wk_d), wait=[xg_free[0]])
    v_wq[1] = K.gdma(wbufA[:], rg(wq_d), wait=[(SPE, r0["wA_free_pe"])])

    r1 = emit_projections(1)
    loc1 = emit_local(1, [(SVX, r1["qvx"]), (SVX, r1["kvx"]), (SVX, r1["vvx"]),
                          (SVX, r1["glv"])])

    v_wgk = K.gdma(wbufA[:, 0:4], rg(wgk_d), wait=[(SPE, r1["wA_free_pe"])])
    pool_step(2 * NPIECE)
    v_wgv = K.gdma(wbufB[:, 0:4], rg(wgv_d), wait=[xg_free[1]])
    gkgv, wgv_pe = emit_pooled_finish(v_wgk, v_wgv)
    v_rout = K.gdma(wbufA[:, 4:8], rg(rout_d), wait=[xg_free[1]])
    v_wout = K.gdma(wbufB[:], rg(wout_d), wait=[(SPE, wgv_pe)])

    dL0 = dump("attnL0", attn_g[0].ap().rearrange("p a n -> p (a n)"), [(SVX, loc0)])
    dL1 = dump("attnL1", attn_g[1].ap().rearrange("p a n -> p (a n)"), [(SVX, loc1)])
    import os
    _lat_lim = int(os.environ.get("KLAT_LIMIT", "0")) or None
    lat0 = emit_latent(0, gkgv, (SVX, r0["gqvx"]), v_rout,
                       extra_gate=(SOUT, dL0) if dL0 else None, only_st=_lat_lim)
    lat1 = emit_latent(1, gkgv, (SVX, r1["gqvx"]), v_rout,
                       extra_gate=(SOUT, dL1) if dL1 else None,
                       only_st=_lat_lim) if _lat_lim is None else lat0
    pe_after_latents = K.s_pe.n
    phase2_ready = (SPE, pe_after_latents)

    dump("q0", q_sb.ap().rearrange("p a n -> p (a n)"), [(SVX, r1["qvx"])])
    dump("k0", k_sb.ap().rearrange("p a n -> p (a n)"), [(SVX, r1["kvx"])])
    dump("gq0", gq_g[0].ap().rearrange("p a n -> p (a n)"), [(SVX, r0["gqvx"])])
    dump("pooled", pooled_tm[:], [(SVX, gkgv)])
    dump("gk", gk_fm.ap().rearrange("p a n -> p (a n)"), [(SVX, gkgv)])
    dump("gl", gl_tm.ap().rearrange("p a n -> p (a n)"), [(SVX, r1["glv"])])
    dump("attn0", attn_g[0].ap().rearrange("p a n -> p (a n)"), [(SVX, lat0)])
    dump("attn1", attn_g[1].ap().rearrange("p a n -> p (a n)"), [(SVX, lat1)])

    if stage == "attn":
        K.final_wait()
        with nc.Block() as blk:
            K.emit(blk)
        es1.close()
        es.close()
        return nc

    es1.close()
    sb2 = lambda n, shp, dty: es.enter_context(nc.sbuf_tensor(n, shp, dty))
    x2tm = sb2("x2tm", [128, ST, D], F32)
    x3acc = sb2("x3acc", [128, ST, D], F32)
    fmbuf = sb2("fmbuf", [128, 8, G], BF16)
    x2bf = sb2("x2bf", [128, 2, D], BF16)
    afm_s = sb2("afm_s", [128, 2, 8, 128], BF16)
    sq_scr = sb2("sq_scr", [128, D], F32)
    outb = sb2("outb", [128, 2, D], F32)
    xo_buf = sb2("xo_buf", [128, 3, D], F32)
    wer = [sb2(f"wer{i}", [128, 8, D], BF16) for i in range(2)]
    ring_gate[0] = phase2_ready
    ring_gate[1] = phase2_ready
    xo_last[0] = phase2_ready
    xo_last[1] = phase2_ready

    ln0, x2ev0 = emit_wout_ln1(0, lat0, v_wout, None)
    dump("x2_0", x2tm.ap().rearrange("p a n -> p (a n)"), [(SVX, ln0[ST - 1])])
    for _st in range(3):
        xo_fetch(1, _st)
    ffn0_done, out0 = emit_ffn(0, x2ev0, ln0)
    dw0 = dump("wtm0", w_tm.ap().rearrange("p a n -> p (a n)"), [(SVX, ffn0_done)])
    dx0 = dump("x3a0", x3acc.ap().rearrange("p a n -> p (a n)"), [(SVX, ffn0_done)])
    g1gate = (SOUT, dx0) if dx0 else (SVX, ffn0_done)
    ln1, x2ev1 = emit_wout_ln1(1, lat1, v_wout, g1gate)
    ffn1_done, out1 = emit_ffn(1, x2ev1, ln1)

    K.final_wait()
    with nc.Block() as blk:
        K.emit(blk)
    es.close()
    return nc



def _host_inputs(inputs):
    x = np.ascontiguousarray(inputs["L0_attn_x"].reshape(S, D))
    xTf = np.ascontiguousarray(x.T)
    expert_wT = np.ascontiguousarray(
        np.asarray(inputs["L0_ffn_expert_w"]).transpose(0, 2, 1))
    gate_wT = np.ascontiguousarray(np.asarray(inputs["L0_ffn_gate_w"]).T)
    gate_b_t = np.ascontiguousarray(
        np.broadcast_to(np.asarray(inputs["L0_ffn_gate_b"]), (128, E)))
    glw = np.asarray(inputs["L0_attn_gate_local_w"])
    grw = np.asarray(inputs["L0_attn_gate_remote_w"])
    glbd = np.zeros((D, NH), dtype=np.float32)
    grbd = np.zeros((D, NH), dtype=np.float32)
    for h in range(NH):
        glbd[64 * h:64 * h + 64, h] = glw[h]
        grbd[64 * h:64 * h + 64, h] = grw[h]
    iden = np.eye(128, dtype=np.float32)
    tri = np.zeros((128, 384), dtype=np.float32)
    kk = np.arange(128)[:, None]
    qq = np.arange(128)[None, :]
    tri[:, 0:128] = (qq >= kk).astype(np.float32)
    tri[:, 128:256] = 1.0
    tri[:, 256:384] = (kk > qq).astype(np.float32)
    psel = np.zeros((128, 256), dtype=np.float32)
    psel[:, 128] = 1.0

    per_core = []
    for c in range(NCORES):
        lo = T * c
        xT_own = np.zeros((D, T + 256), dtype=np.float32)
        if c == 0:
            xT_own[:, 256:] = xTf[:, 0:T]
        else:
            xT_own[:] = xTf[:, lo - 256:lo + T]
        lm = np.zeros((128, 16 * NL), dtype=np.float32)
        for gst in range(16):
            cmax = 8 * c + gst // 2
            lm[:, gst * NL + cmax + 1:(gst + 1) * NL] = NEG
        halov = np.full((128, 1), 0.0 if c == 0 else 1.0, dtype=np.float32)
        per_core.append(dict(
            xT=np.ascontiguousarray(xT_own),
            x_own=np.ascontiguousarray(x[lo:lo + T]),
            x_all=x,
            wq=np.asarray(inputs["L0_attn_wq"]),
            wk=np.asarray(inputs["L0_attn_wk"]),
            wv=np.asarray(inputs["L0_attn_wv"]),
            wgq=np.asarray(inputs["L0_attn_wgq"]),
            wgk=np.asarray(inputs["L0_attn_wgk"]),
            wgv=np.asarray(inputs["L0_attn_wgv"]),
            r_out_w=np.asarray(inputs["L0_attn_r_out_w"]),
            w_out=np.asarray(inputs["L0_attn_w_out"]),
            expert_wT=expert_wT,
            gate_wT=gate_wT,
            gate_b_t=gate_b_t,
            expert_b=np.asarray(inputs["L0_ffn_expert_b"]),
            glw_bd=glbd,
            grw_bd=grbd,
            iden_i=iden,
            trimask_i=tri,
            trihalo_i=tri * (0.0 if c == 0 else 1.0),
            poolsel_i=psel,
            lmask_i=lm,
            halov_i=halov,
        ))
    return per_core


def kernel(**inputs):
    per_core = _host_inputs(inputs)
    nc = build_nc(stage="full")
    res = run_bass_kernel_spmd(nc, per_core, core_ids=list(range(NCORES)))
    out = np.concatenate([res.results[c]["out"] for c in range(NCORES)], axis=0)
    return out.reshape(1, S, D)


# version 13
# speedup vs baseline: 1.1778x; 1.0909x over previous
import contextlib
import numpy as np

import concourse.bass as bass
import concourse.mybir as mybir
from concourse.bass_utils import run_bass_kernel_spmd

F32 = mybir.dt.float32
BF16 = mybir.dt.bfloat16
AX = mybir.AxisListType.X
ALU = mybir.AluOpType
ACT = mybir.ActivationFunctionType

NCORES = 8
S = 16384
D = 1024
T = S // NCORES
G = 1024
NG = T // G
NH = 16
HD = 64
HP = NH // 2
LH = 8
LHD = 64
LD = 512
NL = 64
W = 256
ST = G // 128
CH = (G + 256) // 128
E = 8
NEG = -1.0e30


class Sem:
    def __init__(self, es, nc, name):
        self.h = es.enter_context(nc.semaphore(name))
        self.n = 0


class Kb:
    def __init__(self, nc, es):
        self.nc = nc
        self.steps = []
        self.s_gd = Sem(es, nc, "s_gd")
        self.s_pe = Sem(es, nc, "s_pe")
        self.s_sc = Sem(es, nc, "s_sc")
        self.s_vx = Sem(es, nc, "s_vx")
        self.s_sd = Sem(es, nc, "s_sd")
        self.s_out = Sem(es, nc, "s_out")
        self.pe = self._mk("tensor", self.s_pe, 1)
        self.sc = self._mk("scalar", self.s_sc, 1)
        self.vx = self._mk("vector", self.s_vx, 1)
        self.gd = self._mk("gpsimd", self.s_gd, 16)
        self.sy = self._mk("sync", self.s_sd, 16)
        self.so = self._mk("sync", self.s_out, 16)

    def _mk(self, name, sem, bump):
        def add(fn, wait=None, inc=True):
            if inc:
                sem.n += bump
            tgt = sem.n if inc else None
            wl = [wv for wv in (wait or []) if wv is not None and wv[1] is not None]
            def step(eng, fn=fn, wl=wl, tgt=tgt):
                for (s, v) in wl:
                    eng.wait_ge(s.h, v)
                last = fn(eng)
                if tgt is not None:
                    assert last is not None, f"{name} step returned None with inc"
                    last.then_inc(sem.h, bump)
            self.steps.append((name, step))
            return tgt
        return add

    def gdma(self, dst, src, wait=None):
        return self.gd(lambda g, dst=dst, src=src: g.dma_start(out=dst, in_=src), wait=wait)

    def sdma(self, dst, src, wait=None):
        return self.sy(lambda sy, dst=dst, src=src: sy.dma_start(out=dst, in_=src), wait=wait)

    def odma(self, dst, src, wait=None):
        return self.so(lambda sy, dst=dst, src=src: sy.dma_start(out=dst, in_=src), wait=wait)

    def godma(self, dst, src, wait=None):
        self.s_out.n += 16
        tgt = self.s_out.n
        wl = [wv for wv in (wait or []) if wv is not None and wv[1] is not None]
        def step(eng, dst=dst, src=src, wl=wl, tgt=tgt):
            for (s, v) in wl:
                eng.wait_ge(s.h, v)
            eng.dma_start(out=dst, in_=src).then_inc(self.s_out.h, 16)
        self.steps.append(("gpsimd", step))
        return tgt

    def final_wait(self):
        v = self.s_out.n
        self.steps.append(("sync", lambda sy, v=v: sy.wait_ge(self.s_out.h, v)))

    def emit(self, blk):
        for name in ("sync", "gpsimd", "tensor", "scalar", "vector"):
            fns = [f for (e, f) in self.steps if e == name]
            if not fns:
                continue
            def body(eng, fns=fns):
                for f in fns:
                    f(eng)
            getattr(blk, name)(body)


def ap_off(t, extra_off, pattern):
    a = t.ap() if hasattr(t, "ap") else t
    return bass.AP(a.tensor, a.offset + extra_off, pattern)


def build_nc(stage="full", debug=()):
    nc = bass.Bass("TRN2", target_bir_lowering=False)
    es = contextlib.ExitStack()

    di = lambda n, shp: nc.dram_tensor(n, shp, F32, kind="ExternalInput").ap()
    xT = di("xT", [D, T + 256])
    x_own = di("x_own", [T, D])
    x_all = di("x_all", [S, D])
    wq_d = di("wq", [D, D]); wk_d = di("wk", [D, D]); wv_d = di("wv", [D, D])
    wgq_d = di("wgq", [D, LD]); wgk_d = di("wgk", [D, LD]); wgv_d = di("wgv", [D, LD])
    rout_d = di("r_out_w", [LD, D])
    wout_d = di("w_out", [D, D])
    wet_d = di("expert_wT", [E, D, D])
    gwt_d = di("gate_wT", [D, E])
    gbt_d = di("gate_b_t", [128, E])
    eb_d = di("expert_b", [E, D])
    glbd_d = di("glw_bd", [D, NH])
    grbd_d = di("grw_bd", [D, NH])
    iden_d = di("iden_i", [128, 128])
    tri_d = di("trimask_i", [128, 384])
    trih_d = di("trihalo_i", [128, 384])
    psel_d = di("poolsel_i", [128, 256])
    lmask_d = di("lmask_i", [128, 16 * NL])
    halov_d = di("halov_i", [128, 1])
    out_d = nc.dram_tensor("out", [T, D], F32, kind="ExternalOutput").ap()
    dbg = {}
    for name, shp in debug:
        dbg[name] = nc.dram_tensor("dbg_" + name, list(shp), F32, kind="ExternalOutput").ap()

    sbt = lambda n, shp, dty: es.enter_context(nc.sbuf_tensor(n, shp, dty))
    iden = sbt("iden", [128, 128], BF16)
    gwT = sbt("gwT", [128, 8, E], BF16)
    gbt = sbt("gbt", [128, E], F32)
    ebT = sbt("ebT", [E, D], BF16)
    sc_comb = sbt("sc_comb", [128, 8], F32)
    rd4 = sbt("rd4", [128, 8], F32)
    s1 = sbt("s1", [128, 2], F32); s2 = sbt("s2", [128, 2], F32)
    mvv = sbt("mvv", [128, 2], F32)
    varv = sbt("varv", [128, 2], F32)
    lnl = sbt("lnl", [128, 2], F32)
    rstd = sbt("rstd", [128, 2], F32); nmr = sbt("nmr", [128, 2], F32)
    gsb = sbt("gsb", [128, E], F32)
    gsb2 = sbt("gsb2", [128, E], F32)
    gnm = sbt("gnm", [128, 1], F32)
    gden = sbt("gden", [128, 1], F32)
    grd = sbt("grd", [128, 1], F32)
    gmx = sbt("gmx", [128, E], F32)
    gind = sbt("gind", [128, E], F32)
    w_tm = sbt("w_tm", [128, ST, E], F32)
    wbf = sbt("wbf", [128, 2, E], BF16)
    w_fm = sbt("w_fm", [8, ST, 128], BF16)
    attn_g = [sbt(f"attn{g}", [128, ST, D], BF16) for g in range(NG)]
    wbufB = sbt("wbufB", [128, 8, D], BF16)
    es1 = contextlib.ExitStack()
    sb1 = lambda n, shp, dty: es1.enter_context(nc.sbuf_tensor(n, shp, dty))
    trimask = sb1("trimask", [128, 384], BF16)
    trihalo = sb1("trihalo", [128, 384], BF16)
    poolsel = sb1("poolsel", [128, 256], BF16)
    glwbd = sb1("glwbd", [128, 8, NH], BF16)
    grwbd = sb1("grwbd", [128, 8, NH], BF16)
    lmaskb = sb1("lmaskb", [128, 16 * NL], BF16)
    halov = sb1("halov", [128, 1], F32)
    pooled_tm = sb1("pooled_tm", [NL, D], BF16)
    pooled_fm = sb1("pooled_fm", [128, 8, NL], BF16)
    gk_fm = sb1("gk_fm", [128, 4, NL], BF16)
    gv_fm = sb1("gv_fm", [128, 4, NL], BF16)
    gv_tm = sb1("gv_tm", [128, 4, NL], BF16)
    lp_sb = sb1("lp_sb", [128, 2, 8, NL], BF16)
    lpt2 = sb1("lpt2", [128, 2, 4, 128], BF16)
    combt = sb1("combt", [128, 512], BF16)
    lden = sb1("lden", [128, 2, 8], F32)
    lrd = sb1("lrd", [128, 2, 8], F32)
    gl_tm = sb1("gl_tm", [128, ST, NH], F32)
    gr_tm = [sb1(f"gr_tm{g}", [128, ST, NH], F32) for g in range(NG)]
    glfm = sb1("glfm", [16, G], BF16)
    sige = sb1("sige", [128, 128], F32)
    pbt = sb1("pbt", [128, 6, 2, 384], BF16)
    rem_fm = sb1("rem_fm", [128, 2, 4, 128], BF16)
    wbufA = sb1("wbufA", [128, 8, D], BF16)
    gq_g = [sb1(f"gq{g}", [128, 4, G], BF16) for g in range(NG)]
    q_sb = sb1("q_sb", [128, HP, G], BF16)
    k_sb = sb1("k_sb", [128, HP, G + 256], BF16)
    v_sb = sb1("v_sb", [128, CH, NH, 65], BF16)
    xg = sb1("xg", [128, 8, G + 256], BF16)
    poolb = sb1("poolb", [128, 8, D], BF16)
    x2tm = x2fm = x3acc = fmbuf = x2bf = sq_scr = outb = xo_buf = None
    wer = x2pre = None

    ps = [es.enter_context(nc.psum_tensor(f"ps{i}", [128, 512], F32)) for i in range(6)]
    ps_pool = es.enter_context(nc.psum_tensor("ps_pool", [128, 1024], F32))

    K = Kb(nc, es)
    SGD, SPE, SSC, SVX, SSD, SOUT = K.s_gd, K.s_pe, K.s_sc, K.s_vx, K.s_sd, K.s_out
    bank_gate = [None] * 6
    pspool_gate = [None, None]

    def dump(name, src_ap, wait):
        if name in dbg:
            return K.godma(dbg[name], src_ap, wait=wait)
        return None

    rg = lambda d: d.rearrange("(a p) n -> p a n", p=128)
    K.gdma(iden[:], iden_d)
    K.gdma(trimask[:], tri_d)
    K.gdma(trihalo[:], trih_d)
    K.gdma(poolsel[:], psel_d)
    K.gdma(glwbd[:], rg(glbd_d))
    K.gdma(grwbd[:], rg(grbd_d))
    K.gdma(gwT[:], rg(gwt_d))
    K.gdma(gbt[:], gbt_d)
    K.gdma(ebT[:], eb_d)
    K.gdma(lmaskb[:], lmask_d)
    v_const = K.gdma(halov[:], halov_d)

    v_xg = [None, None]
    v_wq = [None, None]; v_wk = [None, None]; v_wv = [None, None]; v_wgq = [None, None]
    v_xg[0] = K.gdma(xg[:], rg(xT)[:, :, 0:G + 256])
    v_wq[0] = K.gdma(wbufA[:], rg(wq_d))
    v_wk[0] = K.gdma(wbufB[:], rg(wk_d))

    NPIECE = S // 128
    pool_dma_v = []
    pool_pe_v = []

    def _pool_dma1():
        i = len(pool_dma_v)
        w = [(SPE, pool_pe_v[i - 8])] if i >= 8 else None
        v = K.gdma(poolb[:, i % 8], x_all[128 * i:128 * i + 128, :], wait=w)
        pool_dma_v.append(v)

    def pool_step(n):
        for _ in range(n):
            i, j = len(pool_dma_v), len(pool_pe_v)
            if i >= NPIECE and j >= NPIECE:
                return
            if i < NPIECE and i <= j + 7:
                _pool_dma1()
            else:
                _pool_mm1()

    def _pool_mm1():
            i = len(pool_pe_v)
            if i >= NPIECE or i >= len(pool_dma_v):
                return
            def mm(te, i=i):
                last = None
                p = i // 2
                for hf in range(2):
                    last = te.matmul(ps_pool[:, 512 * hf:512 * hf + 512],
                                     poolsel[:, 128 - p:256 - p],
                                     poolb[:, i % 8, 512 * hf:512 * hf + 512],
                                     start=(i == 0),
                                     stop=(i == NPIECE - 1 and hf == 1))
                return last
            v = K.pe(mm, wait=[(SGD, pool_dma_v[i])])
            pool_pe_v.append(v)

    def proj_fm(dst_fn, wsb, w_v, xg_lo, xg_n, nchunk, v_xg_cur, wfree=1024):
        nblk = (xg_n + 511) // 512
        last_pe = last_vx = None
        wflat = wsb.ap().rearrange("p a n -> p (a n)")
        for i in range(nchunk):
            for nb in range(nblk):
                n0 = 512 * nb
                nn = min(xg_n, n0 + 512) - n0
                bi = (i * nblk + nb) % 2
                bank = ps[bi]
                def mm(te, i=i, n0=n0, nn=nn, bank=bank, wflat=wflat, wfree=wfree):
                    last = None
                    for hc in range(8):
                        last = te.matmul(bank[:, 0:nn],
                                         wflat[:, wfree * hc + 128 * i:wfree * hc + 128 * i + 128],
                                         xg[:, hc, xg_lo + n0:xg_lo + n0 + nn],
                                         start=(hc == 0), stop=(hc == 7))
                    return last
                last_pe = K.pe(mm, wait=[(SGD, max(w_v, v_xg_cur)), bank_gate[bi]])
                def ev(vec, i=i, n0=n0, nn=nn, bank=bank, dst_fn=dst_fn):
                    return vec.tensor_copy(dst_fn(i, n0, nn), bank[:, 0:nn])
                last_vx = K.vx(ev, wait=[(SPE, last_pe)])
                bank_gate[bi] = (SVX, last_vx)
        return last_pe, last_vx

    def emit_glgr(g, qvx_gate):
        last = None
        for (wsb_, dst) in ((glwbd, gl_tm), (grwbd, gr_tm[g])):
            ev = None
            for nb in range(2):
                bank = ps[4]
                def mm(te, wsb_=wsb_, nb=nb, bank=bank):
                    l = None
                    for hpp in range(HP):
                        l = te.matmul(bank[0:16, 0:512], wsb_[:, hpp, :],
                                      q_sb[:, hpp, 512 * nb:512 * nb + 512],
                                      start=(hpp == 0), stop=(hpp == HP - 1))
                    return l
                pv = K.pe(mm, wait=[(SVX, qvx_gate), (SGD, v_const), bank_gate[4]])
                ev = K.vx(lambda vec, nb=nb, bank=bank: vec.tensor_copy(
                    glfm[:, 512 * nb:512 * nb + 512], bank[0:16, 0:512]),
                    wait=[(SPE, pv)])
                bank_gate[4] = (SVX, ev)
            bank2 = ps[5]
            def mmt(te, bank2=bank2):
                l = None
                for st in range(ST):
                    l = te.matmul(bank2[:, 16 * st:16 * st + 16],
                                  glfm[:, 128 * st:128 * st + 128],
                                  iden[0:16, 0:16], start=True, stop=True)
                return l
            pv2 = K.pe(mmt, wait=[(SVX, ev), bank_gate[5]])
            c1 = K.sc(lambda s, bank2=bank2: s.activation(
                sige[:], bank2[:, 0:128], ACT.Exp, scale=-1.0),
                wait=[(SPE, pv2)])
            bank_gate[5] = (SSC, c1)
            sa = K.vx(lambda vec: vec.tensor_scalar_add(sige[:], sige[:], 1.0),
                      wait=[(SSC, c1)])
            last = K.vx(lambda vec, dst=dst: vec.reciprocal(
                dst.ap().rearrange("p a n -> p (a n)"), sige[:]), wait=[(SVX, sa)])
        return last

    xg_free = [None, None]

    def emit_projections(g):
        res = {}
        vxc = v_xg[g]
        qpe, qvx = proj_fm(lambda i, n0, nn: q_sb.ap()[:, i, n0:n0 + nn],
                           wbufA, v_wq[g], 256, G, HP, vxc)
        pool_step(8 if g == 0 else 20)
        glv = emit_glgr(g, qvx)
        v_wv[g] = K.gdma(wbufA[:], rg(wv_d), wait=[(SPE, qpe)])
        kpe, kvx = proj_fm(lambda i, n0, nn: k_sb.ap()[:, i, n0:n0 + nn],
                           wbufB, v_wk[g], 0, G + 256, HP, vxc)
        pool_step(8 if g == 0 else 20)
        v_wgq[g] = K.gdma(wbufB[:, 0:4], rg(wgq_d), wait=[(SPE, kpe)])
        vvx = vpe = None
        for sc_ in range(CH):
            for oh in range(2):
                bi = (sc_ * 2 + oh) % 2
                bank = ps[bi]
                def mm(te, sc_=sc_, oh=oh, bank=bank):
                    last = None
                    for hc in range(8):
                        last = te.matmul(bank[:, 0:512],
                                         xg[:, hc, 128 * sc_:128 * sc_ + 128],
                                         wbufA[:, hc, 512 * oh:512 * oh + 512],
                                         start=(hc == 0), stop=(hc == 7))
                    return last
                vpe = K.pe(mm, wait=[(SGD, max(v_wv[g], vxc)), bank_gate[bi]])
                def ev(vec, sc_=sc_, oh=oh, bank=bank):
                    dst = v_sb.ap()[:, sc_, 8 * oh:8 * oh + 8, 0:64]
                    src = bass.AP(bank.ap().tensor, bank.ap().offset,
                                  [[512, 128], [64, 8], [1, 64]])
                    return vec.tensor_copy(dst, src)
                vvx = K.vx(ev, wait=[(SPE, vpe)])
                bank_gate[bi] = (SVX, vvx)
        vone = K.vx(lambda vec: vec.memset(v_sb.ap()[:, :, :, 64:65], 1.0)) if g == 0 else None
        pool_step(8 if g == 0 else 20)
        res["wA_free_pe"] = vpe
        gqpe, gqvx = proj_fm(lambda i, n0, nn: gq_g[g].ap()[:, i, n0:n0 + nn],
                             wbufB, v_wgq[g], 256, G, 4, vxc, wfree=512)
        pool_step(12 if g == 0 else 32)
        xg_free[g] = (SPE, gqpe)
        res.update(qvx=qvx, kvx=kvx, vvx=vvx, vone=vone, glv=glv,
                   gqvx=gqvx, gqpe=gqpe, qpe=qpe)
        return res

    def emit_local(g, pre_gates):
        attn = attn_g[g]
        first = True
        slot_pv = {}
        units = []
        for hpp in range(HP):
            for half in range(2):
                tlo_h, thi_h = 4 * half, 4 * half + 4
                for ca in range(tlo_h, thi_h + 2):
                    tlo = max(tlo_h, ca - 2)
                    thi = min(thi_h, ca + 1)
                    if tlo < thi:
                        units.append((hpp, half, ca, tlo, thi))
        uidx = {(u[0], u[1], u[2]): i for i, u in enumerate(units)}
        pvjobs = []
        for hpp in range(HP):
            for half in range(2):
                for t in range(4 * half, 4 * half + 4):
                    pvjobs.append((hpp, half, t, uidx[(hpp, half, t + 2)]))
        pvjobs.sort(key=lambda j: j[3])
        LAG = 4
        mask_val = {}
        span_of = {}
        slot_of = {}
        pv_emitted = 0
        finv = None
        fin_gate = None
        last_pv_of_half = {}

        def emit_pv(j):
            nonlocal finv, fin_gate
            (hpp, half, t, _) = pvjobs[j]
            parts = tuple((t + jj, slot_of[(hpp, half, t + jj)],
                           t - span_of[(hpp, half, t + jj)][0]) for jj in range(3))
            def mmpv(te, t=t, hpp=hpp, half=half, parts=parts):
                last = None
                for jj, (cj, slj, blk) in enumerate(parts):
                    for par in range(2):
                        head = 2 * hpp + par
                        acc = ps[par][:, 66 * (t - 4 * half):66 * (t - 4 * half) + 65]
                        last = te.matmul(acc, pbt[:, slj, par, 128 * blk:128 * blk + 128],
                                         v_sb[:, cj, head, :],
                                         start=(jj == 0), stop=(jj == 2))
                return last
            wpv = [(SVX, mask_val[(hpp, half, t + 2)])]
            if t == 4 * half:
                wpv += [bank_gate[0], bank_gate[1]]
            pvv = K.pe(mmpv, wait=wpv)
            for jj in range(3):
                slot_pv[slot_of[(hpp, half, t + jj)]] = (SPE, pvv)
            if t == 4 * half + 3:
                def fin_a(vec):
                    vec.reciprocal(rd4[:, 0:4], ap_off(ps[0], 64, [[512, 128], [66, 4]]))
                    return vec.reciprocal(rd4[:, 4:8], ap_off(ps[1], 64, [[512, 128], [66, 4]]))
                fa = K.vx(fin_a, wait=[(SPE, pvv)])
                def fin_b(vec, half=half, hpp=hpp):
                    r = None
                    for par in range(2):
                        head = 2 * hpp + par
                        glc = ap_off(gl_tm, (4 * half) * NH + head, [[ST * NH, 128], [NH, 4]])
                        r = vec.tensor_tensor(sc_comb[:, 4 * par:4 * par + 4],
                                              rd4[:, 4 * par:4 * par + 4], glc, ALU.mult)
                    return r
                fb = K.vx(fin_b, wait=[(SVX, fa)])
                def fin(vec, half=half, hpp=hpp, attn=attn):
                    r = None
                    for par in range(2):
                        head = 2 * hpp + par
                        for t4 in range(4):
                            tt = 4 * half + t4
                            r = vec.tensor_scalar_mul(
                                attn[:, tt, 64 * head:64 * head + 64],
                                ps[par][:, 66 * t4:66 * t4 + 64],
                                sc_comb[:, 4 * par + t4:4 * par + t4 + 1])
                    return r
                finv = K.vx(fin, wait=[(SVX, fb)])
                bank_gate[0] = (SVX, finv)
                bank_gate[1] = (SVX, finv)

        for i, (hpp, half, ca, tlo, thi) in enumerate(units):
            while pv_emitted < len(pvjobs) and pvjobs[pv_emitted][3] <= i - LAG:
                emit_pv(pv_emitted)
                pv_emitted += 1
            nn = 128 * (thi - tlo)
            span_of[(hpp, half, ca)] = (tlo, thi)
            sl = i % 6
            slot_of[(hpp, half, ca)] = sl
            be, bo = 2 + (i % 2) * 2, 3 + (i % 2) * 2
            def mmqk(te, hpp=hpp, ca=ca, tlo=tlo, nn=nn, be=be, bo=bo):
                te.matmul(ps[be][:, 0:nn],
                          k_sb[0:64, hpp, 128 * ca:128 * ca + 128],
                          q_sb[0:64, hpp, 128 * tlo:128 * tlo + nn],
                          start=True, stop=True)
                return te.matmul(ps[bo][:, 0:nn],
                                 k_sb[64:128, hpp, 128 * ca:128 * ca + 128],
                                 q_sb[64:128, hpp, 128 * tlo:128 * tlo + nn],
                                 start=True, stop=True, tile_position=(64, 0))
            wt = list(pre_gates) if first else []
            first = False
            wt += [bank_gate[be], bank_gate[bo]]
            qk = K.pe(mmqk, wait=wt)
            wexp = [(SPE, qk), slot_pv.get(sl)]
            def ex(sc_, sl=sl, nn=nn, be=be, bo=bo):
                sc_.activation(pbt[:, sl, 0, 0:nn], ps[be][:, 0:nn], ACT.Exp, scale=0.125)
                return sc_.activation(pbt[:, sl, 1, 0:nn], ps[bo][:, 0:nn],
                                      ACT.Exp, scale=0.125)
            exv = K.sc(ex, wait=wexp)
            bank_gate[be] = (SSC, exv)
            bank_gate[bo] = (SSC, exv)
            rlo = tlo + 2 - ca
            msk = trihalo if (g == 0 and ca < 2) else trimask
            def mk(vec, sl=sl, nn=nn, rlo=rlo, msk=msk):
                vec.tensor_tensor(pbt[:, sl, 0, 0:nn], pbt[:, sl, 0, 0:nn],
                                  msk[:, 128 * rlo:128 * rlo + nn], ALU.mult)
                return vec.tensor_tensor(pbt[:, sl, 1, 0:nn], pbt[:, sl, 1, 0:nn],
                                         msk[:, 128 * rlo:128 * rlo + nn], ALU.mult)
            mkv = K.vx(mk, wait=[(SSC, exv)])
            mask_val[(hpp, half, ca)] = mkv
            if i % 6 == 0:
                pool_step(3)
        while pv_emitted < len(pvjobs):
            emit_pv(pv_emitted)
            pv_emitted += 1
        return finv

    def emit_pooled_finish(v_wgk, v_wgv):
        pe_last = pool_pe_v[-1]
        c1 = K.sc(lambda s: s.activation(pooled_tm[:], ps_pool[0:64, :], ACT.Copy,
                                         scale=1.0 / 256), wait=[(SPE, pe_last)])
        pspool_gate[0] = (SSC, c1)
        pspool_gate[1] = (SSC, c1)
        def mmt(te):
            l = None
            for hc in range(8):
                l = te.matmul(ps[0][:, 64 * hc:64 * hc + 64],
                              pooled_tm[:, 128 * hc:128 * hc + 128],
                              iden[0:64, 0:64], start=True, stop=True)
            return l
        p1 = K.pe(mmt, wait=[(SSC, c1), bank_gate[0]])
        vfm = K.vx(lambda vec: vec.tensor_copy(
            pooled_fm.ap().rearrange("p a n -> p (a n)"), ps[0][:, 0:512]),
            wait=[(SPE, p1)])
        bank_gate[0] = (SVX, vfm)
        evs = []
        pvl = None
        for (wv_, wsb_, dst, bi) in ((v_wgk, wbufA, gk_fm, 1), (v_wgv, wbufB, gv_fm, 2)):
            wflat = wsb_.ap().rearrange("p a n -> p (a n)")
            def mm(te, wflat=wflat, bi=bi):
                l = None
                for lc in range(4):
                    for hc in range(8):
                        l = te.matmul(ps[bi][:, 64 * lc:64 * lc + 64],
                                      wflat[:, 512 * hc + 128 * lc:512 * hc + 128 * lc + 128],
                                      pooled_fm[:, hc, :],
                                      start=(hc == 0), stop=(hc == 7))
                return l
            pvl = K.pe(mm, wait=[(SGD, wv_), (SVX, vfm), bank_gate[bi]])
            ev = K.vx(lambda vec, dst=dst, bi=bi: vec.tensor_copy(
                dst.ap().rearrange("p a n -> p (a n)"), ps[bi][:, 0:256]),
                wait=[(SPE, pvl)])
            bank_gate[bi] = (SVX, ev)
            evs.append(ev)
        tvs = None
        for lc in range(4):
            def mmtv(te, lc=lc):
                te.matmul(ps[3][0:64, 0:64], gv_fm[0:64, lc, :], iden[0:64, 0:64],
                          start=True, stop=True)
                return te.matmul(ps[4][64:128, 0:64], gv_fm[64:128, lc, :],
                                 iden[64:128, 64:128], start=True, stop=True,
                                 tile_position=(64, 64))
            wtv = [(SVX, evs[-1]), bank_gate[3], bank_gate[4]]
            ptv = K.pe(mmtv, wait=wtv)
            def evtv(vec, lc=lc):
                vec.tensor_copy(gv_tm[0:64, lc, :], ps[3][0:64, 0:64])
                return vec.tensor_copy(gv_tm[64:128, lc, :], ps[4][64:128, 0:64])
            tvs = K.vx(evtv, wait=[(SPE, ptv)])
            bank_gate[3] = (SVX, tvs)
            bank_gate[4] = (SVX, tvs)
        return tvs, pvl

    def emit_latent(g, gkgv_v, gq_gate, v_rout, extra_gate=None, only_st=None):
        attn = attn_g[g]
        comb_gate = [None]
        lp_gate = {}
        lpt_gate = {}
        rem_gate = {}
        st_state = {}
        nst = ST if only_st is None else only_st

        def s1_lsc(st):
            b = st % 2
            gst0 = ST * g + st
            def mml(te, st=st, g=g, gst0=gst0):
                last = None
                onesrow = trimask[0:1, 128:256]
                lrow = lmaskb.ap()[0:1, NL * gst0:NL * gst0 + NL]
                for lp_ in range(4):
                    te.matmul(ps[2][:, 64 * lp_:64 * lp_ + 64],
                              gq_g[g][0:64, lp_, 128 * st:128 * st + 128],
                              gk_fm[0:64, lp_, :], start=True, stop=False)
                    te.matmul(ps[3][:, 64 * lp_:64 * lp_ + 64],
                              gq_g[g][64:128, lp_, 128 * st:128 * st + 128],
                              gk_fm[64:128, lp_, :], start=True, stop=False,
                              tile_position=(64, 0))
                    te.matmul(ps[2][:, 64 * lp_:64 * lp_ + 64], onesrow, lrow,
                              start=False, stop=True)
                    last = te.matmul(ps[3][:, 64 * lp_:64 * lp_ + 64], onesrow, lrow,
                                     start=False, stop=True)
                return last
            wt = [bank_gate[2], bank_gate[3]]
            if st == 0 and g == 0:
                wt += [(SVX, gkgv_v)]
            if st == 0:
                wt += [gq_gate]
                if extra_gate is not None:
                    wt.append(extra_gate)
            return K.pe(mml, wait=wt)

        def s2_softmax(st, p1):
            b = st % 2
            wex = [(SPE, p1), lp_gate.get(b)]
            def ex(s, b=b):
                r = None
                for lp_ in range(4):
                    s.activation(lp_sb[:, b, 2 * lp_, :], ps[2][:, 64 * lp_:64 * lp_ + 64],
                                 ACT.Exp, scale=0.125,
                                 accum_out=lden[:, b, 2 * lp_:2 * lp_ + 1])
                    r = s.activation(lp_sb[:, b, 2 * lp_ + 1, :], ps[3][:, 64 * lp_:64 * lp_ + 64],
                                     ACT.Exp, scale=0.125,
                                     accum_out=lden[:, b, 2 * lp_ + 1:2 * lp_ + 2])
                return r
            c1 = K.sc(ex, wait=wex)
            bank_gate[2] = (SSC, c1)
            bank_gate[3] = (SSC, c1)
            vr = K.vx(lambda vec, b=b: vec.reciprocal(lrd[:, b, :], lden[:, b, :]),
                      wait=[(SSC, c1)])
            def nrm(vec, b=b):
                r = None
                for lh in range(8):
                    r = vec.tensor_scalar_mul(lp_sb[:, b, lh, :], lp_sb[:, b, lh, :],
                                              lrd[:, b, lh:lh + 1])
                return r
            return K.vx(nrm, wait=[(SVX, vr)])

        def s3_transpose(st, v2):
            b = st % 2
            def mmt(te, b=b):
                last = None
                for lp_ in range(4):
                    te.matmul(ps[4][0:64, 128 * lp_:128 * lp_ + 128],
                              lp_sb[:, b, 2 * lp_, :], iden[:], start=True, stop=True)
                    last = te.matmul(ps[5][64:128, 128 * lp_:128 * lp_ + 128],
                                     lp_sb[:, b, 2 * lp_ + 1, :], iden[:],
                                     start=True, stop=True, tile_position=(0, 64))
                return last
            p2 = K.pe(mmt, wait=[(SVX, v2), bank_gate[4], bank_gate[5]])
            lp_gate[b] = (SPE, p2)
            def evt(sc, b=b):
                sc.activation(lpt2.ap()[0:64, b].rearrange("p a n -> p (a n)"),
                              ps[4][0:64, 0:512], ACT.Copy)
                return sc.activation(lpt2.ap()[64:128, b].rearrange("p a n -> p (a n)"),
                                     ps[5][64:128, 0:512], ACT.Copy)
            v3 = K.sc(evt, wait=[(SPE, p2), lpt_gate.get(b)])
            bank_gate[4] = (SSC, v3)
            bank_gate[5] = (SSC, v3)
            return v3

        def s5_pv(st, v3):
            b = st % 2
            def mmp(te, b=b):
                last = None
                for lp_ in range(4):
                    te.matmul(ps[0][0:64, 128 * lp_:128 * lp_ + 128],
                              gv_tm[0:64, lp_, :], lpt2[0:64, b, lp_, :],
                              start=True, stop=True)
                    last = te.matmul(ps[1][64:128, 128 * lp_:128 * lp_ + 128],
                                     gv_tm[64:128, lp_, :], lpt2[64:128, b, lp_, :],
                                     start=True, stop=True, tile_position=(64, 64))
                return last
            p3 = K.pe(mmp, wait=[(SSC, v3), bank_gate[0], bank_gate[1]])
            lpt_gate[b] = (SPE, p3)
            def evp(sc, b=b):
                sc.activation(rem_fm.ap()[0:64, b].rearrange("p a n -> p (a n)"),
                              ps[0][0:64, 0:512], ACT.Copy)
                return sc.activation(rem_fm.ap()[64:128, b].rearrange("p a n -> p (a n)"),
                                     ps[1][64:128, 0:512], ACT.Copy)
            v4 = K.sc(evp, wait=[(SPE, p3), rem_gate.get(b)])
            bank_gate[0] = (SSC, v4)
            bank_gate[1] = (SSC, v4)
            return v4

        def s7_rout(st, v4):
            b = st % 2
            prs = []
            for oh in range(2):
                def mmr(te, oh=oh, b=b):
                    l = None
                    for lc in range(4):
                        l = te.matmul(ps[oh][:, 0:512], rem_fm[:, b, lc, :],
                                      wbufA[:, 4 + lc, 512 * oh:512 * oh + 512],
                                      start=(lc == 0), stop=(lc == 3))
                    return l
                wt = [(SSC, v4), bank_gate[oh]]
                if st == 0 and g == 0:
                    wt.append((SGD, v_rout))
                pr = K.pe(mmr, wait=wt)
                prs.append(pr)
            rem_gate[b] = (SPE, prs[-1])
            last = None
            for oh in range(2):
                def comb_a(vec, oh=oh, st=st, g=g):
                    grx = gr_tm[g].ap()[:, st, 8 * oh:8 * oh + 8].unsqueeze(2)
                    grx = grx.broadcast_to([128, 8, 64])
                    src = ps[oh][:, 0:512].rearrange("p (a n) -> p a n", n=64)
                    return vec.tensor_tensor(
                        combt.ap().rearrange("p (a n) -> p a n", n=64),
                        src, grx, ALU.mult)
                ca_ = K.vx(comb_a, wait=[(SPE, prs[-1]), comb_gate[0]])
                bank_gate[oh] = (SVX, ca_)
                def comb_b(vec, oh=oh, st=st, attn=attn):
                    return vec.tensor_tensor(
                        attn[:, st, 512 * oh:512 * oh + 512],
                        attn[:, st, 512 * oh:512 * oh + 512],
                        combt[:], ALU.add)
                cv = K.vx(comb_b, wait=[(SVX, ca_)])
                comb_gate[0] = (SVX, cv)
                last = cv
            return last

        last = None
        for st in range(nst + 1):
            if st >= 1:
                stp = st - 1
                v3 = s3_transpose(stp, st_state[stp])
                v4 = s5_pv(stp, v3)
            if st < nst:
                p1 = s1_lsc(st)
                v2 = s2_softmax(st, p1)
                st_state[st] = v2
            if st >= 1:
                last = s7_rout(st - 1, v4)
        return last

    ln_prev = {}

    def emit_ln(src_ap, wait_gates, dsts, b=0, dummy=None):
        sq = dummy if dummy is not None else sq_scr.ap()[:, b, :]
        wg = list(wait_gates)
        if b in ln_prev:
            wg.append(ln_prev[b])
        def sums(s):
            s.activation(sq, src_ap, ACT.Copy, accum_out=s1[:, b:b + 1])
            return s.activation(sq, src_ap, ACT.Square, accum_out=s2[:, b:b + 1])
        c1 = K.sc(sums, wait=wg)
        v1 = c1
        def stats_a(vec):
            vec.tensor_scalar_mul(mvv[:, b:b + 1], s1[:, b:b + 1], -1.0 / D)
            return vec.tensor_scalar_mul(varv[:, b:b + 1], s2[:, b:b + 1], 1.0 / D)
        v2a = K.vx(stats_a, wait=[(SSC, c1)])
        v2b = K.vx(lambda vec: vec.scalar_tensor_tensor(
            lnl[:, b:b + 1], mvv[:, b:b + 1], mvv[:, b:b + 1], varv[:, b:b + 1],
            op0=ALU.mult, op1=ALU.subtract), wait=[(SVX, v2a)])
        v2 = K.vx(lambda vec: vec.tensor_scalar(
            varv[:, b:b + 1], lnl[:, b:b + 1], -1.0, 1e-5,
            op0=ALU.mult, op1=ALU.add), wait=[(SVX, v2b)])
        c2 = K.sc(lambda s: s.activation(lnl[:, b:b + 1], varv[:, b:b + 1], ACT.Ln),
                  wait=[(SVX, v2)])
        c3 = K.sc(lambda s: s.activation(rstd[:, b:b + 1], lnl[:, b:b + 1], ACT.Exp,
                                         scale=-0.5), wait=[(SSC, c2)])
        v3 = K.vx(lambda vec: vec.tensor_tensor(nmr[:, b:b + 1], mvv[:, b:b + 1],
                                                rstd[:, b:b + 1], ALU.mult),
                  wait=[(SSC, c3)])
        def norm(vec):
            r = None
            for d_ in dsts:
                r = vec.tensor_scalar(d_, src_ap, rstd[:, b:b + 1], nmr[:, b:b + 1],
                                      op0=ALU.mult, op1=ALU.add)
            return r
        out = K.vx(norm, wait=[(SVX, v3)])
        ln_prev[b] = (SVX, out)
        return out

    xo_q = {}

    def xo_fetch(g, st):
        if st >= ST:
            return
        w = [xo_last.get(st % 3)]
        xo_q[(g, st)] = K.sdma(xo_buf[:, st % 3],
                               x_own[G * g + 128 * st:G * g + 128 * st + 128, :], wait=w)

    def make_wout(g, attn_done, v_wout, prev_x2_free, wout_banks=(2, 3)):
        attn = attn_g[g]
        ln_done = {}
        x2fm_ev = {}
        tr_prev = {}
        afm_gate = {}
        wb0, wb1 = wout_banks
        for st in range(3):
            if (g, st) not in xo_q:
                xo_fetch(g, st)

        tvs_of = {}

        def stage_A1(st):
            tvs = []
            for b2 in range(2):
                bank = ps[4 + b2]
                def mmt(te, st=st, b2=b2, bank=bank, attn=attn):
                    l = None
                    for fc4 in range(4):
                        fc = 4 * b2 + fc4
                        l = te.matmul(bank[:, 128 * fc4:128 * fc4 + 128],
                                      attn[:, st, 128 * fc:128 * fc + 128],
                                      iden[:], start=True, stop=True)
                    return l
                wt = [bank_gate[4 + b2], afm_gate.get(st % 2)]
                if st == 0:
                    wt.append((SVX, attn_done))
                    wt.append(prev_x2_free)
                ptv = K.pe(mmt, wait=wt)
                def evt(vec, st=st, b2=b2, bank=bank):
                    return vec.tensor_copy(afm_s.ap()[:, st % 2, 4 * b2:4 * b2 + 4, :],
                                           bank[:, 0:512])
                tv = K.vx(evt, wait=[(SPE, ptv)])
                bank_gate[4 + b2] = (SVX, tv)
                tvs.append(tv)
            tvs_of[st] = tvs

        def stage_A2(st):
            tvs = tvs_of[st]
            pw = None
            for oh in range(2):
                def mmw(te, st=st, oh=oh):
                    l = None
                    for fc in range(8):
                        l = te.matmul(ps_pool[:, 512 * oh:512 * oh + 512],
                                      afm_s[:, st % 2, fc, :],
                                      wbufB[:, fc, 512 * oh:512 * oh + 512],
                                      start=(fc == 0), stop=(fc == 7))
                    return l
                pw = K.pe(mmw, wait=[(SVX, tvs[1]), (SGD, v_wout), pspool_gate[oh]])
            afm_gate[st % 2] = (SPE, pw)
            def resid(vec, st=st, g=g):
                vec.tensor_tensor(x2pre[:, st % 2, 0:512], ps_pool[:, 0:512],
                                  xo_buf[:, st % 3, 0:512], ALU.add)
                return vec.tensor_tensor(x2pre[:, st % 2, 512:1024],
                                         ps_pool[:, 512:1024],
                                         xo_buf[:, st % 3, 512:1024], ALU.add)
            rv = K.vx(resid, wait=[(SPE, pw), (SSD, xo_q[(g, st)]),
                                   x2pre_gate.get(st % 2)])
            pspool_gate[0] = (SVX, rv)
            pspool_gate[1] = (SVX, rv)
            xo_last[st % 3] = (SVX, rv)
            xo_fetch(g, st + 3)
            return rv

        def stage_B(st, rv):
            wln = [(SVX, rv)]
            lnv = emit_ln(x2pre.ap()[:, st % 2, :], wln, [x2tm[:, st, :]],
                          b=st % 2, dummy=sq_scr[:])
            ln_done[st] = lnv
            x2pre_gate[st % 2] = (SVX, lnv)
            return lnv

        def stage_C(st):
            cb = K.vx(lambda vec, st=st: vec.tensor_copy(x2bf[:, st % 2, :],
                                                         x2tm[:, st, :]),
                      wait=[(SVX, ln_done[st]), tr_prev.get(st % 2)])
            for b2 in range(2):
                bank = ps[b2]
                def mmx(te, st=st, b2=b2, bank=bank):
                    l = None
                    for fc4 in range(4):
                        fc = 4 * b2 + fc4
                        l = te.matmul(bank[:, 128 * fc4:128 * fc4 + 128],
                                      x2bf[:, st % 2, 128 * fc:128 * fc + 128],
                                      iden[:], start=True, stop=True)
                    return l
                px = K.pe(mmx, wait=[(SVX, cb), bank_gate[b2]])
                tr_prev[st % 2] = (SPE, px)
                def evx(vec, st=st, b2=b2, bank=bank):
                    dst = bass.AP(fmbuf.ap().tensor,
                                  fmbuf.ap().offset + (4 * b2) * G + 128 * st,
                                  [[8 * G, 128], [G, 4], [1, 128]])
                    return vec.tensor_copy(dst, bank[:, 0:512])
                xv = K.vx(evx, wait=[(SPE, px)])
                bank_gate[b2] = (SVX, xv)
                x2fm_ev[(st, b2)] = xv

        a1_done = set()

        def step_pre(st):
            if st < ST and st not in a1_done:
                a1_done.add(st)
                stage_A1(st)

        def step_post(st):
            rv = stage_A2(st)
            step_pre(st + 1)
            stage_B(st, rv)

        def finish():
            for st in range(ST):
                stage_C(st)

        return step_pre, step_post, finish, ln_done, x2fm_ev

    def emit_wout_ln1(g, attn_done, v_wout, prev_x2_free):
        step_pre, step_post, finish, ln_done, x2fm_ev = make_wout(
            g, attn_done, v_wout, prev_x2_free)
        step_pre(0)
        for st in range(ST):
            step_post(st)
        finish()
        return ln_done, x2fm_ev

    def emit_ffn(g, x2fm_ev, ln_done, filler=None, epilogue=None):
        cprev = None
        wtprev = [None, None]
        v4_of = {}
        v5_of = {}

        def emit_wt(st):
            b = st % 2
            def mmwt(te, st=st, b=b):
                return te.matmul(ps_pool[0:E, 512 * b:512 * b + 128], wbf[:, b, :],
                                 iden[:], start=True, stop=True)
            pwt = K.pe(mmwt, wait=[(SVX, v4_of[st]), pspool_gate[b]])
            wtprev[b] = (SPE, pwt)
            v5 = K.vx(lambda vec, st=st, b=b: vec.tensor_copy(
                w_fm[:, st, :], ps_pool[0:E, 512 * b:512 * b + 128]),
                wait=[(SPE, pwt)])
            pspool_gate[b] = (SVX, v5)
            v5_of[st] = v5

        for st in range(ST):
            b = st % 2
            def mmg(te, st=st, b=b):
                l = None
                for hc in range(8):
                    l = te.matmul(ps[4 + b][:, 0:E], fmbuf[:, hc, 128 * st:128 * st + 128],
                                  gwT[:, hc, :], start=(hc == 0), stop=(hc == 7))
                return l
            wt = [(SVX, x2fm_ev[(st, 1)]), bank_gate[4 + b]]
            pg = K.pe(mmg, wait=wt)
            v1 = K.vx(lambda vec, b=b: vec.tensor_tensor(gsb[:], ps[4 + b][:, 0:E],
                                                         gbt[:], ALU.add),
                      wait=[(SPE, pg), cprev])
            bank_gate[4 + b] = (SVX, v1)
            v2 = K.vx(lambda vec: vec.reduce_max(gnm[:], gsb[:], axis=AX, negate=True),
                      wait=[(SVX, v1)])
            c1 = K.sc(lambda s: s.activation(gsb2[:], gsb[:], ACT.Exp, bias=gnm[:],
                                             accum_out=gden[:]), wait=[(SVX, v2)])
            cprev = (SSC, c1)
            vgr = K.vx(lambda vec: vec.reciprocal(grd[:], gden[:]), wait=[(SSC, c1)])
            vmx = K.vx(lambda vec: vec.max(out=gmx[:], in_=gsb2[:]), wait=[(SSC, c1)])
            vsel = K.vx(lambda vec: vec.scalar_tensor_tensor(
                gind[:], gsb2[:], gmx[:, 1:2], gsb2[:], op0=ALU.is_ge, op1=ALU.mult),
                wait=[(SVX, vmx)])
            v3 = K.vx(lambda vec, st=st: vec.tensor_scalar_mul(
                w_tm[:, st, :], gind[:], grd[:]), wait=[(SVX, vsel), (SVX, vgr)])
            import os as _os7
            if _os7.environ.get("KFFN") == "1" and g == 0 and st == 0:
                def dcpg(vec):
                    vec.tensor_copy(x2pre[:, 0, 0:8], gsb[:])
                    vec.tensor_copy(x2pre[:, 0, 8:16], gsb2[:])
                    vec.tensor_copy(x2pre[:, 0, 16:24], gmx[:])
                    vec.tensor_copy(x2pre[:, 0, 24:32], gind[:])
                    return vec.tensor_copy(x2pre[:, 0, 32:40], w_tm[:, 0, :])
                dvg = K.vx(dcpg, wait=[(SVX, v3)])
                ddg = dump("gate0", x2pre.ap()[:, 0, :], [(SVX, dvg)])
                bank_gate[4] = (SOUT, ddg)
            v4_of[st] = K.vx(lambda vec, st=st, b=b: vec.tensor_copy(
                wbf[:, b, :], w_tm[:, st, :]),
                wait=[(SVX, v3), wtprev[b]])
            if st >= 1:
                emit_wt(st - 1)
        emit_wt(ST - 1)
        for st in range(ST):
            pb = None
            for oh in range(2):
                def mmb(te, st=st, oh=oh):
                    return te.matmul(ps[2 * (st % 2) + oh][:, 0:512], w_fm[:, st, :],
                                     ebT[:, 512 * oh:512 * oh + 512],
                                     start=True, stop=True)
                pb = K.pe(mmb, wait=[(SVX, v5_of[st]), (SGD, v_const),
                                     bank_gate[2 * (st % 2) + oh]])
            def init3(vec, st=st):
                b0 = ps[2 * (st % 2)]
                b1 = ps[2 * (st % 2) + 1]
                vec.tensor_tensor(x3acc[:, st, 0:512], b0[:, 0:512],
                                  x2tm[:, st, 0:512], ALU.add)
                return vec.tensor_tensor(x3acc[:, st, 512:1024], b1[:, 0:512],
                                         x2tm[:, st, 512:1024], ALU.add)
            iv = K.vx(init3, wait=[(SPE, pb), (SVX, ln_done[st])])
            bank_gate[2 * (st % 2)] = (SVX, iv)
            bank_gate[2 * (st % 2) + 1] = (SVX, iv)
        ev_done = {}
        v_wer = {}
        for e in range(2):
            v_wer[e] = K.gdma(wer[e][:], rg(wet_d[e]), wait=[ring_gate[e]])
        for e in range(E):
            if filler is not None:
                filler(e, "pre")
            pm = None
            for st in range(ST):
                b0i, b1i = 2 * (st % 2), 2 * (st % 2) + 1
                def mme(te, e=e, st=st, b0i=b0i, b1i=b1i):
                    l = None
                    for hc in range(8):
                        te.matmul(ps[b0i][:, 0:512], fmbuf[:, hc, 128 * st:128 * st + 128],
                                  wer[e % 2][:, hc, 0:512],
                                  start=(hc == 0), stop=(hc == 7))
                        l = te.matmul(ps[b1i][:, 0:512], fmbuf[:, hc, 128 * st:128 * st + 128],
                                      wer[e % 2][:, hc, 512:1024],
                                      start=(hc == 0), stop=(hc == 7))
                    return l
                wt = [(SGD, v_wer[e]), bank_gate[b0i], bank_gate[b1i]]
                pm = K.pe(mme, wait=wt)
                def evac(vec, e=e, st=st, b0i=b0i, b1i=b1i):
                    vec.scalar_tensor_tensor(
                        x3acc[:, st, 0:512], ps[b0i][:, 0:512],
                        w_tm[:, st, e:e + 1], x3acc[:, st, 0:512],
                        op0=ALU.mult, op1=ALU.add)
                    return vec.scalar_tensor_tensor(
                        x3acc[:, st, 512:1024], ps[b1i][:, 0:512],
                        w_tm[:, st, e:e + 1], x3acc[:, st, 512:1024],
                        op0=ALU.mult, op1=ALU.add)
                vv = K.vx(evac, wait=[(SPE, pm)])
                bank_gate[b0i] = (SVX, vv)
                bank_gate[b1i] = (SVX, vv)
                ev_done[(e, st)] = vv
            ring_gate[e % 2] = (SPE, pm)
            if e + 2 < E:
                v_wer[e + 2] = K.gdma(wer[(e + 2) % 2][:], rg(wet_d[e + 2]),
                                      wait=[(SPE, pm)])
            if filler is not None:
                filler(e, "post")
        if epilogue is not None:
            epilogue()
        last_out = None
        for st in range(ST):
            wg = [(SVX, ev_done[(E - 1, st)])]
            if len(out_hist) >= 2:
                wg.append((SOUT, out_hist[-2]))
            lnv = emit_ln(x3acc[:, st, :], wg, [outb[:, st % 2, :]], b=st % 2,
                          dummy=sq_scr[:])
            ov = K.odma(out_d[G * g + 128 * st:G * g + 128 * st + 128, :],
                        outb[:, st % 2, :], wait=[(SVX, lnv)])
            out_hist.append(ov)
            last_out = ov
        return ev_done[(E - 1, ST - 1)], last_out

    xo_last = {}
    xo_rv = []
    out_hist = []
    x2pre_gate = {}
    ring_gate = {0: None, 1: None}

    r0 = emit_projections(0)
    loc0 = emit_local(0, [(SVX, r0["qvx"]), (SVX, r0["kvx"]), (SVX, r0["vvx"]),
                          (SVX, r0["vone"]), (SVX, r0["glv"])])

    import os as _os
    if _os.environ.get("KONEGROUP") == "1" and stage == "attn":
        dump("q0", q_sb.ap().rearrange("p a n -> p (a n)"), [(SVX, r0["qvx"])])
        dump("k0", k_sb.ap().rearrange("p a n -> p (a n)"), [(SVX, r0["kvx"])])
        dump("v0", v_sb.ap().rearrange("p a b n -> p (a b n)"), [(SVX, r0["vvx"]), (SVX, r0["vone"])])
        dump("gl", gl_tm.ap().rearrange("p a n -> p (a n)"), [(SVX, r0["glv"])])
        dump("attnL0", attn_g[0].ap().rearrange("p a n -> p (a n)"), [(SVX, loc0)])
        K.final_wait()
        with nc.Block() as blk:
            K.emit(blk)
        es1.close()
        es.close()
        return nc

    v_xg[1] = K.gdma(xg[:], rg(xT)[:, :, G:G + G + 256], wait=[xg_free[0]])
    v_wk[1] = K.gdma(wbufB[:], rg(wk_d), wait=[xg_free[0]])
    v_wq[1] = K.gdma(wbufA[:], rg(wq_d), wait=[(SPE, r0["wA_free_pe"])])

    r1 = emit_projections(1)
    loc1 = emit_local(1, [(SVX, r1["qvx"]), (SVX, r1["kvx"]), (SVX, r1["vvx"]),
                          (SVX, r1["glv"])])

    v_wgk = K.gdma(wbufA[:, 0:4], rg(wgk_d), wait=[(SPE, r1["wA_free_pe"])])
    pool_step(2 * NPIECE)
    v_wgv = K.gdma(wbufB[:, 0:4], rg(wgv_d), wait=[xg_free[1]])
    gkgv, wgv_pe = emit_pooled_finish(v_wgk, v_wgv)
    v_rout = K.gdma(wbufA[:, 4:8], rg(rout_d), wait=[xg_free[1]])
    v_wout = K.gdma(wbufB[:], rg(wout_d), wait=[(SPE, wgv_pe)])

    dL0 = dump("attnL0", attn_g[0].ap().rearrange("p a n -> p (a n)"), [(SVX, loc0)])
    dL1 = dump("attnL1", attn_g[1].ap().rearrange("p a n -> p (a n)"), [(SVX, loc1)])
    import os
    _lat_lim = int(os.environ.get("KLAT_LIMIT", "0")) or None
    lat0 = emit_latent(0, gkgv, (SVX, r0["gqvx"]), v_rout,
                       extra_gate=(SOUT, dL0) if dL0 else None, only_st=_lat_lim)
    lat1 = emit_latent(1, gkgv, (SVX, r1["gqvx"]), v_rout,
                       extra_gate=(SOUT, dL1) if dL1 else None,
                       only_st=_lat_lim) if _lat_lim is None else lat0
    pe_after_latents = K.s_pe.n
    phase2_ready = (SPE, pe_after_latents)

    dump("q0", q_sb.ap().rearrange("p a n -> p (a n)"), [(SVX, r1["qvx"])])
    dump("k0", k_sb.ap().rearrange("p a n -> p (a n)"), [(SVX, r1["kvx"])])
    dump("gq0", gq_g[0].ap().rearrange("p a n -> p (a n)"), [(SVX, r0["gqvx"])])
    dump("pooled", pooled_tm[:], [(SVX, gkgv)])
    dump("gk", gk_fm.ap().rearrange("p a n -> p (a n)"), [(SVX, gkgv)])
    dump("gl", gl_tm.ap().rearrange("p a n -> p (a n)"), [(SVX, r1["glv"])])
    dump("attn0", attn_g[0].ap().rearrange("p a n -> p (a n)"), [(SVX, lat0)])
    dump("attn1", attn_g[1].ap().rearrange("p a n -> p (a n)"), [(SVX, lat1)])

    if stage == "attn":
        K.final_wait()
        with nc.Block() as blk:
            K.emit(blk)
        es1.close()
        es.close()
        return nc

    es1.close()
    sb2 = lambda n, shp, dty: es.enter_context(nc.sbuf_tensor(n, shp, dty))
    x2pre = sb2("x2pre", [128, 2, D], F32)
    x2tm = sb2("x2tm", [128, ST, D], F32)
    x3acc = sb2("x3acc", [128, ST, D], F32)
    fmbuf = sb2("fmbuf", [128, 8, G], BF16)
    x2bf = sb2("x2bf", [128, 2, D], BF16)
    afm_s = sb2("afm_s", [128, 2, 8, 128], BF16)
    sq_scr = sb2("sq_scr", [128, D], F32)
    outb = sb2("outb", [128, 2, D], F32)
    xo_buf = sb2("xo_buf", [128, 3, D], F32)
    wer = [sb2(f"wer{i}", [128, 8, D], BF16) for i in range(2)]
    ring_gate[0] = phase2_ready
    ring_gate[1] = phase2_ready
    xo_last[0] = phase2_ready
    xo_last[1] = phase2_ready

    ln0, x2ev0 = emit_wout_ln1(0, lat0, v_wout, None)
    dump("x2_0", x2tm.ap().rearrange("p a n -> p (a n)"), [(SVX, ln0[ST - 1])])
    for _st in range(3):
        xo_fetch(1, _st)
    pre1, post1, finish1, ln1, x2ev1 = make_wout(1, lat1, v_wout, None)

    def filler1(e, ph):
        if ph == "pre":
            pre1(e)
        else:
            post1(e)
    ffn0_done, out0 = emit_ffn(0, x2ev0, ln0, filler=filler1, epilogue=finish1)
    dw0 = dump("wtm0", w_tm.ap().rearrange("p a n -> p (a n)"), [(SVX, ffn0_done)])
    dx0 = dump("x3a0", x3acc.ap().rearrange("p a n -> p (a n)"), [(SVX, ffn0_done)])
    ffn1_done, out1 = emit_ffn(1, x2ev1, ln1)

    K.final_wait()
    with nc.Block() as blk:
        K.emit(blk)
    es.close()
    return nc



def _host_inputs(inputs):
    x = np.ascontiguousarray(inputs["L0_attn_x"].reshape(S, D))
    xTf = np.ascontiguousarray(x.T)
    expert_wT = np.ascontiguousarray(
        np.asarray(inputs["L0_ffn_expert_w"]).transpose(0, 2, 1))
    gate_wT = np.ascontiguousarray(np.asarray(inputs["L0_ffn_gate_w"]).T)
    gate_b_t = np.ascontiguousarray(
        np.broadcast_to(np.asarray(inputs["L0_ffn_gate_b"]), (128, E)))
    glw = np.asarray(inputs["L0_attn_gate_local_w"])
    grw = np.asarray(inputs["L0_attn_gate_remote_w"])
    glbd = np.zeros((D, NH), dtype=np.float32)
    grbd = np.zeros((D, NH), dtype=np.float32)
    for h in range(NH):
        glbd[64 * h:64 * h + 64, h] = glw[h]
        grbd[64 * h:64 * h + 64, h] = grw[h]
    iden = np.eye(128, dtype=np.float32)
    tri = np.zeros((128, 384), dtype=np.float32)
    kk = np.arange(128)[:, None]
    qq = np.arange(128)[None, :]
    tri[:, 0:128] = (qq >= kk).astype(np.float32)
    tri[:, 128:256] = 1.0
    tri[:, 256:384] = (kk > qq).astype(np.float32)
    psel = np.zeros((128, 256), dtype=np.float32)
    psel[:, 128] = 1.0

    per_core = []
    for c in range(NCORES):
        lo = T * c
        xT_own = np.zeros((D, T + 256), dtype=np.float32)
        if c == 0:
            xT_own[:, 256:] = xTf[:, 0:T]
        else:
            xT_own[:] = xTf[:, lo - 256:lo + T]
        lm = np.zeros((128, 16 * NL), dtype=np.float32)
        for gst in range(16):
            cmax = 8 * c + gst // 2
            lm[:, gst * NL + cmax + 1:(gst + 1) * NL] = NEG
        halov = np.full((128, 1), 0.0 if c == 0 else 1.0, dtype=np.float32)
        per_core.append(dict(
            xT=np.ascontiguousarray(xT_own),
            x_own=np.ascontiguousarray(x[lo:lo + T]),
            x_all=x,
            wq=np.asarray(inputs["L0_attn_wq"]),
            wk=np.asarray(inputs["L0_attn_wk"]),
            wv=np.asarray(inputs["L0_attn_wv"]),
            wgq=np.asarray(inputs["L0_attn_wgq"]),
            wgk=np.asarray(inputs["L0_attn_wgk"]),
            wgv=np.asarray(inputs["L0_attn_wgv"]),
            r_out_w=np.asarray(inputs["L0_attn_r_out_w"]),
            w_out=np.asarray(inputs["L0_attn_w_out"]),
            expert_wT=expert_wT,
            gate_wT=gate_wT,
            gate_b_t=gate_b_t,
            expert_b=np.asarray(inputs["L0_ffn_expert_b"]),
            glw_bd=glbd,
            grw_bd=grbd,
            iden_i=iden,
            trimask_i=tri,
            trihalo_i=tri * (0.0 if c == 0 else 1.0),
            poolsel_i=psel,
            lmask_i=lm,
            halov_i=halov,
        ))
    return per_core


def kernel(**inputs):
    per_core = _host_inputs(inputs)
    nc = build_nc(stage="full")
    res = run_bass_kernel_spmd(nc, per_core, core_ids=list(range(NCORES)))
    out = np.concatenate([res.results[c]["out"] for c in range(NCORES)], axis=0)
    return out.reshape(1, S, D)


# version 14
# speedup vs baseline: 1.2098x; 1.0271x over previous
import contextlib
import numpy as np

import concourse.bass as bass
import concourse.mybir as mybir
from concourse.bass_utils import run_bass_kernel_spmd

F32 = mybir.dt.float32
BF16 = mybir.dt.bfloat16
AX = mybir.AxisListType.X
ALU = mybir.AluOpType
ACT = mybir.ActivationFunctionType

NCORES = 8
S = 16384
D = 1024
T = S // NCORES
G = 1024
NG = T // G
NH = 16
HD = 64
HP = NH // 2
LH = 8
LHD = 64
LD = 512
NL = 64
W = 256
ST = G // 128
CH = (G + 256) // 128
E = 8
NEG = -1.0e30


class Sem:
    def __init__(self, es, nc, name):
        self.h = es.enter_context(nc.semaphore(name))
        self.n = 0


class Kb:
    def __init__(self, nc, es):
        self.nc = nc
        self.steps = []
        self.s_gd = Sem(es, nc, "s_gd")
        self.s_pe = Sem(es, nc, "s_pe")
        self.s_sc = Sem(es, nc, "s_sc")
        self.s_vx = Sem(es, nc, "s_vx")
        self.s_sd = Sem(es, nc, "s_sd")
        self.s_out = Sem(es, nc, "s_out")
        self.pe = self._mk("tensor", self.s_pe, 1)
        self.sc = self._mk("scalar", self.s_sc, 1)
        self.vx = self._mk("vector", self.s_vx, 1)
        self.gd = self._mk("gpsimd", self.s_gd, 16)
        self.sy = self._mk("sync", self.s_sd, 16)
        self.so = self._mk("sync", self.s_out, 16)

    def _mk(self, name, sem, bump):
        def add(fn, wait=None, inc=True):
            if inc:
                sem.n += bump
            tgt = sem.n if inc else None
            wl = [wv for wv in (wait or []) if wv is not None and wv[1] is not None]
            def step(eng, fn=fn, wl=wl, tgt=tgt):
                for (s, v) in wl:
                    eng.wait_ge(s.h, v)
                last = fn(eng)
                if tgt is not None:
                    assert last is not None, f"{name} step returned None with inc"
                    last.then_inc(sem.h, bump)
            self.steps.append((name, step))
            return tgt
        return add

    def gdma(self, dst, src, wait=None):
        return self.gd(lambda g, dst=dst, src=src: g.dma_start(out=dst, in_=src), wait=wait)

    def sdma(self, dst, src, wait=None):
        return self.sy(lambda sy, dst=dst, src=src: sy.dma_start(out=dst, in_=src), wait=wait)

    def odma(self, dst, src, wait=None):
        return self.so(lambda sy, dst=dst, src=src: sy.dma_start(out=dst, in_=src), wait=wait)

    def godma(self, dst, src, wait=None):
        self.s_out.n += 16
        tgt = self.s_out.n
        wl = [wv for wv in (wait or []) if wv is not None and wv[1] is not None]
        def step(eng, dst=dst, src=src, wl=wl, tgt=tgt):
            for (s, v) in wl:
                eng.wait_ge(s.h, v)
            eng.dma_start(out=dst, in_=src).then_inc(self.s_out.h, 16)
        self.steps.append(("gpsimd", step))
        return tgt

    def final_wait(self):
        v = self.s_out.n
        self.steps.append(("sync", lambda sy, v=v: sy.wait_ge(self.s_out.h, v)))

    def emit(self, blk):
        for name in ("sync", "gpsimd", "tensor", "scalar", "vector"):
            fns = [f for (e, f) in self.steps if e == name]
            if not fns:
                continue
            def body(eng, fns=fns):
                for f in fns:
                    f(eng)
            getattr(blk, name)(body)


def ap_off(t, extra_off, pattern):
    a = t.ap() if hasattr(t, "ap") else t
    return bass.AP(a.tensor, a.offset + extra_off, pattern)


def build_nc(stage="full", debug=()):
    nc = bass.Bass("TRN2", target_bir_lowering=False)
    es = contextlib.ExitStack()

    di = lambda n, shp: nc.dram_tensor(n, shp, F32, kind="ExternalInput").ap()
    xT = di("xT", [D, T + 256])
    x_own = di("x_own", [T, D])
    x_all = di("x_all", [S, D])
    wq_d = di("wq", [D, D]); wk_d = di("wk", [D, D]); wv_d = di("wv", [D, D])
    wgq_d = di("wgq", [D, LD]); wgk_d = di("wgk", [D, LD]); wgv_d = di("wgv", [D, LD])
    rout_d = di("r_out_w", [LD, D])
    wout_d = di("w_out", [D, D])
    wet_d = di("expert_wT", [E, D, D])
    gwt_d = di("gate_wT", [D, E])
    gbt_d = di("gate_b_t", [128, E])
    eb_d = di("expert_b", [E, D])
    glbd_d = di("glw_bd", [D, NH])
    grbd_d = di("grw_bd", [D, NH])
    iden_d = di("iden_i", [128, 128])
    tri_d = di("trimask_i", [128, 384])
    trih_d = di("trihalo_i", [128, 384])
    psel_d = di("poolsel_i", [128, 256])
    lmask_d = di("lmask_i", [128, 16 * NL])
    halov_d = di("halov_i", [128, 1])
    out_d = nc.dram_tensor("out", [T, D], F32, kind="ExternalOutput").ap()
    dbg = {}
    for name, shp in debug:
        dbg[name] = nc.dram_tensor("dbg_" + name, list(shp), F32, kind="ExternalOutput").ap()

    sbt = lambda n, shp, dty: es.enter_context(nc.sbuf_tensor(n, shp, dty))
    iden = sbt("iden", [128, 128], BF16)
    gwT = sbt("gwT", [128, 8, E], BF16)
    gbt = sbt("gbt", [128, E], F32)
    ebT = sbt("ebT", [E, D], BF16)
    sc_comb = sbt("sc_comb", [128, 8], F32)
    rd4 = sbt("rd4", [128, 8], F32)
    s1 = sbt("s1", [128, 2], F32); s2 = sbt("s2", [128, 2], F32)
    mvv = sbt("mvv", [128, 2], F32)
    varv = sbt("varv", [128, 2], F32)
    lnl = sbt("lnl", [128, 2], F32)
    rstd = sbt("rstd", [128, 2], F32); nmr = sbt("nmr", [128, 2], F32)
    gsb = sbt("gsb", [128, E], F32)
    gsb2 = sbt("gsb2", [128, E], F32)
    gnm = sbt("gnm", [128, 1], F32)
    gden = sbt("gden", [128, 1], F32)
    grd = sbt("grd", [128, 1], F32)
    gmx = sbt("gmx", [128, E], F32)
    gind = sbt("gind", [128, E], F32)
    w_tm = sbt("w_tm", [128, ST, E], F32)
    wbf = sbt("wbf", [128, 2, E], BF16)
    w_fm = sbt("w_fm", [8, ST, 128], BF16)
    attn_g = [sbt(f"attn{g}", [128, ST, D], BF16) for g in range(NG)]
    wbufB = sbt("wbufB", [128, 8, D], BF16)
    es1 = contextlib.ExitStack()
    sb1 = lambda n, shp, dty: es1.enter_context(nc.sbuf_tensor(n, shp, dty))
    trimask = sb1("trimask", [128, 384], BF16)
    trihalo = sb1("trihalo", [128, 384], BF16)
    poolsel = sb1("poolsel", [128, 256], BF16)
    glwbd = sb1("glwbd", [128, 8, NH], BF16)
    grwbd = sb1("grwbd", [128, 8, NH], BF16)
    lmaskb = sb1("lmaskb", [128, 16 * NL], BF16)
    halov = sb1("halov", [128, 1], F32)
    pooled_tm = sb1("pooled_tm", [NL, D], BF16)
    pooled_fm = sb1("pooled_fm", [128, 8, NL], BF16)
    gk_fm = sb1("gk_fm", [128, 4, NL], BF16)
    gv_fm = sb1("gv_fm", [128, 4, NL], BF16)
    gv_tm = sb1("gv_tm", [128, 4, NL], BF16)
    lp_sb = sb1("lp_sb", [128, 2, 8, NL], BF16)
    lpt2 = sb1("lpt2", [128, 2, 4, 128], BF16)
    combt = sb1("combt", [128, 512], BF16)
    lden = sb1("lden", [128, 2, 8], F32)
    lrd = sb1("lrd", [128, 2, 8], F32)
    gl_tm = sb1("gl_tm", [128, ST, NH], F32)
    gr_tm = [sb1(f"gr_tm{g}", [128, ST, NH], F32) for g in range(NG)]
    glfm = sb1("glfm", [16, G], BF16)
    sige = sb1("sige", [128, 128], F32)
    pbt = sb1("pbt", [128, 6, 2, 384], BF16)
    rem_fm = sb1("rem_fm", [128, 2, 4, 128], BF16)
    wbufA = sb1("wbufA", [128, 8, D], BF16)
    gq_g = [sb1(f"gq{g}", [128, 4, G], BF16) for g in range(NG)]
    q_sb = sb1("q_sb", [128, HP, G], BF16)
    k_sb = sb1("k_sb", [128, HP, G + 256], BF16)
    v_sb = sb1("v_sb", [128, CH, NH, 65], BF16)
    xg = sb1("xg", [128, 8, G + 256], BF16)
    poolb = sb1("poolb", [128, 8, D], BF16)
    x2tm = x2fm = x3acc = fmbuf = x2bf = sq_scr = outb = xo_buf = None
    wer = x2pre = None

    ps = [es.enter_context(nc.psum_tensor(f"ps{i}", [128, 512], F32)) for i in range(6)]
    ps_pool = es.enter_context(nc.psum_tensor("ps_pool", [128, 1024], F32))

    K = Kb(nc, es)
    SGD, SPE, SSC, SVX, SSD, SOUT = K.s_gd, K.s_pe, K.s_sc, K.s_vx, K.s_sd, K.s_out
    bank_gate = [None] * 6
    pspool_gate = [None, None]

    def dump(name, src_ap, wait):
        if name in dbg:
            return K.godma(dbg[name], src_ap, wait=wait)
        return None

    rg = lambda d: d.rearrange("(a p) n -> p a n", p=128)
    K.gdma(iden[:], iden_d)
    K.gdma(trimask[:], tri_d)
    K.gdma(trihalo[:], trih_d)
    K.gdma(poolsel[:], psel_d)
    K.gdma(glwbd[:], rg(glbd_d))
    K.gdma(grwbd[:], rg(grbd_d))
    K.gdma(gwT[:], rg(gwt_d))
    K.gdma(gbt[:], gbt_d)
    K.gdma(ebT[:], eb_d)
    K.gdma(lmaskb[:], lmask_d)
    v_const = K.gdma(halov[:], halov_d)

    v_xg = [None, None]
    v_wq = [None, None]; v_wk = [None, None]; v_wv = [None, None]; v_wgq = [None, None]
    v_xg[0] = K.gdma(xg[:], rg(xT)[:, :, 0:G + 256])
    v_wq[0] = K.gdma(wbufA[:], rg(wq_d))
    v_wk[0] = K.gdma(wbufB[:], rg(wk_d))

    NPIECE = S // 128
    pool_dma_v = []
    pool_pe_v = []

    def _pool_dma1():
        i = len(pool_dma_v)
        w = [(SPE, pool_pe_v[i - 8])] if i >= 8 else None
        v = K.gdma(poolb[:, i % 8], x_all[128 * i:128 * i + 128, :], wait=w)
        pool_dma_v.append(v)

    def pool_step(n):
        for _ in range(n):
            i, j = len(pool_dma_v), len(pool_pe_v)
            if i >= NPIECE and j >= NPIECE:
                return
            if i < NPIECE and i <= j + 7:
                _pool_dma1()
            else:
                _pool_mm1()

    def _pool_mm1():
            i = len(pool_pe_v)
            if i >= NPIECE or i >= len(pool_dma_v):
                return
            def mm(te, i=i):
                last = None
                p = i // 2
                for hf in range(2):
                    last = te.matmul(ps_pool[:, 512 * hf:512 * hf + 512],
                                     poolsel[:, 128 - p:256 - p],
                                     poolb[:, i % 8, 512 * hf:512 * hf + 512],
                                     start=(i == 0),
                                     stop=(i == NPIECE - 1 and hf == 1))
                return last
            v = K.pe(mm, wait=[(SGD, pool_dma_v[i])])
            pool_pe_v.append(v)

    def proj_fm(dst_fn, wsb, w_v, xg_lo, xg_n, nchunk, v_xg_cur, wfree=1024):
        nblk = (xg_n + 511) // 512
        last_pe = last_vx = None
        wflat = wsb.ap().rearrange("p a n -> p (a n)")
        for i in range(nchunk):
            for nb in range(nblk):
                n0 = 512 * nb
                nn = min(xg_n, n0 + 512) - n0
                bi = (i * nblk + nb) % 2
                bank = ps[bi]
                def mm(te, i=i, n0=n0, nn=nn, bank=bank, wflat=wflat, wfree=wfree):
                    last = None
                    for hc in range(8):
                        last = te.matmul(bank[:, 0:nn],
                                         wflat[:, wfree * hc + 128 * i:wfree * hc + 128 * i + 128],
                                         xg[:, hc, xg_lo + n0:xg_lo + n0 + nn],
                                         start=(hc == 0), stop=(hc == 7))
                    return last
                last_pe = K.pe(mm, wait=[(SGD, max(w_v, v_xg_cur)), bank_gate[bi]])
                def ev(vec, i=i, n0=n0, nn=nn, bank=bank, dst_fn=dst_fn):
                    return vec.tensor_copy(dst_fn(i, n0, nn), bank[:, 0:nn])
                last_vx = K.vx(ev, wait=[(SPE, last_pe)])
                bank_gate[bi] = (SVX, last_vx)
        return last_pe, last_vx

    def emit_glgr(g, qvx_gate):
        last = None
        for (wsb_, dst) in ((glwbd, gl_tm), (grwbd, gr_tm[g])):
            ev = None
            for nb in range(2):
                bank = ps[4]
                def mm(te, wsb_=wsb_, nb=nb, bank=bank):
                    l = None
                    for hpp in range(HP):
                        l = te.matmul(bank[0:16, 0:512], wsb_[:, hpp, :],
                                      q_sb[:, hpp, 512 * nb:512 * nb + 512],
                                      start=(hpp == 0), stop=(hpp == HP - 1))
                    return l
                pv = K.pe(mm, wait=[(SVX, qvx_gate), (SGD, v_const), bank_gate[4]])
                ev = K.vx(lambda vec, nb=nb, bank=bank: vec.tensor_copy(
                    glfm[:, 512 * nb:512 * nb + 512], bank[0:16, 0:512]),
                    wait=[(SPE, pv)])
                bank_gate[4] = (SVX, ev)
            bank2 = ps[5]
            def mmt(te, bank2=bank2):
                l = None
                for st in range(ST):
                    l = te.matmul(bank2[:, 16 * st:16 * st + 16],
                                  glfm[:, 128 * st:128 * st + 128],
                                  iden[0:16, 0:16], start=True, stop=True)
                return l
            pv2 = K.pe(mmt, wait=[(SVX, ev), bank_gate[5]])
            c1 = K.sc(lambda s, bank2=bank2: s.activation(
                sige[:], bank2[:, 0:128], ACT.Exp, scale=-1.0),
                wait=[(SPE, pv2)])
            bank_gate[5] = (SSC, c1)
            sa = K.vx(lambda vec: vec.tensor_scalar_add(sige[:], sige[:], 1.0),
                      wait=[(SSC, c1)])
            last = K.vx(lambda vec, dst=dst: vec.reciprocal(
                dst.ap().rearrange("p a n -> p (a n)"), sige[:]), wait=[(SVX, sa)])
        return last

    xg_free = [None, None]

    def emit_projections(g):
        res = {}
        vxc = v_xg[g]
        qpe, qvx = proj_fm(lambda i, n0, nn: q_sb.ap()[:, i, n0:n0 + nn],
                           wbufA, v_wq[g], 256, G, HP, vxc)
        pool_step(8 if g == 0 else 20)
        glv = emit_glgr(g, qvx)
        v_wv[g] = K.gdma(wbufA[:], rg(wv_d), wait=[(SPE, qpe)])
        kpe, kvx = proj_fm(lambda i, n0, nn: k_sb.ap()[:, i, n0:n0 + nn],
                           wbufB, v_wk[g], 0, G + 256, HP, vxc)
        pool_step(8 if g == 0 else 20)
        v_wgq[g] = K.gdma(wbufB[:, 0:4], rg(wgq_d), wait=[(SPE, kpe)])
        vvx = vpe = None
        for sc_ in range(CH):
            for oh in range(2):
                bi = (sc_ * 2 + oh) % 2
                bank = ps[bi]
                def mm(te, sc_=sc_, oh=oh, bank=bank):
                    last = None
                    for hc in range(8):
                        last = te.matmul(bank[:, 0:512],
                                         xg[:, hc, 128 * sc_:128 * sc_ + 128],
                                         wbufA[:, hc, 512 * oh:512 * oh + 512],
                                         start=(hc == 0), stop=(hc == 7))
                    return last
                vpe = K.pe(mm, wait=[(SGD, max(v_wv[g], vxc)), bank_gate[bi]])
                def ev(vec, sc_=sc_, oh=oh, bank=bank):
                    dst = v_sb.ap()[:, sc_, 8 * oh:8 * oh + 8, 0:64]
                    src = bass.AP(bank.ap().tensor, bank.ap().offset,
                                  [[512, 128], [64, 8], [1, 64]])
                    return vec.tensor_copy(dst, src)
                vvx = K.vx(ev, wait=[(SPE, vpe)])
                bank_gate[bi] = (SVX, vvx)
        vone = K.vx(lambda vec: vec.memset(v_sb.ap()[:, :, :, 64:65], 1.0)) if g == 0 else None
        pool_step(8 if g == 0 else 20)
        res["wA_free_pe"] = vpe
        gqpe, gqvx = proj_fm(lambda i, n0, nn: gq_g[g].ap()[:, i, n0:n0 + nn],
                             wbufB, v_wgq[g], 256, G, 4, vxc, wfree=512)
        pool_step(12 if g == 0 else 32)
        xg_free[g] = (SPE, gqpe)
        res.update(qvx=qvx, kvx=kvx, vvx=vvx, vone=vone, glv=glv,
                   gqvx=gqvx, gqpe=gqpe, qpe=qpe)
        return res

    def emit_local(g, pre_gates):
        attn = attn_g[g]
        first = True
        slot_pv = {}
        units = []
        for hpp in range(HP):
            for half in range(2):
                tlo_h, thi_h = 4 * half, 4 * half + 4
                for ca in range(tlo_h, thi_h + 2):
                    tlo = max(tlo_h, ca - 2)
                    thi = min(thi_h, ca + 1)
                    if tlo < thi:
                        units.append((hpp, half, ca, tlo, thi))
        uidx = {(u[0], u[1], u[2]): i for i, u in enumerate(units)}
        pvjobs = []
        for hpp in range(HP):
            for half in range(2):
                for t in range(4 * half, 4 * half + 4):
                    pvjobs.append((hpp, half, t, uidx[(hpp, half, t + 2)]))
        pvjobs.sort(key=lambda j: j[3])
        LAG = 4
        mask_val = {}
        span_of = {}
        slot_of = {}
        pv_emitted = 0
        finv = None
        fin_gate = None
        last_pv_of_half = {}

        def emit_pv(j):
            nonlocal finv, fin_gate
            (hpp, half, t, _) = pvjobs[j]
            parts = tuple((t + jj, slot_of[(hpp, half, t + jj)],
                           t - span_of[(hpp, half, t + jj)][0]) for jj in range(3))
            def mmpv(te, t=t, hpp=hpp, half=half, parts=parts):
                last = None
                for jj, (cj, slj, blk) in enumerate(parts):
                    for par in range(2):
                        head = 2 * hpp + par
                        acc = ps[par][:, 66 * (t - 4 * half):66 * (t - 4 * half) + 65]
                        last = te.matmul(acc, pbt[:, slj, par, 128 * blk:128 * blk + 128],
                                         v_sb[:, cj, head, :],
                                         start=(jj == 0), stop=(jj == 2))
                return last
            wpv = [(SVX, mask_val[(hpp, half, t + 2)])]
            if t == 4 * half:
                wpv += [bank_gate[0], bank_gate[1]]
            pvv = K.pe(mmpv, wait=wpv)
            for jj in range(3):
                slot_pv[slot_of[(hpp, half, t + jj)]] = (SPE, pvv)
            if t == 4 * half + 3:
                def fin_a(vec):
                    vec.reciprocal(rd4[:, 0:4], ap_off(ps[0], 64, [[512, 128], [66, 4]]))
                    return vec.reciprocal(rd4[:, 4:8], ap_off(ps[1], 64, [[512, 128], [66, 4]]))
                fa = K.vx(fin_a, wait=[(SPE, pvv)])
                def fin_b(vec, half=half, hpp=hpp):
                    r = None
                    for par in range(2):
                        head = 2 * hpp + par
                        glc = ap_off(gl_tm, (4 * half) * NH + head, [[ST * NH, 128], [NH, 4]])
                        r = vec.tensor_tensor(sc_comb[:, 4 * par:4 * par + 4],
                                              rd4[:, 4 * par:4 * par + 4], glc, ALU.mult)
                    return r
                fb = K.vx(fin_b, wait=[(SVX, fa)])
                def fin(vec, half=half, hpp=hpp, attn=attn):
                    r = None
                    for par in range(2):
                        head = 2 * hpp + par
                        for t4 in range(4):
                            tt = 4 * half + t4
                            r = vec.tensor_scalar_mul(
                                attn[:, tt, 64 * head:64 * head + 64],
                                ps[par][:, 66 * t4:66 * t4 + 64],
                                sc_comb[:, 4 * par + t4:4 * par + t4 + 1])
                    return r
                finv = K.vx(fin, wait=[(SVX, fb)])
                bank_gate[0] = (SVX, finv)
                bank_gate[1] = (SVX, finv)

        for i, (hpp, half, ca, tlo, thi) in enumerate(units):
            while pv_emitted < len(pvjobs) and pvjobs[pv_emitted][3] <= i - LAG:
                emit_pv(pv_emitted)
                pv_emitted += 1
            nn = 128 * (thi - tlo)
            span_of[(hpp, half, ca)] = (tlo, thi)
            sl = i % 6
            slot_of[(hpp, half, ca)] = sl
            be, bo = 2 + (i % 2) * 2, 3 + (i % 2) * 2
            def mmqk(te, hpp=hpp, ca=ca, tlo=tlo, nn=nn, be=be, bo=bo):
                te.matmul(ps[be][:, 0:nn],
                          k_sb[0:64, hpp, 128 * ca:128 * ca + 128],
                          q_sb[0:64, hpp, 128 * tlo:128 * tlo + nn],
                          start=True, stop=True)
                return te.matmul(ps[bo][:, 0:nn],
                                 k_sb[64:128, hpp, 128 * ca:128 * ca + 128],
                                 q_sb[64:128, hpp, 128 * tlo:128 * tlo + nn],
                                 start=True, stop=True, tile_position=(64, 0))
            wt = list(pre_gates) if first else []
            first = False
            wt += [bank_gate[be], bank_gate[bo]]
            qk = K.pe(mmqk, wait=wt)
            wexp = [(SPE, qk), slot_pv.get(sl)]
            def ex(sc_, sl=sl, nn=nn, be=be, bo=bo):
                sc_.activation(pbt[:, sl, 0, 0:nn], ps[be][:, 0:nn], ACT.Exp, scale=0.125)
                return sc_.activation(pbt[:, sl, 1, 0:nn], ps[bo][:, 0:nn],
                                      ACT.Exp, scale=0.125)
            exv = K.sc(ex, wait=wexp)
            bank_gate[be] = (SSC, exv)
            bank_gate[bo] = (SSC, exv)
            rlo = tlo + 2 - ca
            msk = trihalo if (g == 0 and ca < 2) else trimask
            def mk(vec, sl=sl, nn=nn, rlo=rlo, msk=msk):
                vec.tensor_tensor(pbt[:, sl, 0, 0:nn], pbt[:, sl, 0, 0:nn],
                                  msk[:, 128 * rlo:128 * rlo + nn], ALU.mult)
                return vec.tensor_tensor(pbt[:, sl, 1, 0:nn], pbt[:, sl, 1, 0:nn],
                                         msk[:, 128 * rlo:128 * rlo + nn], ALU.mult)
            mkv = K.vx(mk, wait=[(SSC, exv)])
            mask_val[(hpp, half, ca)] = mkv
            if i % 6 == 0:
                pool_step(3)
        while pv_emitted < len(pvjobs):
            emit_pv(pv_emitted)
            pv_emitted += 1
        return finv

    def emit_pooled_finish(v_wgk, v_wgv):
        pe_last = pool_pe_v[-1]
        c1 = K.sc(lambda s: s.activation(pooled_tm[:], ps_pool[0:64, :], ACT.Copy,
                                         scale=1.0 / 256), wait=[(SPE, pe_last)])
        pspool_gate[0] = (SSC, c1)
        pspool_gate[1] = (SSC, c1)
        def mmt(te):
            l = None
            for hc in range(8):
                l = te.matmul(ps[0][:, 64 * hc:64 * hc + 64],
                              pooled_tm[:, 128 * hc:128 * hc + 128],
                              iden[0:64, 0:64], start=True, stop=True)
            return l
        p1 = K.pe(mmt, wait=[(SSC, c1), bank_gate[0]])
        vfm = K.vx(lambda vec: vec.tensor_copy(
            pooled_fm.ap().rearrange("p a n -> p (a n)"), ps[0][:, 0:512]),
            wait=[(SPE, p1)])
        bank_gate[0] = (SVX, vfm)
        evs = []
        pvl = None
        for (wv_, wsb_, dst, bi) in ((v_wgk, wbufA, gk_fm, 1), (v_wgv, wbufB, gv_fm, 2)):
            wflat = wsb_.ap().rearrange("p a n -> p (a n)")
            def mm(te, wflat=wflat, bi=bi):
                l = None
                for lc in range(4):
                    for hc in range(8):
                        l = te.matmul(ps[bi][:, 64 * lc:64 * lc + 64],
                                      wflat[:, 512 * hc + 128 * lc:512 * hc + 128 * lc + 128],
                                      pooled_fm[:, hc, :],
                                      start=(hc == 0), stop=(hc == 7))
                return l
            pvl = K.pe(mm, wait=[(SGD, wv_), (SVX, vfm), bank_gate[bi]])
            ev = K.vx(lambda vec, dst=dst, bi=bi: vec.tensor_copy(
                dst.ap().rearrange("p a n -> p (a n)"), ps[bi][:, 0:256]),
                wait=[(SPE, pvl)])
            bank_gate[bi] = (SVX, ev)
            evs.append(ev)
        tvs = None
        for lc in range(4):
            def mmtv(te, lc=lc):
                te.matmul(ps[3][0:64, 0:64], gv_fm[0:64, lc, :], iden[0:64, 0:64],
                          start=True, stop=True)
                return te.matmul(ps[4][64:128, 0:64], gv_fm[64:128, lc, :],
                                 iden[64:128, 64:128], start=True, stop=True,
                                 tile_position=(64, 64))
            wtv = [(SVX, evs[-1]), bank_gate[3], bank_gate[4]]
            ptv = K.pe(mmtv, wait=wtv)
            def evtv(vec, lc=lc):
                vec.tensor_copy(gv_tm[0:64, lc, :], ps[3][0:64, 0:64])
                return vec.tensor_copy(gv_tm[64:128, lc, :], ps[4][64:128, 0:64])
            tvs = K.vx(evtv, wait=[(SPE, ptv)])
            bank_gate[3] = (SVX, tvs)
            bank_gate[4] = (SVX, tvs)
        return tvs, pvl

    def emit_latent(g, gkgv_v, gq_gate, v_rout, extra_gate=None, only_st=None):
        attn = attn_g[g]
        comb_gate = [None]
        lp_gate = {}
        lpt_gate = {}
        rem_gate = {}
        st_state = {}
        nst = ST if only_st is None else only_st

        def s1_lsc(st):
            b = st % 2
            gst0 = ST * g + st
            def mml(te, st=st, g=g, gst0=gst0):
                last = None
                onesrow = trimask[0:1, 128:256]
                lrow = lmaskb.ap()[0:1, NL * gst0:NL * gst0 + NL]
                for lp_ in range(4):
                    te.matmul(ps[2][:, 64 * lp_:64 * lp_ + 64],
                              gq_g[g][0:64, lp_, 128 * st:128 * st + 128],
                              gk_fm[0:64, lp_, :], start=True, stop=False)
                    te.matmul(ps[3][:, 64 * lp_:64 * lp_ + 64],
                              gq_g[g][64:128, lp_, 128 * st:128 * st + 128],
                              gk_fm[64:128, lp_, :], start=True, stop=False,
                              tile_position=(64, 0))
                    te.matmul(ps[2][:, 64 * lp_:64 * lp_ + 64], onesrow, lrow,
                              start=False, stop=True)
                    last = te.matmul(ps[3][:, 64 * lp_:64 * lp_ + 64], onesrow, lrow,
                                     start=False, stop=True)
                return last
            wt = [bank_gate[2], bank_gate[3]]
            if st == 0 and g == 0:
                wt += [(SVX, gkgv_v)]
            if st == 0:
                wt += [gq_gate]
                if extra_gate is not None:
                    wt.append(extra_gate)
            return K.pe(mml, wait=wt)

        def s2_softmax(st, p1):
            b = st % 2
            wex = [(SPE, p1), lp_gate.get(b)]
            def ex(s, b=b):
                r = None
                for lp_ in range(4):
                    s.activation(lp_sb[:, b, 2 * lp_, :], ps[2][:, 64 * lp_:64 * lp_ + 64],
                                 ACT.Exp, scale=0.125,
                                 accum_out=lden[:, b, 2 * lp_:2 * lp_ + 1])
                    r = s.activation(lp_sb[:, b, 2 * lp_ + 1, :], ps[3][:, 64 * lp_:64 * lp_ + 64],
                                     ACT.Exp, scale=0.125,
                                     accum_out=lden[:, b, 2 * lp_ + 1:2 * lp_ + 2])
                return r
            c1 = K.sc(ex, wait=wex)
            bank_gate[2] = (SSC, c1)
            bank_gate[3] = (SSC, c1)
            vr = K.vx(lambda vec, b=b: vec.reciprocal(lrd[:, b, :], lden[:, b, :]),
                      wait=[(SSC, c1)])
            def nrm(vec, b=b):
                r = None
                for lh in range(8):
                    r = vec.tensor_scalar_mul(lp_sb[:, b, lh, :], lp_sb[:, b, lh, :],
                                              lrd[:, b, lh:lh + 1])
                return r
            return K.vx(nrm, wait=[(SVX, vr)])

        def s3_transpose(st, v2):
            b = st % 2
            def mmt(te, b=b):
                last = None
                for lp_ in range(4):
                    te.matmul(ps[4][0:64, 128 * lp_:128 * lp_ + 128],
                              lp_sb[:, b, 2 * lp_, :], iden[:], start=True, stop=True)
                    last = te.matmul(ps[5][64:128, 128 * lp_:128 * lp_ + 128],
                                     lp_sb[:, b, 2 * lp_ + 1, :], iden[:],
                                     start=True, stop=True, tile_position=(0, 64))
                return last
            p2 = K.pe(mmt, wait=[(SVX, v2), bank_gate[4], bank_gate[5]])
            lp_gate[b] = (SPE, p2)
            def evt(sc, b=b):
                sc.activation(lpt2.ap()[0:64, b].rearrange("p a n -> p (a n)"),
                              ps[4][0:64, 0:512], ACT.Copy)
                return sc.activation(lpt2.ap()[64:128, b].rearrange("p a n -> p (a n)"),
                                     ps[5][64:128, 0:512], ACT.Copy)
            v3 = K.sc(evt, wait=[(SPE, p2), lpt_gate.get(b)])
            bank_gate[4] = (SSC, v3)
            bank_gate[5] = (SSC, v3)
            return v3

        def s5_pv(st, v3):
            b = st % 2
            def mmp(te, b=b):
                last = None
                for lp_ in range(4):
                    te.matmul(ps[0][0:64, 128 * lp_:128 * lp_ + 128],
                              gv_tm[0:64, lp_, :], lpt2[0:64, b, lp_, :],
                              start=True, stop=True)
                    last = te.matmul(ps[1][64:128, 128 * lp_:128 * lp_ + 128],
                                     gv_tm[64:128, lp_, :], lpt2[64:128, b, lp_, :],
                                     start=True, stop=True, tile_position=(64, 64))
                return last
            p3 = K.pe(mmp, wait=[(SSC, v3), bank_gate[0], bank_gate[1]])
            lpt_gate[b] = (SPE, p3)
            def evp(sc, b=b):
                sc.activation(rem_fm.ap()[0:64, b].rearrange("p a n -> p (a n)"),
                              ps[0][0:64, 0:512], ACT.Copy)
                return sc.activation(rem_fm.ap()[64:128, b].rearrange("p a n -> p (a n)"),
                                     ps[1][64:128, 0:512], ACT.Copy)
            v4 = K.sc(evp, wait=[(SPE, p3), rem_gate.get(b)])
            bank_gate[0] = (SSC, v4)
            bank_gate[1] = (SSC, v4)
            return v4

        def s7_rout(st, v4):
            b = st % 2
            prs = []
            for oh in range(2):
                def mmr(te, oh=oh, b=b):
                    l = None
                    for lc in range(4):
                        l = te.matmul(ps[oh][:, 0:512], rem_fm[:, b, lc, :],
                                      wbufA[:, 4 + lc, 512 * oh:512 * oh + 512],
                                      start=(lc == 0), stop=(lc == 3))
                    return l
                wt = [(SSC, v4), bank_gate[oh]]
                if st == 0 and g == 0:
                    wt.append((SGD, v_rout))
                pr = K.pe(mmr, wait=wt)
                prs.append(pr)
            rem_gate[b] = (SPE, prs[-1])
            last = None
            for oh in range(2):
                def comb_a(vec, oh=oh, st=st, g=g):
                    grx = gr_tm[g].ap()[:, st, 8 * oh:8 * oh + 8].unsqueeze(2)
                    grx = grx.broadcast_to([128, 8, 64])
                    src = ps[oh][:, 0:512].rearrange("p (a n) -> p a n", n=64)
                    return vec.tensor_tensor(
                        combt.ap().rearrange("p (a n) -> p a n", n=64),
                        src, grx, ALU.mult)
                ca_ = K.vx(comb_a, wait=[(SPE, prs[-1]), comb_gate[0]])
                bank_gate[oh] = (SVX, ca_)
                def comb_b(vec, oh=oh, st=st, attn=attn):
                    return vec.tensor_tensor(
                        attn[:, st, 512 * oh:512 * oh + 512],
                        attn[:, st, 512 * oh:512 * oh + 512],
                        combt[:], ALU.add)
                cv = K.vx(comb_b, wait=[(SVX, ca_)])
                comb_gate[0] = (SVX, cv)
                last = cv
            return last

        last = None
        for st in range(nst + 1):
            if st >= 1:
                stp = st - 1
                v3 = s3_transpose(stp, st_state[stp])
                v4 = s5_pv(stp, v3)
            if st < nst:
                p1 = s1_lsc(st)
                v2 = s2_softmax(st, p1)
                st_state[st] = v2
            if st >= 1:
                last = s7_rout(st - 1, v4)
        return last

    ln_prev = {}

    def emit_ln(src_ap, wait_gates, dsts, b=0, dummy=None):
        sq = dummy if dummy is not None else sq_scr.ap()[:, b, :]
        wg = list(wait_gates)
        if b in ln_prev:
            wg.append(ln_prev[b])
        def sums(s):
            s.activation(sq, src_ap, ACT.Copy, accum_out=s1[:, b:b + 1])
            return s.activation(sq, src_ap, ACT.Square, accum_out=s2[:, b:b + 1])
        c1 = K.sc(sums, wait=wg)
        v1 = c1
        def stats_a(vec):
            vec.tensor_scalar_mul(mvv[:, b:b + 1], s1[:, b:b + 1], -1.0 / D)
            return vec.tensor_scalar_mul(varv[:, b:b + 1], s2[:, b:b + 1], 1.0 / D)
        v2a = K.vx(stats_a, wait=[(SSC, c1)])
        v2b = K.vx(lambda vec: vec.scalar_tensor_tensor(
            lnl[:, b:b + 1], mvv[:, b:b + 1], mvv[:, b:b + 1], varv[:, b:b + 1],
            op0=ALU.mult, op1=ALU.subtract), wait=[(SVX, v2a)])
        v2 = K.vx(lambda vec: vec.tensor_scalar(
            varv[:, b:b + 1], lnl[:, b:b + 1], -1.0, 1e-5,
            op0=ALU.mult, op1=ALU.add), wait=[(SVX, v2b)])
        c2 = K.sc(lambda s: s.activation(lnl[:, b:b + 1], varv[:, b:b + 1], ACT.Ln),
                  wait=[(SVX, v2)])
        c3 = K.sc(lambda s: s.activation(rstd[:, b:b + 1], lnl[:, b:b + 1], ACT.Exp,
                                         scale=-0.5), wait=[(SSC, c2)])
        v3 = K.vx(lambda vec: vec.tensor_tensor(nmr[:, b:b + 1], mvv[:, b:b + 1],
                                                rstd[:, b:b + 1], ALU.mult),
                  wait=[(SSC, c3)])
        def norm(vec):
            r = None
            for d_ in dsts:
                r = vec.tensor_scalar(d_, src_ap, rstd[:, b:b + 1], nmr[:, b:b + 1],
                                      op0=ALU.mult, op1=ALU.add)
            return r
        out = K.vx(norm, wait=[(SVX, v3)])
        ln_prev[b] = (SVX, out)
        return out

    xo_q = {}

    def xo_fetch(g, st):
        if st >= ST:
            return
        w = [xo_last.get(st % 3)]
        xo_q[(g, st)] = K.sdma(xo_buf[:, st % 3],
                               x_own[G * g + 128 * st:G * g + 128 * st + 128, :], wait=w)

    def make_wout(g, attn_done, v_wout, prev_x2_free, wout_banks=(2, 3)):
        attn = attn_g[g]
        ln_done = {}
        x2fm_ev = {}
        tr_prev = {}
        afm_gate = {}
        wb0, wb1 = wout_banks
        for st in range(3):
            if (g, st) not in xo_q:
                xo_fetch(g, st)

        tvs_of = {}

        def stage_A1(st):
            tvs = []
            for b2 in range(2):
                bank = ps[4 + b2]
                def mmt(te, st=st, b2=b2, bank=bank, attn=attn):
                    l = None
                    for fc4 in range(4):
                        fc = 4 * b2 + fc4
                        l = te.matmul(bank[:, 128 * fc4:128 * fc4 + 128],
                                      attn[:, st, 128 * fc:128 * fc + 128],
                                      iden[:], start=True, stop=True)
                    return l
                wt = [bank_gate[4 + b2], afm_gate.get(st % 2)]
                if st == 0:
                    wt.append((SVX, attn_done))
                    wt.append(prev_x2_free)
                ptv = K.pe(mmt, wait=wt)
                def evt(vec, st=st, b2=b2, bank=bank):
                    return vec.tensor_copy(afm_s.ap()[:, st % 2, 4 * b2:4 * b2 + 4, :],
                                           bank[:, 0:512])
                tv = K.vx(evt, wait=[(SPE, ptv)])
                bank_gate[4 + b2] = (SVX, tv)
                tvs.append(tv)
            tvs_of[st] = tvs

        def stage_A2(st):
            tvs = tvs_of[st]
            pw = None
            for oh in range(2):
                def mmw(te, st=st, oh=oh):
                    l = None
                    for fc in range(8):
                        l = te.matmul(ps_pool[:, 512 * oh:512 * oh + 512],
                                      afm_s[:, st % 2, fc, :],
                                      wbufB[:, fc, 512 * oh:512 * oh + 512],
                                      start=(fc == 0), stop=(fc == 7))
                    return l
                pw = K.pe(mmw, wait=[(SVX, tvs[1]), (SGD, v_wout), pspool_gate[oh]])
            afm_gate[st % 2] = (SPE, pw)
            def resid(vec, st=st, g=g):
                vec.tensor_tensor(x2pre[:, st % 2, 0:512], ps_pool[:, 0:512],
                                  xo_buf[:, st % 3, 0:512], ALU.add)
                return vec.tensor_tensor(x2pre[:, st % 2, 512:1024],
                                         ps_pool[:, 512:1024],
                                         xo_buf[:, st % 3, 512:1024], ALU.add)
            rv = K.vx(resid, wait=[(SPE, pw), (SSD, xo_q[(g, st)]),
                                   x2pre_gate.get(st % 2)])
            pspool_gate[0] = (SVX, rv)
            pspool_gate[1] = (SVX, rv)
            xo_last[st % 3] = (SVX, rv)
            xo_fetch(g, st + 3)
            return rv

        def stage_B(st, rv):
            wln = [(SVX, rv)]
            lnv = emit_ln(x2pre.ap()[:, st % 2, :], wln, [x2tm[:, st, :]],
                          b=st % 2, dummy=sq_scr[:])
            ln_done[st] = lnv
            x2pre_gate[st % 2] = (SVX, lnv)
            return lnv

        def stage_C(st):
            cb = K.vx(lambda vec, st=st: vec.tensor_copy(x2bf[:, st % 2, :],
                                                         x2tm[:, st, :]),
                      wait=[(SVX, ln_done[st]), tr_prev.get(st % 2)])
            for b2 in range(2):
                bank = ps[b2]
                def mmx(te, st=st, b2=b2, bank=bank):
                    l = None
                    for fc4 in range(4):
                        fc = 4 * b2 + fc4
                        l = te.matmul(bank[:, 128 * fc4:128 * fc4 + 128],
                                      x2bf[:, st % 2, 128 * fc:128 * fc + 128],
                                      iden[:], start=True, stop=True)
                    return l
                px = K.pe(mmx, wait=[(SVX, cb), bank_gate[b2]])
                tr_prev[st % 2] = (SPE, px)
                def evx(vec, st=st, b2=b2, bank=bank):
                    dst = bass.AP(fmbuf.ap().tensor,
                                  fmbuf.ap().offset + (4 * b2) * G + 128 * st,
                                  [[8 * G, 128], [G, 4], [1, 128]])
                    return vec.tensor_copy(dst, bank[:, 0:512])
                xv = K.vx(evx, wait=[(SPE, px)])
                bank_gate[b2] = (SVX, xv)
                x2fm_ev[(st, b2)] = xv

        a1_done = set()

        def step_pre(st):
            if st < ST and st not in a1_done:
                a1_done.add(st)
                stage_A1(st)

        def step_post(st):
            rv = stage_A2(st)
            step_pre(st + 1)
            stage_B(st, rv)

        def finish():
            for st in range(ST):
                stage_C(st)

        return step_pre, step_post, stage_C, ln_done, x2fm_ev

    def emit_wout_ln1(g, attn_done, v_wout, prev_x2_free):
        step_pre, step_post, stage_C, ln_done, x2fm_ev = make_wout(
            g, attn_done, v_wout, prev_x2_free)
        step_pre(0)
        for st in range(ST):
            step_post(st)
        return ln_done, x2fm_ev, stage_C

    def emit_ffn(g, x2fm_ev, ln_done, filler=None, stage_C_fn=None):
        cprev = None
        wtprev = [None, None]
        v4_of = {}
        v5_of = {}

        def emit_wt(st):
            b = st % 2
            def mmwt(te, st=st, b=b):
                return te.matmul(ps_pool[0:E, 512 * b:512 * b + 128], wbf[:, b, :],
                                 iden[:], start=True, stop=True)
            pwt = K.pe(mmwt, wait=[(SVX, v4_of[st]), pspool_gate[b]])
            wtprev[b] = (SPE, pwt)
            v5 = K.vx(lambda vec, st=st, b=b: vec.tensor_copy(
                w_fm[:, st, :], ps_pool[0:E, 512 * b:512 * b + 128]),
                wait=[(SPE, pwt)])
            pspool_gate[b] = (SVX, v5)
            v5_of[st] = v5

        for st in range(ST):
            if stage_C_fn is not None:
                stage_C_fn(st)
            b = st % 2
            def mmg(te, st=st, b=b):
                l = None
                for hc in range(8):
                    l = te.matmul(ps[4 + b][:, 0:E], fmbuf[:, hc, 128 * st:128 * st + 128],
                                  gwT[:, hc, :], start=(hc == 0), stop=(hc == 7))
                return l
            wt = [(SVX, x2fm_ev[(st, 1)]), bank_gate[4 + b]]
            pg = K.pe(mmg, wait=wt)
            v1 = K.vx(lambda vec, b=b: vec.tensor_tensor(gsb[:], ps[4 + b][:, 0:E],
                                                         gbt[:], ALU.add),
                      wait=[(SPE, pg), cprev])
            bank_gate[4 + b] = (SVX, v1)
            v2 = K.vx(lambda vec: vec.reduce_max(gnm[:], gsb[:], axis=AX, negate=True),
                      wait=[(SVX, v1)])
            c1 = K.sc(lambda s: s.activation(gsb2[:], gsb[:], ACT.Exp, bias=gnm[:],
                                             accum_out=gden[:]), wait=[(SVX, v2)])
            cprev = (SSC, c1)
            vgr = K.vx(lambda vec: vec.reciprocal(grd[:], gden[:]), wait=[(SSC, c1)])
            vmx = K.vx(lambda vec: vec.max(out=gmx[:], in_=gsb2[:]), wait=[(SSC, c1)])
            vsel = K.vx(lambda vec: vec.scalar_tensor_tensor(
                gind[:], gsb2[:], gmx[:, 1:2], gsb2[:], op0=ALU.is_ge, op1=ALU.mult),
                wait=[(SVX, vmx)])
            v3 = K.vx(lambda vec, st=st: vec.tensor_scalar_mul(
                w_tm[:, st, :], gind[:], grd[:]), wait=[(SVX, vsel), (SVX, vgr)])
            import os as _os7
            if _os7.environ.get("KFFN") == "1" and g == 0 and st == 0:
                def dcpg(vec):
                    vec.tensor_copy(x2pre[:, 0, 0:8], gsb[:])
                    vec.tensor_copy(x2pre[:, 0, 8:16], gsb2[:])
                    vec.tensor_copy(x2pre[:, 0, 16:24], gmx[:])
                    vec.tensor_copy(x2pre[:, 0, 24:32], gind[:])
                    return vec.tensor_copy(x2pre[:, 0, 32:40], w_tm[:, 0, :])
                dvg = K.vx(dcpg, wait=[(SVX, v3)])
                ddg = dump("gate0", x2pre.ap()[:, 0, :], [(SVX, dvg)])
                bank_gate[4] = (SOUT, ddg)
            v4_of[st] = K.vx(lambda vec, st=st, b=b: vec.tensor_copy(
                wbf[:, b, :], w_tm[:, st, :]),
                wait=[(SVX, v3), wtprev[b]])
            if st >= 1:
                emit_wt(st - 1)
        emit_wt(ST - 1)
        for st in range(ST):
            pb = None
            for oh in range(2):
                def mmb(te, st=st, oh=oh):
                    return te.matmul(ps[2 * (st % 2) + oh][:, 0:512], w_fm[:, st, :],
                                     ebT[:, 512 * oh:512 * oh + 512],
                                     start=True, stop=True)
                pb = K.pe(mmb, wait=[(SVX, v5_of[st]), (SGD, v_const),
                                     bank_gate[2 * (st % 2) + oh]])
            def init3(vec, st=st):
                b0 = ps[2 * (st % 2)]
                b1 = ps[2 * (st % 2) + 1]
                vec.tensor_tensor(x3acc[:, st, 0:512], b0[:, 0:512],
                                  x2tm[:, st, 0:512], ALU.add)
                return vec.tensor_tensor(x3acc[:, st, 512:1024], b1[:, 0:512],
                                         x2tm[:, st, 512:1024], ALU.add)
            iv = K.vx(init3, wait=[(SPE, pb), (SVX, ln_done[st])])
            bank_gate[2 * (st % 2)] = (SVX, iv)
            bank_gate[2 * (st % 2) + 1] = (SVX, iv)
        ev_done = {}
        v_wer = {}
        last_out = None

        def emit_ln2(st_ln):
            wg = [(SVX, ev_done[(E - 1, st_ln)])]
            if len(out_hist) >= 2:
                wg.append((SOUT, out_hist[-2]))
            lnv = emit_ln(x3acc[:, st_ln, :], wg, [outb[:, st_ln % 2, :]],
                          b=st_ln % 2, dummy=sq_scr[:])
            ov = K.odma(out_d[G * g + 128 * st_ln:G * g + 128 * st_ln + 128, :],
                        outb[:, st_ln % 2, :], wait=[(SVX, lnv)])
            out_hist.append(ov)
            return ov

        for e in range(2):
            v_wer[e] = K.gdma(wer[e][:], rg(wet_d[e]), wait=[ring_gate[e]])
        for e in range(E):
            if filler is not None:
                filler(e, "pre")
            pm = None
            for st in range(ST):
                b0i, b1i = 2 * (st % 2), 2 * (st % 2) + 1
                def mme(te, e=e, st=st, b0i=b0i, b1i=b1i):
                    l = None
                    for hc in range(8):
                        te.matmul(ps[b0i][:, 0:512], fmbuf[:, hc, 128 * st:128 * st + 128],
                                  wer[e % 2][:, hc, 0:512],
                                  start=(hc == 0), stop=(hc == 7))
                        l = te.matmul(ps[b1i][:, 0:512], fmbuf[:, hc, 128 * st:128 * st + 128],
                                      wer[e % 2][:, hc, 512:1024],
                                      start=(hc == 0), stop=(hc == 7))
                    return l
                wt = [(SGD, v_wer[e]), bank_gate[b0i], bank_gate[b1i]]
                pm = K.pe(mme, wait=wt)
                def evac(vec, e=e, st=st, b0i=b0i, b1i=b1i):
                    vec.scalar_tensor_tensor(
                        x3acc[:, st, 0:512], ps[b0i][:, 0:512],
                        w_tm[:, st, e:e + 1], x3acc[:, st, 0:512],
                        op0=ALU.mult, op1=ALU.add)
                    return vec.scalar_tensor_tensor(
                        x3acc[:, st, 512:1024], ps[b1i][:, 0:512],
                        w_tm[:, st, e:e + 1], x3acc[:, st, 512:1024],
                        op0=ALU.mult, op1=ALU.add)
                vv = K.vx(evac, wait=[(SPE, pm)])
                bank_gate[b0i] = (SVX, vv)
                bank_gate[b1i] = (SVX, vv)
                ev_done[(e, st)] = vv
                if e == E - 1 and st >= 2:
                    last_out = emit_ln2(st - 2)
            ring_gate[e % 2] = (SPE, pm)
            if e + 2 < E:
                v_wer[e + 2] = K.gdma(wer[(e + 2) % 2][:], rg(wet_d[e + 2]),
                                      wait=[(SPE, pm)])
            if filler is not None:
                filler(e, "post")
        for st_ln in range(ST - 2, ST):
            last_out = emit_ln2(st_ln)
        return ev_done[(E - 1, ST - 1)], last_out

    xo_last = {}
    xo_rv = []
    out_hist = []
    x2pre_gate = {}
    ring_gate = {0: None, 1: None}

    r0 = emit_projections(0)
    loc0 = emit_local(0, [(SVX, r0["qvx"]), (SVX, r0["kvx"]), (SVX, r0["vvx"]),
                          (SVX, r0["vone"]), (SVX, r0["glv"])])

    import os as _os
    if _os.environ.get("KONEGROUP") == "1" and stage == "attn":
        dump("q0", q_sb.ap().rearrange("p a n -> p (a n)"), [(SVX, r0["qvx"])])
        dump("k0", k_sb.ap().rearrange("p a n -> p (a n)"), [(SVX, r0["kvx"])])
        dump("v0", v_sb.ap().rearrange("p a b n -> p (a b n)"), [(SVX, r0["vvx"]), (SVX, r0["vone"])])
        dump("gl", gl_tm.ap().rearrange("p a n -> p (a n)"), [(SVX, r0["glv"])])
        dump("attnL0", attn_g[0].ap().rearrange("p a n -> p (a n)"), [(SVX, loc0)])
        K.final_wait()
        with nc.Block() as blk:
            K.emit(blk)
        es1.close()
        es.close()
        return nc

    v_xg[1] = K.gdma(xg[:], rg(xT)[:, :, G:G + G + 256], wait=[xg_free[0]])
    v_wk[1] = K.gdma(wbufB[:], rg(wk_d), wait=[xg_free[0]])
    v_wq[1] = K.gdma(wbufA[:], rg(wq_d), wait=[(SPE, r0["wA_free_pe"])])

    r1 = emit_projections(1)
    loc1 = emit_local(1, [(SVX, r1["qvx"]), (SVX, r1["kvx"]), (SVX, r1["vvx"]),
                          (SVX, r1["glv"])])

    v_wgk = K.gdma(wbufA[:, 0:4], rg(wgk_d), wait=[(SPE, r1["wA_free_pe"])])
    pool_step(2 * NPIECE)
    v_wgv = K.gdma(wbufB[:, 0:4], rg(wgv_d), wait=[xg_free[1]])
    gkgv, wgv_pe = emit_pooled_finish(v_wgk, v_wgv)
    v_rout = K.gdma(wbufA[:, 4:8], rg(rout_d), wait=[xg_free[1]])
    v_wout = K.gdma(wbufB[:], rg(wout_d), wait=[(SPE, wgv_pe)])

    dL0 = dump("attnL0", attn_g[0].ap().rearrange("p a n -> p (a n)"), [(SVX, loc0)])
    dL1 = dump("attnL1", attn_g[1].ap().rearrange("p a n -> p (a n)"), [(SVX, loc1)])
    import os
    _lat_lim = int(os.environ.get("KLAT_LIMIT", "0")) or None
    lat0 = emit_latent(0, gkgv, (SVX, r0["gqvx"]), v_rout,
                       extra_gate=(SOUT, dL0) if dL0 else None, only_st=_lat_lim)
    lat1 = emit_latent(1, gkgv, (SVX, r1["gqvx"]), v_rout,
                       extra_gate=(SOUT, dL1) if dL1 else None,
                       only_st=_lat_lim) if _lat_lim is None else lat0
    pe_after_latents = K.s_pe.n
    phase2_ready = (SPE, pe_after_latents)

    dump("q0", q_sb.ap().rearrange("p a n -> p (a n)"), [(SVX, r1["qvx"])])
    dump("k0", k_sb.ap().rearrange("p a n -> p (a n)"), [(SVX, r1["kvx"])])
    dump("gq0", gq_g[0].ap().rearrange("p a n -> p (a n)"), [(SVX, r0["gqvx"])])
    dump("pooled", pooled_tm[:], [(SVX, gkgv)])
    dump("gk", gk_fm.ap().rearrange("p a n -> p (a n)"), [(SVX, gkgv)])
    dump("gl", gl_tm.ap().rearrange("p a n -> p (a n)"), [(SVX, r1["glv"])])
    dump("attn0", attn_g[0].ap().rearrange("p a n -> p (a n)"), [(SVX, lat0)])
    dump("attn1", attn_g[1].ap().rearrange("p a n -> p (a n)"), [(SVX, lat1)])

    if stage == "attn":
        K.final_wait()
        with nc.Block() as blk:
            K.emit(blk)
        es1.close()
        es.close()
        return nc

    es1.close()
    sb2 = lambda n, shp, dty: es.enter_context(nc.sbuf_tensor(n, shp, dty))
    x2pre = sb2("x2pre", [128, 2, D], F32)
    x2tm = sb2("x2tm", [128, ST, D], F32)
    x3acc = sb2("x3acc", [128, ST, D], F32)
    fmbuf = sb2("fmbuf", [128, 8, G], BF16)
    x2bf = sb2("x2bf", [128, 2, D], BF16)
    afm_s = sb2("afm_s", [128, 2, 8, 128], BF16)
    sq_scr = sb2("sq_scr", [128, D], F32)
    outb = sb2("outb", [128, 2, D], F32)
    xo_buf = sb2("xo_buf", [128, 3, D], F32)
    wer = [sb2(f"wer{i}", [128, 8, D], BF16) for i in range(2)]
    ring_gate[0] = phase2_ready
    ring_gate[1] = phase2_ready
    xo_last[0] = phase2_ready
    xo_last[1] = phase2_ready

    ln0, x2ev0, stC0 = emit_wout_ln1(0, lat0, v_wout, None)
    dump("x2_0", x2tm.ap().rearrange("p a n -> p (a n)"), [(SVX, ln0[ST - 1])])
    for _st in range(3):
        xo_fetch(1, _st)
    pre1, post1, stC1, ln1, x2ev1 = make_wout(1, lat1, v_wout, None)

    def filler1(e, ph):
        if ph == "pre":
            pre1(e)
        else:
            post1(e)
    ffn0_done, out0 = emit_ffn(0, x2ev0, ln0, filler=filler1, stage_C_fn=stC0)
    dw0 = dump("wtm0", w_tm.ap().rearrange("p a n -> p (a n)"), [(SVX, ffn0_done)])
    dx0 = dump("x3a0", x3acc.ap().rearrange("p a n -> p (a n)"), [(SVX, ffn0_done)])
    ffn1_done, out1 = emit_ffn(1, x2ev1, ln1, stage_C_fn=stC1)

    K.final_wait()
    with nc.Block() as blk:
        K.emit(blk)
    es.close()
    return nc



def _host_inputs(inputs):
    x = np.ascontiguousarray(inputs["L0_attn_x"].reshape(S, D))
    xTf = np.ascontiguousarray(x.T)
    expert_wT = np.ascontiguousarray(
        np.asarray(inputs["L0_ffn_expert_w"]).transpose(0, 2, 1))
    gate_wT = np.ascontiguousarray(np.asarray(inputs["L0_ffn_gate_w"]).T)
    gate_b_t = np.ascontiguousarray(
        np.broadcast_to(np.asarray(inputs["L0_ffn_gate_b"]), (128, E)))
    glw = np.asarray(inputs["L0_attn_gate_local_w"])
    grw = np.asarray(inputs["L0_attn_gate_remote_w"])
    glbd = np.zeros((D, NH), dtype=np.float32)
    grbd = np.zeros((D, NH), dtype=np.float32)
    for h in range(NH):
        glbd[64 * h:64 * h + 64, h] = glw[h]
        grbd[64 * h:64 * h + 64, h] = grw[h]
    iden = np.eye(128, dtype=np.float32)
    tri = np.zeros((128, 384), dtype=np.float32)
    kk = np.arange(128)[:, None]
    qq = np.arange(128)[None, :]
    tri[:, 0:128] = (qq >= kk).astype(np.float32)
    tri[:, 128:256] = 1.0
    tri[:, 256:384] = (kk > qq).astype(np.float32)
    psel = np.zeros((128, 256), dtype=np.float32)
    psel[:, 128] = 1.0

    per_core = []
    for c in range(NCORES):
        lo = T * c
        xT_own = np.zeros((D, T + 256), dtype=np.float32)
        if c == 0:
            xT_own[:, 256:] = xTf[:, 0:T]
        else:
            xT_own[:] = xTf[:, lo - 256:lo + T]
        lm = np.zeros((128, 16 * NL), dtype=np.float32)
        for gst in range(16):
            cmax = 8 * c + gst // 2
            lm[:, gst * NL + cmax + 1:(gst + 1) * NL] = NEG
        halov = np.full((128, 1), 0.0 if c == 0 else 1.0, dtype=np.float32)
        per_core.append(dict(
            xT=np.ascontiguousarray(xT_own),
            x_own=np.ascontiguousarray(x[lo:lo + T]),
            x_all=x,
            wq=np.asarray(inputs["L0_attn_wq"]),
            wk=np.asarray(inputs["L0_attn_wk"]),
            wv=np.asarray(inputs["L0_attn_wv"]),
            wgq=np.asarray(inputs["L0_attn_wgq"]),
            wgk=np.asarray(inputs["L0_attn_wgk"]),
            wgv=np.asarray(inputs["L0_attn_wgv"]),
            r_out_w=np.asarray(inputs["L0_attn_r_out_w"]),
            w_out=np.asarray(inputs["L0_attn_w_out"]),
            expert_wT=expert_wT,
            gate_wT=gate_wT,
            gate_b_t=gate_b_t,
            expert_b=np.asarray(inputs["L0_ffn_expert_b"]),
            glw_bd=glbd,
            grw_bd=grbd,
            iden_i=iden,
            trimask_i=tri,
            trihalo_i=tri * (0.0 if c == 0 else 1.0),
            poolsel_i=psel,
            lmask_i=lm,
            halov_i=halov,
        ))
    return per_core


def kernel(**inputs):
    per_core = _host_inputs(inputs)
    nc = build_nc(stage="full")
    res = run_bass_kernel_spmd(nc, per_core, core_ids=list(range(NCORES)))
    out = np.concatenate([res.results[c]["out"] for c in range(NCORES)], axis=0)
    return out.reshape(1, S, D)
